# version 02
# baseline (speedup 1.0000x reference)
import numpy as np
import ml_dtypes
import concourse.bass as bass
import concourse.mybir as mybir
from concourse.bass_utils import run_bass_kernel_spmd

F32 = mybir.dt.float32
BF16 = mybir.dt.bfloat16
AF = mybir.ActivationFunctionType
ALU = mybir.AluOpType
AX = mybir.AxisListType
bf16 = ml_dtypes.bfloat16

T = 2048
D = 2048
H = 16
HD = 128
C = 128
NCH = T // C
HFF = 8192
NCORE = 8
EPS_ATTN = 1e-5
EPS_LN = 1e-5
EPS_FFN = 1.1920929e-07


def build_program(dump=()):
    nc = bass.Bass("TRN2", target_bir_lowering=False)

    x_in = nc.dram_tensor("x", [T, D], F32, kind="ExternalInput").ap()
    wq_in = nc.dram_tensor("Wq", [D, D], F32, kind="ExternalInput").ap()
    wk_in = nc.dram_tensor("Wk", [D, D], F32, kind="ExternalInput").ap()
    wv_in = nc.dram_tensor("Wv", [D, D], F32, kind="ExternalInput").ap()
    wo_in = nc.dram_tensor("Wo", [D, D], F32, kind="ExternalInput").ap()
    wg1_in = nc.dram_tensor("Wg1", [D, HD], F32, kind="ExternalInput").ap()
    wg2_in = nc.dram_tensor("Wg2", [HD, D], F32, kind="ExternalInput").ap()
    nw16_in = nc.dram_tensor("nw16", [128, 16], F32, kind="ExternalInput").ap()
    ldbc_in = nc.dram_tensor("ldbc", [128, 16], F32, kind="ExternalInput").ap()
    wgu_sh_in = nc.dram_tensor("wgu_sh", [2 * HFF, D // NCORE], F32, kind="ExternalInput").ap()
    wp_sh_in = nc.dram_tensor("wp_sh", [D, HFF // NCORE], F32, kind="ExternalInput").ap()
    flag_in = nc.dram_tensor("flag", [128, 1], F32, kind="ExternalInput").ap()
    diffT_in = nc.dram_tensor("diffT", [128, 128], F32, kind="ExternalInput").ap()
    cplus1_in = nc.dram_tensor("cplus1", [128, 128], F32, kind="ExternalInput").ap()
    c127rep_in = nc.dram_tensor("c127rep", [128, 512], F32, kind="ExternalInput").ap()
    ident_in = nc.dram_tensor("ident", [128, 128], F32, kind="ExternalInput").ap()
    onehot_in = nc.dram_tensor("onehot", [128, 256], F32, kind="ExternalInput").ap()
    out_t = nc.dram_tensor("out", [T, D], F32, kind="ExternalOutput").ap()

    xn_scr = nc.dram_tensor("xn_scr", [T, D], BF16, kind="Internal").ap()
    q_scr = nc.dram_tensor("q_scr", [D, T], BF16, kind="Internal").ap()
    k_scr = nc.dram_tensor("k_scr", [D, T], BF16, kind="Internal").ap()
    ktd_scr = nc.dram_tensor("ktd_scr", [D, T], BF16, kind="Internal").ap()
    v_scr = nc.dram_tensor("v_scr", [T, D], BF16, kind="Internal").ap()
    sg_scr = nc.dram_tensor("sg_scr", [T, D], BF16, kind="Internal").ap()
    h_scr = nc.dram_tensor("h_scr", [8 * 64 * 128, 256], BF16, kind="Internal").ap()
    h_scr5 = h_scr.rearrange("(tp hb p) t -> tp hb p t", hb=64, p=128)
    attnx_scr = nc.dram_tensor("attnx_scr", [T, D], F32, kind="Internal").ap()
    wgu_my = nc.dram_tensor("wgu_my", [D // NCORE, 2 * HFF], BF16, kind="Internal").ap()
    wp_my = nc.dram_tensor("wp_my", [HFF // NCORE, D], BF16, kind="Internal").ap()
    wgu_hat = nc.dram_tensor("wgu_hat", [D, 2 * HFF], BF16, kind="Internal").ap()
    wp_hat = nc.dram_tensor("wp_hat", [HFF, D], BF16, kind="Internal").ap()
    xnt_scr = nc.dram_tensor("xnt_scr", [128, 32768], BF16, kind="Internal").ap()
    s_end = nc.dram_tensor("s_end", [128, H * HD], BF16, kind="Internal").ap()
    s_gath = nc.dram_tensor("s_gath", [256, H * HD], BF16, kind="Internal").ap()
    scratches = {"xn_scr": (xn_scr, [T, D], BF16), "q_scr": (q_scr, [D, T], BF16),
                 "k_scr": (k_scr, [D, T], BF16), "ktd_scr": (ktd_scr, [D, T], BF16),
                 "v_scr": (v_scr, [T, D], BF16), "sg_scr": (sg_scr, [T, D], BF16),
                 "attnx_scr": (attnx_scr, [T, D], F32),
                 "wgu_hat": (wgu_hat, [D, 2 * HFF], BF16),
                 "wp_hat": (wp_hat, [HFF, D], BF16),
                 "s_gath": (s_gath, [256, H * HD], BF16),
                 "s_end": (s_end, [128, H * HD], BF16),
                 "h_scr": (h_scr, [8 * 64 * 128, 256], BF16)}
    taps = {"onormt": ([128, 32768], BF16), "wA0": ([128, 16384], BF16),
            "xnt2": ([128, 32768], BF16), "gu0": ([128, 2048], F32),
            "p2c1": ([128, 4096], F32), "p2o1": ([128, 4096], F32),
            "ktd15": ([128, 4096], F32)}
    dbg = {}
    for name in dump:
        if name.startswith("_"):
            continue
        if name in taps:
            shp, dt = taps[name]
        else:
            ap0, shp, dt = scratches[name]
        dbg[name] = nc.dram_tensor("dbg_" + name, shp, dt, kind="ExternalOutput").ap()

    A = nc.alloc_sbuf_tensor("A", [128, 32768], BF16)
    B = nc.alloc_sbuf_tensor("B", [128, 32768], BF16)
    PBF = nc.alloc_sbuf_tensor("PBF", [128, 12288], BF16)
    PF = nc.alloc_sbuf_tensor("PF", [128, 4096], F32)
    CA = nc.alloc_sbuf_tensor("CA", [128, 8192], BF16)
    CF = nc.alloc_sbuf_tensor("CF", [128, 1664], F32)

    CF_DIFFT, CF_CP1, CF_C127, CF_LDBC, CF_NW16, CF_FLAG = 0, 128, 256, 768, 784, 800
    CF_KD, CF_CD, CF_IDENT, CF_OH, CF_STATS, CF_INVF, CF_SM = 801, 817, 833, 961, 1217, 1233, 1280
    CA_DT, CA_QD, CA_CDI, CA_I128, CA_OH = 0, 2048, 4096, 6144, 6272

    PSW = nc.alloc_psum_tensor("PSW", [128, 2048], F32)
    PS0 = nc.alloc_psum_tensor("PS0", [128, 512], F32)
    PS1 = nc.alloc_psum_tensor("PS1", [128, 512], F32)
    PS2 = nc.alloc_psum_tensor("PS2", [128, 512], F32)
    PSTR = nc.alloc_psum_tensor("PSTR", [128, 1024], BF16)

    class Sem:
        def __init__(self, name):
            self.h = nc.alloc_semaphore(name)
            self.n = 0

        def inc(self, inst, k=1):
            inst.then_inc(self.h, k)
            self.n += k
            return self.n

    scc = Sem("scc")
    spe = Sem("spe")
    sdv = Sem("sdv")
    sac = Sem("sac")

    class DStream:

        def __init__(self, name, k):
            self.sems = [nc.alloc_semaphore(f"{name}{j}") for j in range(k)]
            self.cnt = [0] * k
            self.k = k

        def inc(self, inst, slot=0):
            j = slot % self.k
            inst.then_inc(self.sems[j], 16)
            self.cnt[j] += 16
            return (j, self.cnt[j])

        def tok(self, slot=0):
            j = slot % self.k
            return (j, self.cnt[j])

        def wait(self, eng, token):
            j, v = token
            eng.wait_ge(self.sems[j], v)

        def wait_all(self, eng):
            for j in range(self.k):
                eng.wait_ge(self.sems[j], self.cnt[j])

    def a_(t, off, w):
        return t[:, off:off + w]

    def g3(ap, k):
        return ap.rearrange("p (g k) -> p g k", k=k)

    STREAMS = {}

    with nc.Block() as blk:
        TE, VE, SE, SY, GP = nc.tensor, nc.vector, nc.scalar, nc.sync, nc.gpsimd

        def ST(name, k=1):
            if name not in STREAMS:
                STREAMS[name] = DStream(name, k)
            return STREAMS[name]

        def dS(st, slot, dst, src):
            return st.inc(SY.dma_start(out=dst, in_=src), slot)

        def dTS(st, slot, dst, src):
            return st.inc(SY.dma_start_transpose(dst, src), slot)

        def dA(st, slot, dst, src):
            return st.inc(SE.dma_start(out=dst, in_=src), slot)

        def dG(st, slot, dst, src):
            return st.inc(GP.dma_start(out=dst, in_=src), slot)

        stC = ST("c")
        dS(stC, 0, a_(CF, CF_DIFFT, 128), diffT_in)
        dS(stC, 0, a_(CF, CF_CP1, 128), cplus1_in)
        dS(stC, 0, a_(CF, CF_C127, 512), c127rep_in)
        dS(stC, 0, a_(CF, CF_LDBC, 16), ldbc_in)
        dS(stC, 0, a_(CF, CF_NW16, 16), nw16_in)
        dS(stC, 0, a_(CF, CF_FLAG, 1), flag_in)
        dS(stC, 0, a_(CF, CF_IDENT, 128), ident_in)
        dS(stC, 0, a_(CF, CF_OH, 256), onehot_in)

        stC.wait_all(SE)
        for h in range(H):
            SE.activation(a_(CA, CA_DT + h * 128, 128), a_(CF, CF_DIFFT, 128), AF.Exp,
                          scale=a_(CF, CF_LDBC + h, 1))
            SE.activation(a_(CA, CA_QD + h * 128, 128), a_(CF, CF_CP1, 128), AF.Exp,
                          scale=a_(CF, CF_LDBC + h, 1))
            i = SE.activation(a_(CF, CF_KD + h, 1), a_(CF, CF_C127, 1), AF.Exp,
                              scale=a_(CF, CF_LDBC + h, 1))
        i = SE.activation(a_(CF, CF_CD, 16), a_(CF, CF_LDBC, 16), AF.Exp, scale=float(C))
        consts_act = sac.inc(i)
        stC.wait_all(VE)
        VE.tensor_copy(a_(CA, CA_I128, 128), a_(CF, CF_IDENT, 128))
        i = VE.tensor_copy(a_(CA, CA_OH, 256), a_(CF, CF_OH, 256))
        VE.wait_ge(sdv.h, sdv.inc(i))
        VE.wait_ge(sac.h, consts_act)
        for h in range(H):
            i = VE.tensor_scalar(a_(CA, CA_CDI + h * 128, 128), a_(CA, CA_I128, 128),
                                 a_(CF, CF_CD + h, 1), None, ALU.mult)
        consts_cdi = sdv.inc(i)

        q_tiles = []
        for r in range(16):
            q_tiles.append(("gu", r))
        for r in range(16):
            q_tiles.append(("wp", r))
        p1b = a_(PBF, 4096, 2048); p2b = a_(PBF, 6144, 2048)
        tqb = a_(PBF, 8192, 2048); whb = a_(PBF, 10240, 2048)
        redq = a_(CF, CF_SM + 96, 32)
        sbfq = a_(CA, 6528, 32); hsq_ = a_(CA, 6560, 32); nhsq = a_(CA, 6592, 32)
        qdv_done = {}
        qtr_done = 0
        qev_done = 0
        if "_noq" in dump:
            q_tiles = []
        for it, (kind, r) in enumerate(q_tiles):
            W = 2048 if kind == "gu" else 1024
            NG = W // 64
            wbf = a_(PBF, (it % 2) * 2048, 2048)[:, 0:W]
            if it >= 2:
                GP.wait_ge(sdv.h, qdv_done[it - 2])
            stQL = ST("ql", 2)
            if kind == "gu":
                qld = dG(stQL, it, g3(wbf, 256),
                         wgu_sh_in[r * 1024:(r + 1) * 1024, :]
                         .rearrange("(a p) d -> p a d", p=128))
            else:
                qld = dG(stQL, it, wbf, wp_sh_in[r * 128:(r + 1) * 128, :])
            stQL.wait(VE, qld)
            i = VE.tensor_reduce(redq[:, 0:NG], g3(wbf, 64), axis=AX.X, op=ALU.add,
                                 apply_absolute_value=True)
            VE.wait_ge(sdv.h, sdv.inc(i))
            i = VE.tensor_scalar(sbfq[:, 0:NG], redq[:, 0:NG], 1.0 / 64.0, 1e-8,
                                 ALU.mult, ALU.max)
            VE.wait_ge(sdv.h, sdv.inc(i))
            VE.tensor_scalar(hsq_[:, 0:NG], sbfq[:, 0:NG], 0.5, None, ALU.mult)
            i = VE.tensor_scalar(nhsq[:, 0:NG], sbfq[:, 0:NG], -0.5, None, ALU.mult)
            VE.wait_ge(sdv.h, sdv.inc(i))
            a0, a1 = bass.broadcast_tensor_aps(g3(wbf, 64), g3(hsq_[:, 0:NG], 1))
            VE.tensor_tensor(g3(p1b[:, 0:W], 64), a0, a1, op=ALU.is_gt)
            b0, b1 = bass.broadcast_tensor_aps(g3(wbf, 64), g3(nhsq[:, 0:NG], 1))
            i = VE.tensor_tensor(g3(p2b[:, 0:W], 64), b0, b1, op=ALU.is_lt)
            qdv_done[it] = sdv.inc(i)
            VE.wait_ge(sdv.h, qdv_done[it])
            i = VE.tensor_sub(tqb[:, 0:W], p1b[:, 0:W], p2b[:, 0:W])
            VE.wait_ge(sdv.h, sdv.inc(i))
            VE.wait_ge(spe.h, qtr_done)
            c0, c1 = bass.broadcast_tensor_aps(g3(tqb[:, 0:W], 64), g3(sbfq[:, 0:NG], 1))
            i = VE.tensor_tensor(g3(whb[:, 0:W], 64), c0, c1, op=ALU.mult)
            wh_dv = sdv.inc(i)
            TE.wait_ge(sdv.h, wh_dv)
            TE.wait_ge(sdv.h, consts_cdi)
            if kind == "gu":
                for dhalf in range(2):
                    TE.wait_ge(sdv.h, qev_done)
                    for a8 in range(8):
                        i = TE.transpose(PSTR[:, a8 * 128:(a8 + 1) * 128],
                                         whb[:, a8 * 256 + dhalf * 128:
                                             a8 * 256 + dhalf * 128 + 128],
                                         a_(CA, CA_I128, 128))
                    qtr_done = spe.inc(i)
                    VE.wait_ge(spe.h, qtr_done)
                    i = VE.tensor_copy(a_(A, dhalf * 16384 + r * 1024, 1024), PSTR[:])
                    qev_done = sdv.inc(i)
            else:
                TE.wait_ge(sdv.h, qev_done)
                for a8 in range(8):
                    i = TE.transpose(PSTR[:, a8 * 128:(a8 + 1) * 128],
                                     whb[:, a8 * 128:(a8 + 1) * 128],
                                     a_(CA, CA_I128, 128))
                qtr_done = spe.inc(i)
                VE.wait_ge(spe.h, qtr_done)
                dstA = A[:].rearrange("p (ht d) -> p ht d", d=2048)[
                    :, 0:8, r * 128:(r + 1) * 128]
                i = VE.tensor_copy(dstA, g3(PSTR[:], 128))
                qev_done = sdv.inc(i)
            if kind == "gu" and r == 15:
                stQS = ST("qs")
                GP.wait_ge(sdv.h, qev_done)
                dG(stQS, 0, wgu_my[0:128, :], a_(A, 0, 16384))
                wgu_out = dG(stQS, 0, wgu_my[128:256, :], a_(A, 16384, 16384))
                stQS.wait(VE, wgu_out)
        stQS = ST("qs")
        GP.wait_ge(sdv.h, qev_done)
        dG(stQS, 0, wp_my.rearrange("(a p) d -> p a d", p=128), g3(a_(A, 0, 16384), 2048))
        stQS.wait_all(GP)
        stQL = ST("ql", 2)
        stQL.wait_all(GP)
        if "_nocc" in dump:
            for rr in range(NCORE):
                i = GP.dma_start(out=wgu_hat[rr * (D // NCORE):(rr + 1) * (D // NCORE), :],
                                 in_=wgu_my)
                i.then_inc(scc.h, 16); scc.n += 16
                i = GP.dma_start(out=wp_hat[rr * (HFF // NCORE):(rr + 1) * (HFF // NCORE), :],
                                 in_=wp_my)
                i.then_inc(scc.h, 16); scc.n += 16
            GP.wait_ge(scc.h, scc.n)
        else:
            GP.collective_compute("AllGather", ALU.bypass, replica_groups=[list(range(NCORE))],
                                  ins=[wgu_my.opt()], outs=[wgu_hat.opt()]).then_inc(scc.h, 1)
            scc.n += 1
            GP.collective_compute("AllGather", ALU.bypass, replica_groups=[list(range(NCORE))],
                                  ins=[wp_my.opt()], outs=[wp_hat.opt()]).then_inc(scc.h, 1)
            scc.n += 1
        cc_w = scc.n

        p0_act = [0] * 16
        p0_dv = [0] * 16
        p0_tr = [0] * 16
        xnT_evac = qev_done
        for tt in range(16):
            if tt >= 2:
                SY.wait_ge(sac.h, p0_act[tt - 2])
                SY.wait_ge(sdv.h, p0_dv[tt - 2])
            stX = ST("x", 2)
            xl = dS(stX, tt, a_(PF, (tt % 2) * 2048, 2048), x_in[tt * 128:(tt + 1) * 128, :])
            xt = a_(PF, (tt % 2) * 2048, 2048)
            sums = a_(CF, CF_SM, 1); sums2 = a_(CF, CF_SM + 1, 1)
            m_ = a_(CF, CF_SM + 2, 1); var_ = a_(CF, CF_SM + 3, 1)
            sd_ = a_(CF, CF_SM + 4, 1); rstd_ = a_(CF, CF_SM + 5, 1)
            msq_ = a_(CF, CF_SM + 6, 1)
            stX.wait(SE, xl)
            if tt >= 1:
                SE.wait_ge(sdv.h, p0_dv[tt - 1])
            i = SE.activation(PSW[:], xt, AF.Copy, accum_out=sums)
            SE.wait_ge(sac.h, sac.inc(i))
            i = SE.activation(PSW[:], xt, AF.Square, accum_out=sums2)
            p0_act[tt] = sac.inc(i)
            VE.wait_ge(sac.h, p0_act[tt])
            VE.tensor_scalar(m_, sums, 1.0 / D, None, ALU.mult)
            i = VE.tensor_scalar(var_, sums2, 1.0 / D, None, ALU.mult)
            VE.wait_ge(sdv.h, sdv.inc(i))
            i = VE.tensor_tensor(msq_, m_, m_, op=ALU.mult)
            VE.wait_ge(sdv.h, sdv.inc(i))
            i = VE.tensor_tensor(var_, var_, msq_, op=ALU.subtract)
            VE.wait_ge(sdv.h, sdv.inc(i))
            i = VE.tensor_scalar(var_, var_, 1.0, EPS_LN, ALU.mult, ALU.add)
            SE.wait_ge(sdv.h, sdv.inc(i))
            i = SE.activation(sd_, var_, AF.Sqrt)
            VE.wait_ge(sac.h, sac.inc(i))
            i = VE.reciprocal(rstd_, sd_)
            VE.wait_ge(sdv.h, sdv.inc(i))
            if tt >= 2:
                VE.wait_ge(spe.h, p0_tr[tt - 2])
            i = VE.tensor_scalar(a_(PBF, (tt % 2) * 2048, 2048), xt, m_, rstd_,
                                 ALU.subtract, ALU.mult)
            p0_dv[tt] = sdv.inc(i)
            TE.wait_ge(sdv.h, p0_dv[tt])
            for rnd in range(2):
                TE.wait_ge(sdv.h, xnT_evac)
                for a8 in range(8):
                    dt = rnd * 8 + a8
                    i = TE.transpose(PSTR[:, a8 * 128:(a8 + 1) * 128],
                                     a_(PBF, (tt % 2) * 2048 + dt * 128, 128),
                                     a_(CA, CA_I128, 128))
                p0_tr[tt] = spe.inc(i)
                VE.wait_ge(spe.h, p0_tr[tt])
                dstB = B[:].rearrange("p (dt t) -> p dt t", t=2048)[
                    :, rnd * 8:(rnd + 1) * 8, tt * 128:(tt + 1) * 128]
                i = VE.tensor_copy(dstB, g3(PSTR[:], 128))
                xnT_evac = sdv.inc(i)
        xnT_dv = xnT_evac
        stXNT = ST("xnt")
        SY.wait_ge(sdv.h, xnT_dv)
        xnt_st = dS(stXNT, 0, xnt_scr, B[:])

        stWG1 = ST("wg1")
        for dt in range(16):
            wg1_ld = dG(stWG1, 0, a_(PBF, 10240 + dt * 128, 128),
                        wg1_in[dt * 128:(dt + 1) * 128, :])
        stWG1.wait(TE, wg1_ld)
        TE.wait_ge(sdv.h, xnT_dv)
        g1_ev = [0] * 4
        for tc in range(4):
            ps = (PS0, PS1)[tc % 2]
            if tc >= 2:
                TE.wait_ge(sdv.h, g1_ev[tc - 2])
            for dt in range(16):
                i = TE.matmul(ps[:], a_(PBF, 10240 + dt * 128, 128),
                              a_(B, dt * 2048 + tc * 512, 512),
                              start=(dt == 0), stop=(dt == 15))
            spe.inc(i)
            VE.wait_ge(spe.h, spe.n)
            i = VE.tensor_copy(a_(PBF, tc * 512, 512), ps[:])
            g1_ev[tc] = sdv.inc(i)
        g1_done = sdv.n
        g1_pe = spe.n

        stWH = ST("wh2", 2)
        whalf_idx = [0]

        def load_w_half(w_in, half, base, wait_pe):
            GP.wait_ge(spe.h, wait_pe)
            slot = whalf_idx[0]
            whalf_idx[0] += 1
            for dt in range(16):
                r = dG(stWH, slot, a_(A, base + dt * 1024, 1024),
                       w_in[dt * 128:(dt + 1) * 128, half * 1024:(half + 1) * 1024])
            return r

        state = {"qrow_free": None, "krow_free": None, "ktdrow_free": None, "kdrep_rd": 0}

        def proj_lhsT(w_in, dst_scr, decay_scr, wait_pe0):
            h0 = load_w_half(w_in, 0, 0, wait_pe0)
            h1 = load_w_half(w_in, 1, 16384, wait_pe0)
            half_ld = (h0, h1)
            for et in range(16):
                half = et // 8
                hd_ = et % 8
                base = 16384 * half
                stWH.wait(TE, half_ld[half])
                row = a_(PBF, 2048, 2048) if decay_scr is None else a_(PBF, 4096, 2048)
                stR = ST("rq") if decay_scr is None else ST("rk")
                stRT = ST("rkt")
                row_free = state["qrow_free"] if decay_scr is None else state["krow_free"]
                for tc in range(4):
                    ps = (PS0, PS1)[tc % 2]
                    TE.wait_ge(sac.h, sac.n)
                    for dt in range(16):
                        i = TE.matmul(ps[:], a_(A, base + dt * 1024 + hd_ * 128, 128),
                                      a_(B, dt * 2048 + tc * 512, 512),
                                      start=(dt == 0), stop=(dt == 15))
                    spe.inc(i)
                    SE.wait_ge(spe.h, spe.n)
                    if tc == 0 and row_free:
                        stR.wait(SE, row_free)
                    i = SE.activation(row[:, tc * 512:(tc + 1) * 512], ps[:], AF.Silu)
                    sac.inc(i)
                act_done = sac.n
                if decay_scr is not None:
                    SE.wait_ge(sdv.h, state["kdrep_rd"])
                    i = SE.activation(a_(PBF, 10240, 512), a_(CF, CF_C127, 512), AF.Exp,
                                      scale=a_(CF, CF_LDBC + et, 1))
                    sac.inc(i)
                    VE.wait_ge(sac.h, sac.n)
                    if state["ktdrow_free"]:
                        stRT.wait(VE, state["ktdrow_free"])
                    r0, r1 = bass.broadcast_tensor_aps(
                        row.rearrange("p (r k) -> p r k", k=512),
                        a_(PBF, 10240, 512).rearrange("p (r k) -> p r k", r=1))
                    i = VE.tensor_tensor(
                        a_(PBF, 6144, 2048).rearrange("p (r k) -> p r k", k=512),
                        r0, r1, op=ALU.mult)
                    state["kdrep_rd"] = sdv.inc(i)
                    if "ktd15" in dump and et == 15:
                        VE.wait_ge(sdv.h, state["kdrep_rd"])
                        VE.tensor_copy(a_(PF, 0, 2048), a_(PBF, 6144, 2048))
                        VE.tensor_copy(a_(PF, 2048, 512), a_(PBF, 10240, 512))
                        i = VE.tensor_copy(a_(PF, 2560, 1024), a_(PBF, 4096, 1024))
                        k15dv = sdv.inc(i)
                        SY.wait_ge(sdv.h, k15dv)
                        dS(ST("dbg2"), 0, dbg["ktd15"], PF[:])
                    SY.wait_ge(sdv.h, state["kdrep_rd"])
                    state["ktdrow_free"] = dS(stRT, 0, decay_scr[et * 128:(et + 1) * 128, :],
                                              a_(PBF, 6144, 2048))
                SY.wait_ge(sac.h, act_done)
                st = dS(stR, 0, dst_scr[et * 128:(et + 1) * 128, :], row)
                if decay_scr is None:
                    state["qrow_free"] = st
                else:
                    state["krow_free"] = st

        TE.wait_ge(sdv.h, g1_done)
        proj_lhsT(wq_in, q_scr, None, g1_pe)
        proj_lhsT(wk_in, k_scr, ktd_scr, spe.n)

        vh0 = load_w_half(wv_in, 0, 0, spe.n)
        vh1 = load_w_half(wv_in, 1, 16384, spe.n)
        vrow_free = {}
        stRV = ST("rv", 2)
        for tt in range(16):
            for half in range(2):
                base = 16384 * half
                stWH.wait(TE, (vh0, vh1)[half])
                vr = a_(PBF, 8192 + ((tt * 2 + half) % 2) * 1024, 1024)
                for ec in range(2):
                    ps = (PS0, PS1)[ec]
                    TE.wait_ge(sac.h, sac.n)
                    for dt in range(16):
                        i = TE.matmul(ps[:], a_(B, dt * 2048 + tt * 128, 128),
                                      a_(A, base + dt * 1024 + ec * 512, 512),
                                      start=(dt == 0), stop=(dt == 15))
                    spe.inc(i)
                    SE.wait_ge(spe.h, spe.n)
                    if ec == 0 and vrow_free.get((tt * 2 + half) % 2):
                        stRV.wait(SE, vrow_free[(tt * 2 + half) % 2])
                    i = SE.activation(vr[:, ec * 512:(ec + 1) * 512], ps[:], AF.Silu)
                    sac.inc(i)
                SY.wait_ge(sac.h, sac.n)
                vrow_free[(tt * 2 + half) % 2] = dS(
                    stRV, (tt * 2 + half),
                    v_scr[tt * 128:(tt + 1) * 128, half * 1024:(half + 1) * 1024], vr)
        p1_pe = spe.n

        VE.wait_ge(sdv.h, consts_cdi)
        for s_ in ("rq", "rk", "rkt", "rv"):
            ST(s_).wait_all(VE)
        i = VE.memset(a_(PBF, 6144, 2048), 0.0)
        a_sprev = sdv.inc(i)
        stACH = ST("ach")
        for s_ in ("rq", "rk", "rkt", "rv"):
            ST(s_).wait_all(SY)
        dTS(stACH, 0, a_(PBF, 2048, 2048), ktd_scr[:, 0:128])
        a_ld = dS(stACH, 0, a_(PBF, 4096, 2048), v_scr[0:128, :])
        for ch in range(NCH):
            scur = a_(PBF, 6144 + (ch % 2) * 2048, 2048)
            snxt = a_(PBF, 6144 + ((ch + 1) % 2) * 2048, 2048)
            stACH.wait(TE, a_ld)
            TE.wait_ge(sdv.h, a_sprev)
            if ch == 0:
                TE.wait_ge(sac.h, sac.n)
            for q in range(4):
                ps = (PS0, PS1, PS2, PSW[:, 0:512])[q]
                for hh in range(4 * q, 4 * q + 4):
                    TE.matmul(ps[:, (hh % 4) * 128:(hh % 4 + 1) * 128],
                              a_(PBF, 2048 + hh * 128, 128),
                              a_(PBF, 4096 + hh * 128, 128),
                              start=(hh % 4 == 0), stop=False, skip_group_check=True)
                for hh in range(4 * q, 4 * q + 4):
                    i = TE.matmul(ps[:, (hh % 4) * 128:(hh % 4 + 1) * 128],
                                  a_(CA, CA_CDI + hh * 128, 128),
                                  scur[:, hh * 128:(hh + 1) * 128],
                                  start=False, stop=(hh % 4 == 3), skip_group_check=True)
                spe.inc(i)
                VE.wait_ge(spe.h, spe.n)
                i = VE.tensor_copy(snxt[:, q * 512:(q + 1) * 512], ps[:])
                sdv.inc(i)
            a_sprev = sdv.n
            if ch + 1 < NCH:
                SY.wait_ge(spe.h, spe.n)
                dTS(stACH, 0, a_(PBF, 2048, 2048), ktd_scr[:, (ch + 1) * 128:(ch + 2) * 128])
                a_ld = dS(stACH, 0, a_(PBF, 4096, 2048),
                          v_scr[(ch + 1) * 128:(ch + 2) * 128, :])
        s_final = a_(PBF, 6144 + (NCH % 2) * 2048, 2048)
        stSE = ST("se")
        SY.wait_ge(sdv.h, a_sprev)
        se_st = dS(stSE, 0, s_end, s_final)
        stSE.wait(GP, se_st)
        if "_nocc" in dump:
            i = GP.dma_start(out=s_gath[0:128, :], in_=s_end)
            i.then_inc(scc.h, 16); scc.n += 16
        else:
            GP.collective_compute("AllGather", ALU.bypass,
                                  replica_groups=[[0, 1], [2, 3], [4, 5], [6, 7]],
                                  ins=[s_end.opt()], outs=[s_gath.opt()]).then_inc(scc.h, 1)
            scc.n += 1
        cc_s = scc.n

        stWG2 = ST("wg2")
        wg2_ld = dG(stWG2, 0, a_(PBF, 2048, 2048), wg2_in)
        stSG = ST("sgst", 2)
        sg_st = [None, None]
        g_idx = 0
        for tt in range(16):
            for ec in range(4):
                ps = (PS0, PS1)[g_idx % 2]
                if g_idx == 0:
                    stWG2.wait(TE, wg2_ld)
                    TE.wait_ge(sdv.h, sdv.n)
                TE.wait_ge(sac.h, sac.n)
                i = TE.matmul(ps[:], a_(PBF, tt * 128, 128),
                              a_(PBF, 2048 + ec * 512, 512), start=True, stop=True)
                spe.inc(i)
                SE.wait_ge(spe.h, spe.n)
                if sg_st[g_idx % 2]:
                    stSG.wait(SE, sg_st[g_idx % 2])
                i = SE.activation(a_(PBF, 4096 + (g_idx % 2) * 512, 512), ps[:], AF.Sigmoid)
                sac.inc(i)
                SY.wait_ge(sac.h, sac.n)
                sg_st[g_idx % 2] = dS(
                    stSG, g_idx,
                    sg_scr[tt * 128:(tt + 1) * 128, ec * 512:(ec + 1) * 512],
                    a_(PBF, 4096 + (g_idx % 2) * 512, 512))
                g_idx += 1
        g_pe = spe.n

        def bslot(stream, ch):
            offs = {"q": 0, "k": 4096, "kt": 8192, "v": 12288, "sg": 16384}
            return a_(B, offs[stream] + (ch % 2) * 2048, 2048)

        stSI = ST("si")
        SY.wait_ge(scc.h, scc.n)
        stXNT.wait(SY, xnt_st)
        si_ld = dS(stSI, 0, a_(B, 20480, 2048), s_gath[0:128, :])
        stSI.wait(VE, si_ld)
        i = VE.tensor_scalar(a_(PBF, 8192, 2048), a_(B, 20480, 2048),
                             a_(CF, CF_FLAG, 1), None, ALU.mult)
        s_ready = sdv.inc(i)

        stC2S = ST("c2s", 2)
        stC2A = ST("c2a", 2)

        def p2_loads(ch):
            dS(stC2S, ch, g3(bslot("q", ch), 128),
               q_scr[:, ch * 128:(ch + 1) * 128].rearrange("(a p) t -> p a t", p=128))
            dS(stC2S, ch, g3(bslot("k", ch), 128),
               k_scr[:, ch * 128:(ch + 1) * 128].rearrange("(a p) t -> p a t", p=128))
            l1 = dTS(stC2S, ch, bslot("kt", ch), ktd_scr[:, ch * 128:(ch + 1) * 128])
            dA(stC2A, ch, bslot("v", ch), v_scr[ch * 128:(ch + 1) * 128, :])
            l2 = dA(stC2A, ch, bslot("sg", ch), sg_scr[ch * 128:(ch + 1) * 128, :])
            return (l1, l2)

        ST("rq").wait_all(SY)
        ST("rk").wait_all(SY)
        ST("sgst", 2).wait_all(SY)
        ST("rq").wait_all(SE)
        ST("rv", 2).wait_all(SE)
        ST("sgst", 2).wait_all(SE)
        stXNT.wait(SE, xnt_st)
        p2_ld = {0: p2_loads(0), 1: p2_loads(1)}
        sct_ev = {}
        o_pe_prev = g_pe
        su_pe_prev = g_pe
        odot_prev = 0
        tr_pe_prev = 0
        p2_sdone = s_ready
        for ch in range(NCH):
            scur = a_(PBF, 8192 + (ch % 2) * 2048, 2048)
            snxt = a_(PBF, 8192 + ((ch + 1) % 2) * 2048, 2048)
            stC2S.wait(TE, p2_ld[ch][0])
            stC2A.wait(TE, p2_ld[ch][1])
            TE.wait_ge(sdv.h, p2_sdone)
            if ch == 0:
                TE.wait_ge(sac.h, sac.n)
            VE.wait_ge(spe.h, o_pe_prev)
            VE.wait_ge(sac.h, sac.n)
            for hq in range(4):
                ps = (PS0, PS1)[hq % 2]
                TE.wait_ge(sdv.h, sct_ev.get((ch, hq - 2), sct_ev.get((ch - 1, hq + 2), 0)))
                for hh in range(4 * hq, 4 * hq + 4):
                    i = TE.matmul(ps[:, (hh % 4) * 128:(hh % 4 + 1) * 128],
                                  bslot("k", ch)[:, hh * 128:(hh + 1) * 128],
                                  bslot("q", ch)[:, hh * 128:(hh + 1) * 128],
                                  start=(hh % 4 == 0), stop=(hh % 4 == 3),
                                  skip_group_check=True)
                spe.inc(i)
                VE.wait_ge(spe.h, spe.n)
                i = VE.tensor_tensor(a_(PBF, hq * 512, 512), ps[:],
                                     a_(CA, CA_DT + hq * 512, 512), op=ALU.mult)
                sct_ev[(ch, hq)] = sdv.inc(i)
            i = VE.tensor_mul(a_(PBF, 2048, 2048), bslot("q", ch), a_(CA, CA_QD, 2048))
            qtl_dv = sdv.inc(i)
            TE.wait_ge(sdv.h, qtl_dv)
            TE.wait_ge(sdv.h, odot_prev)
            for hh in range(H):
                TE.wait_ge(sdv.h, sct_ev[(ch, hh // 4)]) if hh % 4 == 0 else None
                TE.matmul(PSW[:, hh * 128:(hh + 1) * 128],
                          a_(PBF, hh * 128, 128),
                          bslot("v", ch)[:, hh * 128:(hh + 1) * 128],
                          start=(hh % 4 == 0), stop=False, skip_group_check=True)
                i = TE.matmul(PSW[:, hh * 128:(hh + 1) * 128],
                              a_(PBF, 2048 + hh * 128, 128),
                              scur[:, hh * 128:(hh + 1) * 128],
                              start=False, stop=(hh % 4 == 3), skip_group_check=True)
            o_pe = spe.inc(i)
            su_ev = 0
            for q in range(4):
                if q > 0:
                    TE.wait_ge(sdv.h, su_ev)
                for hh in range(4 * q, 4 * q + 4):
                    TE.matmul(PS2[:, (hh % 4) * 128:(hh % 4 + 1) * 128],
                              bslot("kt", ch)[:, hh * 128:(hh + 1) * 128],
                              bslot("v", ch)[:, hh * 128:(hh + 1) * 128],
                              start=(hh % 4 == 0), stop=False, skip_group_check=True)
                for hh in range(4 * q, 4 * q + 4):
                    i = TE.matmul(PS2[:, (hh % 4) * 128:(hh % 4 + 1) * 128],
                                  a_(CA, CA_CDI + hh * 128, 128),
                                  scur[:, hh * 128:(hh + 1) * 128],
                                  start=False, stop=(hh % 4 == 3), skip_group_check=True)
                spe.inc(i)
                VE.wait_ge(spe.h, spe.n)
                i = VE.tensor_copy(snxt[:, q * 512:(q + 1) * 512], PS2[:])
                su_ev = sdv.inc(i)
            p2_sdone = su_ev
            su_pe = spe.n
            VE.wait_ge(spe.h, o_pe)
            if "p2o1" in dump and ch == 1:
                VE.tensor_copy(a_(PF, 0, 2048), PSW[:])
                VE.tensor_copy(a_(PF, 2048, 512), scur[:, 1536:2048])
                VE.tensor_copy(a_(PF, 2560, 128), a_(PBF, 2048 + 15 * 128, 128))
                VE.tensor_copy(a_(PF, 2688, 128), a_(PBF, 15 * 128, 128))
                i = VE.tensor_copy(a_(PF, 2816, 128), bslot("v", ch)[:, 15 * 128:16 * 128])
                o1dv = sdv.inc(i)
                SY.wait_ge(sdv.h, o1dv)
                dS(ST("dbg2"), 0, dbg["p2o1"], PF[:])
            i = VE.tensor_mul(a_(PBF, 2048, 2048), PSW[:], bslot("sg", ch))
            odot_dv = sdv.inc(i)
            odot_prev = odot_dv
            SE.wait_ge(sdv.h, odot_dv)
            i = SE.activation(a_(PBF, 4096, 2048), a_(PBF, 2048, 2048), AF.Square)
            osq_ac = sac.inc(i)
            VE.wait_ge(sac.h, osq_ac)
            i = VE.tensor_reduce(a_(CF, CF_SM + 16, 16), g3(a_(PBF, 4096, 2048), 128),
                                 axis=AX.X, op=ALU.add)
            VE.wait_ge(sdv.h, sdv.inc(i))
            i = VE.tensor_scalar(a_(CF, CF_SM + 32, 16), a_(CF, CF_SM + 16, 16),
                                 1.0 / HD, EPS_ATTN, ALU.mult, ALU.add)
            SE.wait_ge(sdv.h, sdv.inc(i))
            i = SE.activation(a_(CF, CF_SM + 48, 16), a_(CF, CF_SM + 32, 16), AF.Sqrt)
            VE.wait_ge(sac.h, sac.inc(i))
            i = VE.reciprocal(a_(CF, CF_SM + 64, 16), a_(CF, CF_SM + 48, 16))
            VE.wait_ge(sdv.h, sdv.inc(i))
            VE.wait_ge(spe.h, tr_pe_prev)
            in0, in1 = bass.broadcast_tensor_aps(g3(a_(PBF, 2048, 2048), 128),
                                                 g3(a_(CF, CF_SM + 64, 16), 1))
            i = VE.tensor_tensor(g3(a_(PBF, 6144, 2048), 128), in0, in1, op=ALU.mult)
            onrm_dv = sdv.inc(i)
            if "p2c1" in dump and ch == 1:
                VE.tensor_copy(a_(PF, 0, 2048), a_(PBF, 2048, 2048))
                VE.tensor_copy(a_(PF, 2048, 16), a_(CF, CF_SM + 16, 16))
                VE.tensor_copy(a_(PF, 2064, 16), a_(CF, CF_SM + 64, 16))
                i = VE.tensor_copy(a_(PF, 2080, 2016), a_(PBF, 6144, 2016))
                c1dv = sdv.inc(i)
                SY.wait_ge(sdv.h, c1dv)
                dS(ST("dbg2"), 0, dbg["p2c1"], PF[:])
            tr_ev = {}
            for grp in range(2):
                TE.wait_ge(sdv.h, onrm_dv)
                if ch == 0 and grp == 0:
                    p2_trev = {}
                if grp == 1:
                    TE.wait_ge(sdv.h, p2_trev[(ch, 0)])
                elif ch > 0:
                    TE.wait_ge(sdv.h, p2_trev[(ch - 1, 1)])
                for s8 in range(8):
                    hh = grp * 8 + s8
                    i = TE.transpose(PSTR[:, s8 * 128:(s8 + 1) * 128],
                                     a_(PBF, 6144 + hh * 128, 128), a_(CA, CA_I128, 128))
                spe.inc(i)
                VE.wait_ge(spe.h, spe.n)
                dst = A[:].rearrange("p (dt t) -> p dt t", t=2048)[
                    :, grp * 8:(grp + 1) * 8, ch * 128:(ch + 1) * 128]
                i = VE.tensor_copy(dst, g3(PSTR[:], 128))
                p2_trev[(ch, grp)] = sdv.inc(i)
            tr_pe_prev = spe.n
            o_pe_prev = su_pe
            if ch + 2 < NCH:
                SY.wait_ge(spe.h, su_pe)
                SY.wait_ge(sdv.h, odot_dv)
                SE.wait_ge(sdv.h, odot_dv)
                p2_ld[ch + 2] = p2_loads(ch + 2)
        p2_done = sdv.n

        p3_ev = {}
        p3_st = {}
        p3_ev_list = [0]
        p3_st_list = [None]
        wo_scaled = 0
        stWO = ST("wo", 2)
        stP3X = ST("p3x", 2)
        stP3T = ST("p3t", 2)
        for eh in range(2):
            GP.wait_ge(sdv.h, p2_done if eh == 0 else wo_scaled)
            for dt in range(16):
                wo_ld = dG(stWO, eh, a_(B, 16384 + dt * 1024, 1024),
                           wo_in[dt * 128:(dt + 1) * 128, eh * 1024:(eh + 1) * 1024])
            stWO.wait(VE, wo_ld)
            VE.wait_ge(spe.h, spe.n)
            for dt in range(16):
                i = VE.tensor_scalar(a_(B, dt * 1024, 1024), a_(B, 16384 + dt * 1024, 1024),
                                     a_(CF, CF_NW16 + dt, 1), None, ALU.mult)
            wo_scaled = sdv.inc(i)
            TE.wait_ge(sdv.h, wo_scaled)
            for tt in range(16):
                idx = eh * 16 + tt
                pso = (tt % 2) * 1024
                if idx >= 2:
                    TE.wait_ge(sdv.h, p3_ev_list[idx - 2 + 1])
                for ec in range(2):
                    for dt in range(16):
                        i = TE.matmul(PSW[:, pso + ec * 512:pso + (ec + 1) * 512],
                                      a_(A, dt * 2048 + tt * 128, 128),
                                      a_(B, dt * 1024 + ec * 512, 512),
                                      start=(dt == 0), stop=(dt == 15))
                spe.inc(i)
                if idx >= 2:
                    SE.wait_ge(sdv.h, p3_ev_list[idx - 2 + 1])
                xld = dA(stP3X, idx, a_(PF, (idx % 2) * 1024, 1024),
                         x_in[tt * 128:(tt + 1) * 128, eh * 1024:(eh + 1) * 1024])
                stP3X.wait(VE, xld)
                VE.wait_ge(spe.h, spe.n)
                if idx >= 2:
                    stP3T.wait(VE, p3_st_list[idx - 2 + 1])
                i = VE.tensor_add(a_(PF, 2048 + (idx % 2) * 1024, 1024),
                                  PSW[:, pso:pso + 1024], a_(PF, (idx % 2) * 1024, 1024))
                p3_ev_list.append(sdv.inc(i))
                SY.wait_ge(sdv.h, sdv.n)
                p3_st_list.append(dS(
                    stP3T, idx,
                    attnx_scr[tt * 128:(tt + 1) * 128, eh * 1024:(eh + 1) * 1024],
                    a_(PF, 2048 + (idx % 2) * 1024, 1024)))
        p3_pe = spe.n
        onormt_ld = None
        if "onormt" in dump:
            GP.wait_ge(spe.h, p3_pe)
            onormt_ld = dG(ST("dbg3"), 0, dbg["onormt"], A[:])

        stP3T.wait_all(SY)
        stXNT.wait_all(SY)
        xnT_ld2 = dS(stXNT, 0, B[:], xnt_scr)
        NSUP = 16
        SUPH = 512

        stSUPS = ST("sups", 2)
        stSUPA = ST("supa", 2)

        def p4_load_super(s, par, wait_pe):
            base = (par % 2) * 16384
            SY.wait_ge(spe.h, wait_pe)
            SE.wait_ge(spe.h, wait_pe)
            if onormt_ld is not None:
                ST("dbg3").wait(SY, onormt_ld)
                ST("dbg3").wait(SE, onormt_ld)
            for gup in range(2):
                coff = gup * HFF + s * SUPH
                boff = base + gup * 8192
                for dt in range(16):
                    dst = a_(A, boff + dt * 512, 512)
                    srcw = wgu_hat[dt * 128:(dt + 1) * 128, coff:coff + SUPH]
                    if dt % 2 == 0:
                        ls = dS(stSUPS, s, dst, srcw)
                    else:
                        l2 = dA(stSUPA, s, dst, srcw)
            return (ls, l2)

        SY.wait_ge(scc.h, scc.n)
        SE.wait_ge(scc.h, scc.n)
        sup_ld = {0: p4_load_super(0, 0, p3_pe), 1: p4_load_super(1, 1, p3_pe)}
        wa0_ld = None
        if "wA0" in dump:
            stSUPS.wait(GP, sup_ld[0][0])
            stSUPA.wait(GP, sup_ld[0][1])
            wa0_ld = dG(ST("dbg3"), 0, dbg["wA0"], a_(A, 0, 16384))
        if "xnt2" in dump:
            stXNT.wait(GP, xnT_ld2)
            dG(ST("dbg3"), 0, dbg["xnt2"], B[:])
        first_stats = True
        PP = [(PS0, PS1), (PSW[:, 0:512], PSW[:, 512:1024]),
              (PSW[:, 1024:1536], PSW[:, 1536:2048])]
        sil_done = [0] * 3
        h_done = [0] * 3
        hst_done = [None] * 3
        stH = ST("hst", 3)
        stats_pe = 0
        pidx = 0
        for s in range(NSUP):
            base = (s % 2) * 16384
            stSUPS.wait(TE, sup_ld[s][0])
            stSUPA.wait(TE, sup_ld[s][1])
            if s == 0:
                stXNT.wait(TE, xnT_ld2)
                TE.wait_ge(sdv.h, p3_ev_list[-1])
            for pb in range(4):
                for tc in range(4):
                    par = pidx % 3
                    pidx += 1
                    psg, psu = PP[par]
                    TE.wait_ge(sac.h, sil_done[par])
                    TE.wait_ge(sdv.h, h_done[par])
                    for dt in range(16):
                        TE.matmul(psg[:], a_(A, base + dt * 512 + pb * 128, 128),
                                  a_(B, dt * 2048 + tc * 512, 512),
                                  start=(dt == 0), stop=(dt == 15))
                    for dt in range(16):
                        i = TE.matmul(psu[:], a_(A, base + 8192 + dt * 512 + pb * 128, 128),
                                      a_(B, dt * 2048 + tc * 512, 512),
                                      start=(dt == 0), stop=(dt == 15))
                    gu_pe = spe.inc(i)
                    if "gu0" in dump and s == 0 and pb == 0 and tc == 0:
                        VE.wait_ge(spe.h, gu_pe)
                        VE.tensor_copy(a_(PF, 0, 512), psg[:])
                        i = VE.tensor_copy(a_(PF, 512, 512), psu[:])
                        gu0dv = sdv.inc(i)
                    SE.wait_ge(spe.h, gu_pe)
                    SE.wait_ge(sdv.h, h_done[par])
                    i = SE.activation(a_(PBF, par * 1536, 512), psg[:], AF.Silu)
                    sil_done[par] = sac.inc(i)
                    VE.wait_ge(sac.h, sil_done[par])
                    if hst_done[par]:
                        stH.wait(VE, hst_done[par])
                    VE.wait_ge(spe.h, stats_pe)
                    i = VE.tensor_mul(a_(PBF, 512 + par * 1536, 512),
                                      a_(PBF, par * 1536, 512), psu[:])
                    h_done[par] = sdv.inc(i)
                    SE.wait_ge(sdv.h, h_done[par])
                    i = SE.activation(a_(PBF, 1024 + par * 1536, 512),
                                      a_(PBF, 512 + par * 1536, 512), AF.Square)
                    hsq_ac = sac.inc(i)
                    TE.wait_ge(sac.h, hsq_ac)
                    for ts in range(4):
                        gsub = tc * 4 + ts
                        i = TE.matmul(PS2[:, 0:16],
                                      a_(PBF, 1024 + par * 1536 + ts * 128, 128),
                                      a_(CA, CA_OH + gsub * 16, 16),
                                      start=first_stats, stop=False, skip_group_check=True)
                        first_stats = False
                    stats_pe = spe.inc(i)
                    if "gu0" in dump and s == 0 and pb == 0 and tc == 0:
                        VE.wait_ge(sac.h, sil_done[par])
                        VE.tensor_copy(a_(PF, 1024, 512), a_(PBF, par * 1536, 512))
                        VE.wait_ge(sdv.h, h_done[par])
                        i = VE.tensor_copy(a_(PF, 1536, 512), a_(PBF, 512 + par * 1536, 512))
                        gu0dv = sdv.inc(i)
                        SY.wait_ge(sdv.h, gu0dv)
                        dS(ST("dbg2"), 0, dbg["gu0"], PF[:, 0:2048])
                    SY.wait_ge(sdv.h, h_done[par])
                    hb = s * 4 + pb
                    hst_done[par] = dS(
                        stH, par,
                        h_scr5[tc * 2:(tc + 1) * 2, hb, :, :].rearrange("a p t -> p a t"),
                        g3(a_(PBF, 512 + par * 1536, 512), 256))
            if s + 2 < NSUP:
                if wa0_ld is not None and s == 0:
                    ST("dbg3").wait(SY, wa0_ld)
                    ST("dbg3").wait(SE, wa0_ld)
                sup_ld[s + 2] = p4_load_super(s + 2, s, spe.n)
        p4_pe = spe.n

        VE.wait_ge(spe.h, p4_pe)
        i = VE.tensor_scalar(a_(CF, CF_STATS, 16), PS2[:, 0:16], 1.0 / HFF, EPS_FFN,
                             ALU.mult, ALU.add)
        SE.wait_ge(sdv.h, sdv.inc(i))
        i = SE.activation(a_(CF, CF_STATS, 16), a_(CF, CF_STATS, 16), AF.Sqrt)
        VE.wait_ge(sac.h, sac.inc(i))
        i = VE.reciprocal(a_(CF, CF_INVF, 16), a_(CF, CF_STATS, 16))
        invf_dv = sdv.inc(i)

        p5_ev_list = [0, 0, 0, 0]
        p5_st_list = [None] * 4
        p5_pe_pair = [0, 0]
        stWP5 = ST("wp5", 2)
        stWP5A = ST("wp5a", 2)
        stHT = ST("ht", 2)
        stAX = ST("ax", 4)
        stOUT = ST("out", 4)
        idx = 0
        last_q_pe = p4_pe
        for dq in range(4):
            SY.wait_ge(spe.h, last_q_pe)
            SE.wait_ge(spe.h, last_q_pe)
            if dq == 0:
                stP3T.wait_all(SE)
                TE.wait_ge(sac.h, sac.n)
                TE.wait_ge(sdv.h, invf_dv)
            for hb in range(64):
                dst = a_(A, hb * 512, 512)
                srcw = wp_hat[hb * 128:(hb + 1) * 128, dq * 512:(dq + 1) * 512]
                if hb % 2 == 0:
                    wp_s = dS(stWP5, dq, dst, srcw)
                else:
                    wp_2 = dA(stWP5A, dq, dst, srcw)
            stWP5.wait(TE, wp_s)
            stWP5A.wait(TE, wp_2)
            for ttp in range(8):
                bs = a_(B, (ttp % 2) * 16384, 16384)
                SY.wait_ge(spe.h, p5_pe_pair[ttp % 2])
                if dq == 0 and ttp < 2:
                    stH.wait_all(SY)
                for hb8 in range(8):
                    ld = dS(stHT, ttp, g3(bs, 256)[:, hb8 * 8:(hb8 + 1) * 8, :],
                            h_scr5[ttp][hb8 * 8:(hb8 + 1) * 8].rearrange(
                                "a p t -> p a t"))
                stHT.wait(TE, ld)
                for sub in range(2):
                    tt = ttp * 2 + sub
                    q4 = idx % 4
                    psum = PSW[:, q4 * 512:(q4 + 1) * 512]
                    if idx >= 4:
                        TE.wait_ge(sdv.h, p5_ev_list[idx - 4 + 4])
                    for hb in range(64):
                        i = TE.matmul(psum, bs[:, hb * 256 + sub * 128:
                                               hb * 256 + sub * 128 + 128],
                                      a_(A, hb * 512, 512),
                                      start=(hb == 0), stop=(hb == 63))
                    mm_pe = spe.inc(i)
                    p5_pe_pair[ttp % 2] = mm_pe
                    last_q_pe = mm_pe
                    if idx >= 4:
                        SE.wait_ge(sdv.h, p5_ev_list[idx - 4 + 4])
                    axld = dA(stAX, idx, a_(PF, q4 * 512, 512),
                              attnx_scr[tt * 128:(tt + 1) * 128,
                                        dq * 512:(dq + 1) * 512])
                    stAX.wait(VE, axld)
                    VE.wait_ge(spe.h, mm_pe)
                    if idx == 0:
                        VE.wait_ge(sdv.h, invf_dv)
                    if idx >= 4 and p5_st_list[idx - 4 + 4]:
                        stOUT.wait(VE, p5_st_list[idx - 4 + 4])
                    i = VE.tensor_scalar(a_(PF, 2048 + q4 * 512, 512), psum,
                                         a_(CF, CF_INVF + tt, 1), None, ALU.mult)
                    VE.wait_ge(sdv.h, sdv.inc(i))
                    i = VE.tensor_add(a_(PF, 2048 + q4 * 512, 512),
                                      a_(PF, 2048 + q4 * 512, 512),
                                      a_(PF, q4 * 512, 512))
                    p5_ev_list.append(sdv.inc(i))
                    SY.wait_ge(sdv.h, sdv.n)
                    i = SY.dma_start(out=out_t[tt * 128:(tt + 1) * 128,
                                               dq * 512:(dq + 1) * 512],
                                     in_=a_(PF, 2048 + q4 * 512, 512))
                    p5_st_list.append(stOUT.inc(i, idx))
                    idx += 1
        stOUT.wait_all(SY)

        stDBG = ST("dbg")
        for name in dump:
            if name.startswith("_") or name in taps:
                continue
            ap0, shp, dt = scratches[name]
            stOUT.wait_all(GP)
            d_ = dG(stDBG, 0, dbg[name], ap0)
        if dump:
            stDBG.wait_all(GP)
            ST("dbg2").wait_all(GP)

    return nc


def _host_consts():
    idx = np.arange(128)
    diff = idx[None, :] - idx[:, None]
    diffT = np.where(diff >= 0, diff, 1e5).astype(np.float32)
    cplus1 = np.broadcast_to((idx + 1)[None, :], (128, 128)).astype(np.float32).copy()
    c127 = 127 - (np.arange(512) % 128)
    c127rep = np.broadcast_to(c127[None, :], (128, 512)).astype(np.float32).copy()
    ident = np.eye(128, dtype=np.float32)
    onehot = np.zeros((128, 256), dtype=np.float32)
    for j in range(16):
        onehot[:, j * 16 + j] = 1.0
    return diffT, cplus1, c127rep, ident, onehot


def kernel(L0_attn_x, L0_attn_log_decay, L0_attn_Wq, L0_attn_Wk, L0_attn_Wv,
           L0_attn_Wo, L0_attn_Wg1, L0_attn_Wg2, L0_attn_norm_w,
           L0_ffn_w_gate_up, L0_ffn_w_proj, _dump=(), _nc=None):
    f32 = np.float32
    x = np.ascontiguousarray(np.asarray(L0_attn_x, dtype=f32))
    ld = np.asarray(L0_attn_log_decay, dtype=f32)
    Wq = np.ascontiguousarray(np.asarray(L0_attn_Wq, dtype=f32))
    Wk = np.ascontiguousarray(np.asarray(L0_attn_Wk, dtype=f32))
    Wv = np.ascontiguousarray(np.asarray(L0_attn_Wv, dtype=f32))
    Wo = np.ascontiguousarray(np.asarray(L0_attn_Wo, dtype=f32))
    Wg1 = np.ascontiguousarray(np.asarray(L0_attn_Wg1, dtype=f32))
    Wg2 = np.ascontiguousarray(np.asarray(L0_attn_Wg2, dtype=f32))
    nw = np.asarray(L0_attn_norm_w, dtype=f32)
    Wgu = np.ascontiguousarray(np.asarray(L0_ffn_w_gate_up, dtype=f32))
    Wp = np.ascontiguousarray(np.asarray(L0_ffn_w_proj, dtype=f32))

    diffT, cplus1, c127rep, ident, onehot = _host_consts()
    nw16 = np.ascontiguousarray(nw.reshape(16, 128).T)
    ldbc = np.ascontiguousarray(np.broadcast_to(ld[None, :], (128, 16)))

    nc = _nc if _nc is not None else build_program(dump=_dump)
    in_maps = []
    for c in range(NCORE):
        b, hf = c // 2, c % 2
        m = {
            "x": np.ascontiguousarray(x[b, hf * T:(hf + 1) * T, :]),
            "Wq": Wq, "Wk": Wk, "Wv": Wv, "Wo": Wo, "Wg1": Wg1, "Wg2": Wg2,
            "nw16": nw16, "ldbc": ldbc,
            "wgu_sh": np.ascontiguousarray(Wgu[:, c * 256:(c + 1) * 256]),
            "wp_sh": np.ascontiguousarray(Wp[:, c * 1024:(c + 1) * 1024]),
            "flag": np.full((128, 1), float(hf), dtype=f32),
            "diffT": diffT, "cplus1": cplus1, "c127rep": c127rep,
            "ident": ident, "onehot": onehot,
        }
        in_maps.append(m)
    res = run_bass_kernel_spmd(nc, in_maps, core_ids=list(range(NCORE)))
    out = np.empty((4, 4096, 2048), dtype=f32)
    for c in range(NCORE):
        b, hf = c // 2, c % 2
        out[b, hf * T:(hf + 1) * T, :] = res.results[c]["out"]
    if _dump:
        return out, res
    return out


# version 03
# speedup vs baseline: 1.1951x; 1.1951x over previous
import numpy as np
import ml_dtypes
import concourse.bass as bass
import concourse.mybir as mybir
from concourse.bass_utils import run_bass_kernel_spmd

F32 = mybir.dt.float32
BF16 = mybir.dt.bfloat16
AF = mybir.ActivationFunctionType
ALU = mybir.AluOpType
AX = mybir.AxisListType
bf16 = ml_dtypes.bfloat16

T = 2048
D = 2048
H = 16
HD = 128
C = 128
NCH = T // C
HFF = 8192
NCORE = 8
EPS_ATTN = 1e-5
EPS_LN = 1e-5
EPS_FFN = 1.1920929e-07


def build_program(dump=()):
    nc = bass.Bass("TRN2", target_bir_lowering=False)

    x_in = nc.dram_tensor("x", [T, D], F32, kind="ExternalInput").ap()
    wq_in = nc.dram_tensor("Wq", [D, D], F32, kind="ExternalInput").ap()
    wk_in = nc.dram_tensor("Wk", [D, D], F32, kind="ExternalInput").ap()
    wv_in = nc.dram_tensor("Wv", [D, D], F32, kind="ExternalInput").ap()
    wo_in = nc.dram_tensor("Wo", [D, D], F32, kind="ExternalInput").ap()
    wg1_in = nc.dram_tensor("Wg1", [D, HD], F32, kind="ExternalInput").ap()
    wg2_in = nc.dram_tensor("Wg2", [HD, D], F32, kind="ExternalInput").ap()
    nw16_in = nc.dram_tensor("nw16", [128, 16], F32, kind="ExternalInput").ap()
    ldbc_in = nc.dram_tensor("ldbc", [128, 16], F32, kind="ExternalInput").ap()
    wgu_sh_in = nc.dram_tensor("wgu_sh", [2 * HFF, D // NCORE], F32, kind="ExternalInput").ap()
    wp_sh_in = nc.dram_tensor("wp_sh", [D, HFF // NCORE], F32, kind="ExternalInput").ap()
    flag_in = nc.dram_tensor("flag", [128, 1], F32, kind="ExternalInput").ap()
    diffT_in = nc.dram_tensor("diffT", [128, 128], F32, kind="ExternalInput").ap()
    cplus1_in = nc.dram_tensor("cplus1", [128, 128], F32, kind="ExternalInput").ap()
    c127rep_in = nc.dram_tensor("c127rep", [128, 512], F32, kind="ExternalInput").ap()
    ident_in = nc.dram_tensor("ident", [128, 128], F32, kind="ExternalInput").ap()
    onehot_in = nc.dram_tensor("onehot", [128, 256], F32, kind="ExternalInput").ap()
    out_t = nc.dram_tensor("out", [T, D], F32, kind="ExternalOutput").ap()

    xn_scr = nc.dram_tensor("xn_scr", [T, D], BF16, kind="Internal").ap()
    q_scr = nc.dram_tensor("q_scr", [D, T], BF16, kind="Internal").ap()
    k_scr = nc.dram_tensor("k_scr", [D, T], BF16, kind="Internal").ap()
    ktd_scr = nc.dram_tensor("ktd_scr", [D, T], BF16, kind="Internal").ap()
    v_scr = nc.dram_tensor("v_scr", [T, D], BF16, kind="Internal").ap()
    sg_scr = nc.dram_tensor("sg_scr", [T, D], BF16, kind="Internal").ap()
    h_scr = nc.dram_tensor("h_scr", [8 * 64 * 128, 256], BF16, kind="Internal").ap()
    h_scr5 = h_scr.rearrange("(tp hb p) t -> tp hb p t", hb=64, p=128)
    attnx_scr = nc.dram_tensor("attnx_scr", [T, D], F32, kind="Internal").ap()
    wgu_my = nc.dram_tensor("wgu_my", [D // NCORE, 2 * HFF], BF16, kind="Internal").ap()
    wp_my = nc.dram_tensor("wp_my", [HFF // NCORE, D], BF16, kind="Internal").ap()
    wgu_hat = nc.dram_tensor("wgu_hat", [D, 2 * HFF], BF16, kind="Internal").ap()
    wp_hat = nc.dram_tensor("wp_hat", [HFF, D], BF16, kind="Internal").ap()
    xnt_scr = nc.dram_tensor("xnt_scr", [128, 32768], BF16, kind="Internal").ap()
    s_end = nc.dram_tensor("s_end", [128, H * HD], BF16, kind="Internal").ap()
    s_gath = nc.dram_tensor("s_gath", [256, H * HD], BF16, kind="Internal").ap()
    scratches = {"xn_scr": (xn_scr, [T, D], BF16), "q_scr": (q_scr, [D, T], BF16),
                 "k_scr": (k_scr, [D, T], BF16), "ktd_scr": (ktd_scr, [D, T], BF16),
                 "v_scr": (v_scr, [T, D], BF16), "sg_scr": (sg_scr, [T, D], BF16),
                 "attnx_scr": (attnx_scr, [T, D], F32),
                 "wgu_hat": (wgu_hat, [D, 2 * HFF], BF16),
                 "wp_hat": (wp_hat, [HFF, D], BF16),
                 "s_gath": (s_gath, [256, H * HD], BF16),
                 "s_end": (s_end, [128, H * HD], BF16),
                 "h_scr": (h_scr, [8 * 64 * 128, 256], BF16)}
    taps = {"onormt": ([128, 32768], BF16), "wA0": ([128, 16384], BF16),
            "xnt2": ([128, 32768], BF16), "gu0": ([128, 2048], F32),
            "p2c1": ([128, 4096], F32), "p2o1": ([128, 4096], F32),
            "ktd15": ([128, 4096], F32)}
    dbg = {}
    for name in dump:
        if name.startswith("_"):
            continue
        if name in taps:
            shp, dt = taps[name]
        else:
            ap0, shp, dt = scratches[name]
        dbg[name] = nc.dram_tensor("dbg_" + name, shp, dt, kind="ExternalOutput").ap()

    A = nc.alloc_sbuf_tensor("A", [128, 32768], BF16)
    B = nc.alloc_sbuf_tensor("B", [128, 32768], BF16)
    PBF = nc.alloc_sbuf_tensor("PBF", [128, 12288], BF16)
    PF = nc.alloc_sbuf_tensor("PF", [128, 4096], F32)
    CA = nc.alloc_sbuf_tensor("CA", [128, 8192], BF16)
    CF = nc.alloc_sbuf_tensor("CF", [128, 1664], F32)
    QB = nc.alloc_sbuf_tensor("QB", [128, 7168], BF16)

    CF_DIFFT, CF_CP1, CF_C127, CF_LDBC, CF_NW16, CF_FLAG = 0, 128, 256, 768, 784, 800
    CF_KD, CF_CD, CF_IDENT, CF_OH, CF_STATS, CF_INVF, CF_SM = 801, 817, 833, 961, 1217, 1233, 1280
    CA_DT, CA_QD, CA_CDI, CA_I128, CA_OH = 0, 2048, 4096, 6144, 6272

    PSW = nc.alloc_psum_tensor("PSW", [128, 2048], F32)
    PS0 = nc.alloc_psum_tensor("PS0", [128, 512], F32)
    PS1 = nc.alloc_psum_tensor("PS1", [128, 512], F32)
    PS2 = nc.alloc_psum_tensor("PS2", [128, 512], F32)
    PSTR = nc.alloc_psum_tensor("PSTR", [128, 1024], BF16)

    class Sem:
        def __init__(self, name):
            self.h = nc.alloc_semaphore(name)
            self.n = 0

        def inc(self, inst, k=1):
            inst.then_inc(self.h, k)
            self.n += k
            return self.n

    scc = Sem("scc")
    spe = Sem("spe")
    sdv = Sem("sdv")
    sac = Sem("sac")

    class DStream:

        def __init__(self, name, k):
            self.sems = [nc.alloc_semaphore(f"{name}{j}") for j in range(k)]
            self.cnt = [0] * k
            self.k = k

        def inc(self, inst, slot=0):
            j = slot % self.k
            inst.then_inc(self.sems[j], 16)
            self.cnt[j] += 16
            return (j, self.cnt[j])

        def tok(self, slot=0):
            j = slot % self.k
            return (j, self.cnt[j])

        def wait(self, eng, token):
            j, v = token
            eng.wait_ge(self.sems[j], v)

        def wait_all(self, eng):
            for j in range(self.k):
                eng.wait_ge(self.sems[j], self.cnt[j])

    def a_(t, off, w):
        return t[:, off:off + w]

    def g3(ap, k):
        return ap.rearrange("p (g k) -> p g k", k=k)

    STREAMS = {}

    with nc.Block() as blk:
        TE, VE, SE, SY, GP = nc.tensor, nc.vector, nc.scalar, nc.sync, nc.gpsimd

        def ST(name, k=1):
            if name not in STREAMS:
                STREAMS[name] = DStream(name, k)
            return STREAMS[name]

        def dS(st, slot, dst, src):
            return st.inc(SY.dma_start(out=dst, in_=src), slot)

        def dTS(st, slot, dst, src):
            return st.inc(SY.dma_start_transpose(dst, src), slot)

        def dA(st, slot, dst, src):
            return st.inc(SE.dma_start(out=dst, in_=src), slot)

        def dG(st, slot, dst, src):
            return st.inc(GP.dma_start(out=dst, in_=src), slot)

        stC = ST("c")
        dS(stC, 0, a_(CF, CF_DIFFT, 128), diffT_in)
        dS(stC, 0, a_(CF, CF_CP1, 128), cplus1_in)
        dS(stC, 0, a_(CF, CF_C127, 512), c127rep_in)
        dS(stC, 0, a_(CF, CF_LDBC, 16), ldbc_in)
        dS(stC, 0, a_(CF, CF_NW16, 16), nw16_in)
        dS(stC, 0, a_(CF, CF_FLAG, 1), flag_in)
        dS(stC, 0, a_(CF, CF_IDENT, 128), ident_in)
        dS(stC, 0, a_(CF, CF_OH, 256), onehot_in)

        stC.wait_all(SE)
        for h in range(H):
            SE.activation(a_(CA, CA_DT + h * 128, 128), a_(CF, CF_DIFFT, 128), AF.Exp,
                          scale=a_(CF, CF_LDBC + h, 1))
            SE.activation(a_(CA, CA_QD + h * 128, 128), a_(CF, CF_CP1, 128), AF.Exp,
                          scale=a_(CF, CF_LDBC + h, 1))
            i = SE.activation(a_(CF, CF_KD + h, 1), a_(CF, CF_C127, 1), AF.Exp,
                              scale=a_(CF, CF_LDBC + h, 1))
        i = SE.activation(a_(CF, CF_CD, 16), a_(CF, CF_LDBC, 16), AF.Exp, scale=float(C))
        consts_act = sac.inc(i)
        stC.wait_all(VE)
        VE.tensor_copy(a_(CA, CA_I128, 128), a_(CF, CF_IDENT, 128))
        i = VE.tensor_copy(a_(CA, CA_OH, 256), a_(CF, CF_OH, 256))
        VE.wait_ge(sdv.h, sdv.inc(i))
        VE.wait_ge(sac.h, consts_act)
        for h in range(H):
            i = VE.tensor_scalar(a_(CA, CA_CDI + h * 128, 128), a_(CA, CA_I128, 128),
                                 a_(CF, CF_CD + h, 1), None, ALU.mult)
        consts_cdi = sdv.inc(i)

        q_tiles = []
        for r in range(32):
            q_tiles.append(("gu", r))
        for r in range(16):
            q_tiles.append(("wp", r))
        if "_noq" in dump:
            q_tiles = []
        qp1 = a_(QB, 2048, 1024); qp2 = a_(QB, 3072, 1024)
        qtq = a_(QB, 4096, 1024); qwh = a_(QB, 5120, 1024)
        redq = a_(CF, CF_SM + 96, 16)
        sbfq = a_(CA, 6528, 16); hsqq = a_(CA, 6560, 16); nhsq = a_(CA, 6592, 16)
        qstate = {"dv": {}, "tr": 0, "ev": 0, "stg": [None, None], "next": 0,
                  "seeded": False}
        stQL = ST("ql", 2)
        stQS = ST("qs", 2)

        def emit_q_tile():
            it = qstate["next"]
            if it >= len(q_tiles):
                return False
            qstate["next"] += 1
            kind, r = q_tiles[it]
            wbf = a_(QB, (it % 2) * 1024, 1024)
            if it >= 2:
                GP.wait_ge(sdv.h, qstate["dv"][it - 2])
            if kind == "gu":
                qld = dG(stQL, it, g3(wbf, 256),
                         wgu_sh_in[r * 512:(r + 1) * 512, :]
                         .rearrange("(a p) d -> p a d", p=128))
            else:
                qld = dG(stQL, it, wbf, wp_sh_in[r * 128:(r + 1) * 128, :])
            stQL.wait(VE, qld)
            i = VE.tensor_reduce(redq, g3(wbf, 64), axis=AX.X, op=ALU.add,
                                 apply_absolute_value=True)
            VE.wait_ge(sdv.h, sdv.inc(i))
            i = VE.tensor_scalar(sbfq, redq, 1.0 / 64.0, 1e-8, ALU.mult, ALU.max)
            VE.wait_ge(sdv.h, sdv.inc(i))
            VE.tensor_scalar(hsqq, sbfq, 0.5, None, ALU.mult)
            i = VE.tensor_scalar(nhsq, sbfq, -0.5, None, ALU.mult)
            VE.wait_ge(sdv.h, sdv.inc(i))
            a0, a1 = bass.broadcast_tensor_aps(g3(wbf, 64), g3(hsqq, 1))
            VE.tensor_tensor(g3(qp1, 64), a0, a1, op=ALU.is_gt)
            b0, b1 = bass.broadcast_tensor_aps(g3(wbf, 64), g3(nhsq, 1))
            i = VE.tensor_tensor(g3(qp2, 64), b0, b1, op=ALU.is_lt)
            qstate["dv"][it] = sdv.inc(i)
            VE.wait_ge(sdv.h, qstate["dv"][it])
            i = VE.tensor_sub(qtq, qp1, qp2)
            VE.wait_ge(sdv.h, sdv.inc(i))
            VE.wait_ge(spe.h, qstate["tr"])
            c0, c1 = bass.broadcast_tensor_aps(g3(qtq, 64), g3(sbfq, 1))
            i = VE.tensor_tensor(g3(qwh, 64), c0, c1, op=ALU.mult)
            wh_dv = sdv.inc(i)
            TE.wait_ge(sdv.h, wh_dv)
            if not qstate["seeded"]:
                qstate["ev"] = xnT_dv
                qstate["seeded"] = True
            if kind == "gu":
                for dhalf in range(2):
                    par = (2 * it + dhalf) % 2
                    stg = a_(QB, 6144 + par * 512, 512)
                    TE.wait_ge(sdv.h, qstate["ev"])
                    for a4 in range(4):
                        i = TE.transpose(PSTR[:, a4 * 128:(a4 + 1) * 128],
                                         qwh[:, a4 * 256 + dhalf * 128:
                                             a4 * 256 + dhalf * 128 + 128],
                                         a_(CA, CA_I128, 128))
                    qstate["tr"] = spe.inc(i)
                    VE.wait_ge(spe.h, qstate["tr"])
                    if qstate["stg"][par]:
                        stQS.wait(VE, qstate["stg"][par])
                    i = VE.tensor_copy(stg, PSTR[:, 0:512])
                    qstate["ev"] = sdv.inc(i)
                    SY.wait_ge(sdv.h, qstate["ev"])
                    qstate["stg"][par] = dS(
                        stQS, par,
                        wgu_my[dhalf * 128:(dhalf + 1) * 128, r * 512:(r + 1) * 512],
                        stg)
            else:
                for half in range(2):
                    par = (2 * it + half) % 2
                    stg = a_(QB, 6144 + par * 512, 512)
                    TE.wait_ge(sdv.h, qstate["ev"])
                    for a4 in range(4):
                        i = TE.transpose(PSTR[:, a4 * 128:(a4 + 1) * 128],
                                         qwh[:, (half * 4 + a4) * 128:
                                             (half * 4 + a4) * 128 + 128],
                                         a_(CA, CA_I128, 128))
                    qstate["tr"] = spe.inc(i)
                    VE.wait_ge(spe.h, qstate["tr"])
                    if qstate["stg"][par]:
                        stQS.wait(VE, qstate["stg"][par])
                    i = VE.tensor_copy(stg, PSTR[:, 0:512])
                    qstate["ev"] = sdv.inc(i)
                    SY.wait_ge(sdv.h, qstate["ev"])
                    dstw = wp_my.rearrange("(a p) d -> p a d", p=128)[
                        :, half * 4:(half + 1) * 4, r * 128:(r + 1) * 128]
                    qstate["stg"][par] = dS(stQS, par, dstw, g3(stg, 128))
            return True

        def emit_q_gathers():
            stQS.wait_all(GP)
            stQL.wait_all(GP)
            if "_nocc" in dump:
                for rr in range(NCORE):
                    i = GP.dma_start(out=wgu_hat[rr * (D // NCORE):(rr + 1) * (D // NCORE), :],
                                     in_=wgu_my)
                    i.then_inc(scc.h, 16); scc.n += 16
                    i = GP.dma_start(out=wp_hat[rr * (HFF // NCORE):(rr + 1) * (HFF // NCORE), :],
                                     in_=wp_my)
                    i.then_inc(scc.h, 16); scc.n += 16
                GP.wait_ge(scc.h, scc.n)
            else:
                GP.collective_compute("AllGather", ALU.bypass,
                                      replica_groups=[list(range(NCORE))],
                                      ins=[wgu_my.opt()], outs=[wgu_hat.opt()]).then_inc(scc.h, 1)
                scc.n += 1
                GP.collective_compute("AllGather", ALU.bypass,
                                      replica_groups=[list(range(NCORE))],
                                      ins=[wp_my.opt()], outs=[wp_hat.opt()]).then_inc(scc.h, 1)
                scc.n += 1


        p0_act = [0] * 16
        p0_dv = [0] * 16
        p0_tr = [0] * 16
        xnT_evac = 0
        for tt in range(16):
            if tt >= 2:
                SY.wait_ge(sac.h, p0_act[tt - 2])
                SY.wait_ge(sdv.h, p0_dv[tt - 2])
            stX = ST("x", 2)
            xl = dS(stX, tt, a_(PF, (tt % 2) * 2048, 2048), x_in[tt * 128:(tt + 1) * 128, :])
            xt = a_(PF, (tt % 2) * 2048, 2048)
            sums = a_(CF, CF_SM, 1); sums2 = a_(CF, CF_SM + 1, 1)
            m_ = a_(CF, CF_SM + 2, 1); var_ = a_(CF, CF_SM + 3, 1)
            sd_ = a_(CF, CF_SM + 4, 1); rstd_ = a_(CF, CF_SM + 5, 1)
            msq_ = a_(CF, CF_SM + 6, 1)
            stX.wait(SE, xl)
            if tt >= 1:
                SE.wait_ge(sdv.h, p0_dv[tt - 1])
            i = SE.activation(PSW[:], xt, AF.Copy, accum_out=sums)
            SE.wait_ge(sac.h, sac.inc(i))
            i = SE.activation(PSW[:], xt, AF.Square, accum_out=sums2)
            p0_act[tt] = sac.inc(i)
            VE.wait_ge(sac.h, p0_act[tt])
            VE.tensor_scalar(m_, sums, 1.0 / D, None, ALU.mult)
            i = VE.tensor_scalar(var_, sums2, 1.0 / D, None, ALU.mult)
            VE.wait_ge(sdv.h, sdv.inc(i))
            i = VE.tensor_tensor(msq_, m_, m_, op=ALU.mult)
            VE.wait_ge(sdv.h, sdv.inc(i))
            i = VE.tensor_tensor(var_, var_, msq_, op=ALU.subtract)
            VE.wait_ge(sdv.h, sdv.inc(i))
            i = VE.tensor_scalar(var_, var_, 1.0, EPS_LN, ALU.mult, ALU.add)
            SE.wait_ge(sdv.h, sdv.inc(i))
            i = SE.activation(sd_, var_, AF.Sqrt)
            VE.wait_ge(sac.h, sac.inc(i))
            i = VE.reciprocal(rstd_, sd_)
            VE.wait_ge(sdv.h, sdv.inc(i))
            if tt >= 2:
                VE.wait_ge(spe.h, p0_tr[tt - 2])
            i = VE.tensor_scalar(a_(PBF, (tt % 2) * 2048, 2048), xt, m_, rstd_,
                                 ALU.subtract, ALU.mult)
            p0_dv[tt] = sdv.inc(i)
            TE.wait_ge(sdv.h, p0_dv[tt])
            for rnd in range(2):
                TE.wait_ge(sdv.h, xnT_evac)
                for a8 in range(8):
                    dt = rnd * 8 + a8
                    i = TE.transpose(PSTR[:, a8 * 128:(a8 + 1) * 128],
                                     a_(PBF, (tt % 2) * 2048 + dt * 128, 128),
                                     a_(CA, CA_I128, 128))
                p0_tr[tt] = spe.inc(i)
                VE.wait_ge(spe.h, p0_tr[tt])
                dstB = B[:].rearrange("p (dt t) -> p dt t", t=2048)[
                    :, rnd * 8:(rnd + 1) * 8, tt * 128:(tt + 1) * 128]
                i = VE.tensor_copy(dstB, g3(PSTR[:], 128))
                xnT_evac = sdv.inc(i)
        xnT_dv = xnT_evac
        stXNT = ST("xnt")
        SY.wait_ge(sdv.h, xnT_dv)
        xnt_st = dS(stXNT, 0, xnt_scr, B[:])

        stWG1 = ST("wg1")
        for dt in range(16):
            wg1_ld = dG(stWG1, 0, a_(PBF, 10240 + dt * 128, 128),
                        wg1_in[dt * 128:(dt + 1) * 128, :])
        stWG1.wait(TE, wg1_ld)
        TE.wait_ge(sdv.h, xnT_dv)
        g1_ev = [0] * 4
        for tc in range(4):
            ps = (PS0, PS1)[tc % 2]
            if tc >= 2:
                TE.wait_ge(sdv.h, g1_ev[tc - 2])
            for dt in range(16):
                i = TE.matmul(ps[:], a_(PBF, 10240 + dt * 128, 128),
                              a_(B, dt * 2048 + tc * 512, 512),
                              start=(dt == 0), stop=(dt == 15))
            spe.inc(i)
            VE.wait_ge(spe.h, spe.n)
            i = VE.tensor_copy(a_(PBF, tc * 512, 512), ps[:])
            g1_ev[tc] = sdv.inc(i)
        g1_done = sdv.n
        g1_pe = spe.n

        stWH = ST("wh2", 2)
        whalf_idx = [0]

        def load_w_half(w_in, half, base, wait_pe):
            GP.wait_ge(spe.h, wait_pe)
            slot = whalf_idx[0]
            whalf_idx[0] += 1
            for dt in range(16):
                r = dG(stWH, slot, a_(A, base + dt * 1024, 1024),
                       w_in[dt * 128:(dt + 1) * 128, half * 1024:(half + 1) * 1024])
            return r

        state = {"qrow_free": None, "krow_free": None, "ktdrow_free": None, "kdrep_rd": 0}

        def proj_lhsT(w_in, dst_scr, decay_scr, wait_pe0):
            h0 = load_w_half(w_in, 0, 0, wait_pe0)
            h1 = load_w_half(w_in, 1, 16384, wait_pe0)
            half_ld = (h0, h1)
            for et in range(16):
                emit_q_tile()
                half = et // 8
                hd_ = et % 8
                base = 16384 * half
                stWH.wait(TE, half_ld[half])
                row = a_(PBF, 2048, 2048) if decay_scr is None else a_(PBF, 4096, 2048)
                stR = ST("rq") if decay_scr is None else ST("rk")
                stRT = ST("rkt")
                row_free = state["qrow_free"] if decay_scr is None else state["krow_free"]
                for tc in range(4):
                    ps = (PS0, PS1)[tc % 2]
                    TE.wait_ge(sac.h, sac.n)
                    for dt in range(16):
                        i = TE.matmul(ps[:], a_(A, base + dt * 1024 + hd_ * 128, 128),
                                      a_(B, dt * 2048 + tc * 512, 512),
                                      start=(dt == 0), stop=(dt == 15))
                    spe.inc(i)
                    SE.wait_ge(spe.h, spe.n)
                    if tc == 0 and row_free:
                        stR.wait(SE, row_free)
                    i = SE.activation(row[:, tc * 512:(tc + 1) * 512], ps[:], AF.Silu)
                    sac.inc(i)
                act_done = sac.n
                if decay_scr is not None:
                    SE.wait_ge(sdv.h, state["kdrep_rd"])
                    i = SE.activation(a_(PBF, 10240, 512), a_(CF, CF_C127, 512), AF.Exp,
                                      scale=a_(CF, CF_LDBC + et, 1))
                    sac.inc(i)
                    VE.wait_ge(sac.h, sac.n)
                    if state["ktdrow_free"]:
                        stRT.wait(VE, state["ktdrow_free"])
                    r0, r1 = bass.broadcast_tensor_aps(
                        row.rearrange("p (r k) -> p r k", k=512),
                        a_(PBF, 10240, 512).rearrange("p (r k) -> p r k", r=1))
                    i = VE.tensor_tensor(
                        a_(PBF, 6144, 2048).rearrange("p (r k) -> p r k", k=512),
                        r0, r1, op=ALU.mult)
                    state["kdrep_rd"] = sdv.inc(i)
                    if "ktd15" in dump and et == 15:
                        VE.wait_ge(sdv.h, state["kdrep_rd"])
                        VE.tensor_copy(a_(PF, 0, 2048), a_(PBF, 6144, 2048))
                        VE.tensor_copy(a_(PF, 2048, 512), a_(PBF, 10240, 512))
                        i = VE.tensor_copy(a_(PF, 2560, 1024), a_(PBF, 4096, 1024))
                        k15dv = sdv.inc(i)
                        SY.wait_ge(sdv.h, k15dv)
                        dS(ST("dbg2"), 0, dbg["ktd15"], PF[:])
                    SY.wait_ge(sdv.h, state["kdrep_rd"])
                    state["ktdrow_free"] = dS(stRT, 0, decay_scr[et * 128:(et + 1) * 128, :],
                                              a_(PBF, 6144, 2048))
                SY.wait_ge(sac.h, act_done)
                st = dS(stR, 0, dst_scr[et * 128:(et + 1) * 128, :], row)
                if decay_scr is None:
                    state["qrow_free"] = st
                else:
                    state["krow_free"] = st

        TE.wait_ge(sdv.h, g1_done)
        proj_lhsT(wq_in, q_scr, None, 0)
        proj_lhsT(wk_in, k_scr, ktd_scr, spe.n)

        vh0 = load_w_half(wv_in, 0, 0, spe.n)
        vh1 = load_w_half(wv_in, 1, 16384, spe.n)
        vrow_free = {}
        stRV = ST("rv", 2)
        for tt in range(16):
            for half in range(2):
                emit_q_tile()
                base = 16384 * half
                stWH.wait(TE, (vh0, vh1)[half])
                vr = a_(PBF, 8192 + ((tt * 2 + half) % 2) * 1024, 1024)
                for ec in range(2):
                    ps = (PS0, PS1)[ec]
                    TE.wait_ge(sac.h, sac.n)
                    for dt in range(16):
                        i = TE.matmul(ps[:], a_(B, dt * 2048 + tt * 128, 128),
                                      a_(A, base + dt * 1024 + ec * 512, 512),
                                      start=(dt == 0), stop=(dt == 15))
                    spe.inc(i)
                    SE.wait_ge(spe.h, spe.n)
                    if ec == 0 and vrow_free.get((tt * 2 + half) % 2):
                        stRV.wait(SE, vrow_free[(tt * 2 + half) % 2])
                    i = SE.activation(vr[:, ec * 512:(ec + 1) * 512], ps[:], AF.Silu)
                    sac.inc(i)
                SY.wait_ge(sac.h, sac.n)
                vrow_free[(tt * 2 + half) % 2] = dS(
                    stRV, (tt * 2 + half),
                    v_scr[tt * 128:(tt + 1) * 128, half * 1024:(half + 1) * 1024], vr)
        p1_pe = spe.n

        while emit_q_tile():
            pass
        emit_q_gathers()

        VE.wait_ge(sdv.h, consts_cdi)
        for s_ in ("rq", "rk", "rkt", "rv"):
            ST(s_).wait_all(VE)
        i = VE.memset(a_(PBF, 6144, 2048), 0.0)
        a_sprev = sdv.inc(i)
        stACH = ST("ach")
        for s_ in ("rq", "rk", "rkt", "rv"):
            ST(s_).wait_all(SY)
        dTS(stACH, 0, a_(PBF, 2048, 2048), ktd_scr[:, 0:128])
        a_ld = dS(stACH, 0, a_(PBF, 4096, 2048), v_scr[0:128, :])
        for ch in range(NCH):
            scur = a_(PBF, 6144 + (ch % 2) * 2048, 2048)
            snxt = a_(PBF, 6144 + ((ch + 1) % 2) * 2048, 2048)
            stACH.wait(TE, a_ld)
            TE.wait_ge(sdv.h, a_sprev)
            if ch == 0:
                TE.wait_ge(sac.h, sac.n)
            for q in range(4):
                ps = (PS0, PS1, PS2, PSW[:, 0:512])[q]
                for hh in range(4 * q, 4 * q + 4):
                    TE.matmul(ps[:, (hh % 4) * 128:(hh % 4 + 1) * 128],
                              a_(PBF, 2048 + hh * 128, 128),
                              a_(PBF, 4096 + hh * 128, 128),
                              start=(hh % 4 == 0), stop=False, skip_group_check=True)
                for hh in range(4 * q, 4 * q + 4):
                    i = TE.matmul(ps[:, (hh % 4) * 128:(hh % 4 + 1) * 128],
                                  a_(CA, CA_CDI + hh * 128, 128),
                                  scur[:, hh * 128:(hh + 1) * 128],
                                  start=False, stop=(hh % 4 == 3), skip_group_check=True)
                spe.inc(i)
                VE.wait_ge(spe.h, spe.n)
                i = VE.tensor_copy(snxt[:, q * 512:(q + 1) * 512], ps[:])
                sdv.inc(i)
            a_sprev = sdv.n
            if ch + 1 < NCH:
                SY.wait_ge(spe.h, spe.n)
                dTS(stACH, 0, a_(PBF, 2048, 2048), ktd_scr[:, (ch + 1) * 128:(ch + 2) * 128])
                a_ld = dS(stACH, 0, a_(PBF, 4096, 2048),
                          v_scr[(ch + 1) * 128:(ch + 2) * 128, :])
        s_final = a_(PBF, 6144 + (NCH % 2) * 2048, 2048)
        stSE = ST("se")
        SY.wait_ge(sdv.h, a_sprev)
        se_st = dS(stSE, 0, s_end, s_final)
        stSE.wait(GP, se_st)
        if "_nocc" in dump:
            i = GP.dma_start(out=s_gath[0:128, :], in_=s_end)
            i.then_inc(scc.h, 16); scc.n += 16
        else:
            GP.collective_compute("AllGather", ALU.bypass,
                                  replica_groups=[[0, 1], [2, 3], [4, 5], [6, 7]],
                                  ins=[s_end.opt()], outs=[s_gath.opt()]).then_inc(scc.h, 1)
            scc.n += 1
        cc_s = scc.n

        stWG2 = ST("wg2")
        wg2_ld = dG(stWG2, 0, a_(PBF, 2048, 2048), wg2_in)
        stSG = ST("sgst", 2)
        sg_st = [None, None]
        g_idx = 0
        for tt in range(16):
            for ec in range(4):
                ps = (PS0, PS1)[g_idx % 2]
                if g_idx == 0:
                    stWG2.wait(TE, wg2_ld)
                    TE.wait_ge(sdv.h, sdv.n)
                TE.wait_ge(sac.h, sac.n)
                i = TE.matmul(ps[:], a_(PBF, tt * 128, 128),
                              a_(PBF, 2048 + ec * 512, 512), start=True, stop=True)
                spe.inc(i)
                SE.wait_ge(spe.h, spe.n)
                if sg_st[g_idx % 2]:
                    stSG.wait(SE, sg_st[g_idx % 2])
                i = SE.activation(a_(PBF, 4096 + (g_idx % 2) * 512, 512), ps[:], AF.Sigmoid)
                sac.inc(i)
                SY.wait_ge(sac.h, sac.n)
                sg_st[g_idx % 2] = dS(
                    stSG, g_idx,
                    sg_scr[tt * 128:(tt + 1) * 128, ec * 512:(ec + 1) * 512],
                    a_(PBF, 4096 + (g_idx % 2) * 512, 512))
                g_idx += 1
        g_pe = spe.n

        def bslot(stream, ch):
            offs = {"q": 0, "k": 4096, "kt": 8192, "v": 12288, "sg": 16384}
            return a_(B, offs[stream] + (ch % 2) * 2048, 2048)

        stSI = ST("si")
        SY.wait_ge(scc.h, scc.n)
        stXNT.wait(SY, xnt_st)
        si_ld = dS(stSI, 0, a_(B, 20480, 2048), s_gath[0:128, :])
        stSI.wait(VE, si_ld)
        i = VE.tensor_scalar(a_(PBF, 8192, 2048), a_(B, 20480, 2048),
                             a_(CF, CF_FLAG, 1), None, ALU.mult)
        s_ready = sdv.inc(i)

        stC2S = ST("c2s", 2)
        stC2A = ST("c2a", 2)

        def p2_loads(ch):
            dS(stC2S, ch, g3(bslot("q", ch), 128),
               q_scr[:, ch * 128:(ch + 1) * 128].rearrange("(a p) t -> p a t", p=128))
            dS(stC2S, ch, g3(bslot("k", ch), 128),
               k_scr[:, ch * 128:(ch + 1) * 128].rearrange("(a p) t -> p a t", p=128))
            l1 = dTS(stC2S, ch, bslot("kt", ch), ktd_scr[:, ch * 128:(ch + 1) * 128])
            dA(stC2A, ch, bslot("v", ch), v_scr[ch * 128:(ch + 1) * 128, :])
            l2 = dA(stC2A, ch, bslot("sg", ch), sg_scr[ch * 128:(ch + 1) * 128, :])
            return (l1, l2)

        ST("rq").wait_all(SY)
        ST("rk").wait_all(SY)
        ST("sgst", 2).wait_all(SY)
        ST("rq").wait_all(SE)
        ST("rv", 2).wait_all(SE)
        ST("sgst", 2).wait_all(SE)
        stXNT.wait(SE, xnt_st)
        p2_ld = {0: p2_loads(0), 1: p2_loads(1)}
        sct_ev = {}
        o_pe_prev = g_pe
        su_pe_prev = g_pe
        odot_prev = 0
        tr_pe_prev = 0
        p2_sdone = s_ready
        for ch in range(NCH):
            scur = a_(PBF, 8192 + (ch % 2) * 2048, 2048)
            snxt = a_(PBF, 8192 + ((ch + 1) % 2) * 2048, 2048)
            stC2S.wait(TE, p2_ld[ch][0])
            stC2A.wait(TE, p2_ld[ch][1])
            TE.wait_ge(sdv.h, p2_sdone)
            if ch == 0:
                TE.wait_ge(sac.h, sac.n)
            VE.wait_ge(spe.h, o_pe_prev)
            VE.wait_ge(sac.h, sac.n)
            for hq in range(4):
                ps = (PS0, PS1)[hq % 2]
                TE.wait_ge(sdv.h, sct_ev.get((ch, hq - 2), sct_ev.get((ch - 1, hq + 2), 0)))
                for hh in range(4 * hq, 4 * hq + 4):
                    i = TE.matmul(ps[:, (hh % 4) * 128:(hh % 4 + 1) * 128],
                                  bslot("k", ch)[:, hh * 128:(hh + 1) * 128],
                                  bslot("q", ch)[:, hh * 128:(hh + 1) * 128],
                                  start=(hh % 4 == 0), stop=(hh % 4 == 3),
                                  skip_group_check=True)
                spe.inc(i)
                VE.wait_ge(spe.h, spe.n)
                i = VE.tensor_tensor(a_(PBF, hq * 512, 512), ps[:],
                                     a_(CA, CA_DT + hq * 512, 512), op=ALU.mult)
                sct_ev[(ch, hq)] = sdv.inc(i)
            i = VE.tensor_mul(a_(PBF, 2048, 2048), bslot("q", ch), a_(CA, CA_QD, 2048))
            qtl_dv = sdv.inc(i)
            TE.wait_ge(sdv.h, qtl_dv)
            TE.wait_ge(sdv.h, odot_prev)
            for hh in range(H):
                TE.wait_ge(sdv.h, sct_ev[(ch, hh // 4)]) if hh % 4 == 0 else None
                TE.matmul(PSW[:, hh * 128:(hh + 1) * 128],
                          a_(PBF, hh * 128, 128),
                          bslot("v", ch)[:, hh * 128:(hh + 1) * 128],
                          start=(hh % 4 == 0), stop=False, skip_group_check=True)
                i = TE.matmul(PSW[:, hh * 128:(hh + 1) * 128],
                              a_(PBF, 2048 + hh * 128, 128),
                              scur[:, hh * 128:(hh + 1) * 128],
                              start=False, stop=(hh % 4 == 3), skip_group_check=True)
            o_pe = spe.inc(i)
            su_ev = 0
            for q in range(4):
                if q > 0:
                    TE.wait_ge(sdv.h, su_ev)
                for hh in range(4 * q, 4 * q + 4):
                    TE.matmul(PS2[:, (hh % 4) * 128:(hh % 4 + 1) * 128],
                              bslot("kt", ch)[:, hh * 128:(hh + 1) * 128],
                              bslot("v", ch)[:, hh * 128:(hh + 1) * 128],
                              start=(hh % 4 == 0), stop=False, skip_group_check=True)
                for hh in range(4 * q, 4 * q + 4):
                    i = TE.matmul(PS2[:, (hh % 4) * 128:(hh % 4 + 1) * 128],
                                  a_(CA, CA_CDI + hh * 128, 128),
                                  scur[:, hh * 128:(hh + 1) * 128],
                                  start=False, stop=(hh % 4 == 3), skip_group_check=True)
                spe.inc(i)
                VE.wait_ge(spe.h, spe.n)
                i = VE.tensor_copy(snxt[:, q * 512:(q + 1) * 512], PS2[:])
                su_ev = sdv.inc(i)
            p2_sdone = su_ev
            su_pe = spe.n
            VE.wait_ge(spe.h, o_pe)
            if "p2o1" in dump and ch == 1:
                VE.tensor_copy(a_(PF, 0, 2048), PSW[:])
                VE.tensor_copy(a_(PF, 2048, 512), scur[:, 1536:2048])
                VE.tensor_copy(a_(PF, 2560, 128), a_(PBF, 2048 + 15 * 128, 128))
                VE.tensor_copy(a_(PF, 2688, 128), a_(PBF, 15 * 128, 128))
                i = VE.tensor_copy(a_(PF, 2816, 128), bslot("v", ch)[:, 15 * 128:16 * 128])
                o1dv = sdv.inc(i)
                SY.wait_ge(sdv.h, o1dv)
                dS(ST("dbg2"), 0, dbg["p2o1"], PF[:])
            i = VE.tensor_mul(a_(PBF, 2048, 2048), PSW[:], bslot("sg", ch))
            odot_dv = sdv.inc(i)
            odot_prev = odot_dv
            SE.wait_ge(sdv.h, odot_dv)
            i = SE.activation(a_(PBF, 4096, 2048), a_(PBF, 2048, 2048), AF.Square)
            osq_ac = sac.inc(i)
            VE.wait_ge(sac.h, osq_ac)
            i = VE.tensor_reduce(a_(CF, CF_SM + 16, 16), g3(a_(PBF, 4096, 2048), 128),
                                 axis=AX.X, op=ALU.add)
            VE.wait_ge(sdv.h, sdv.inc(i))
            i = VE.tensor_scalar(a_(CF, CF_SM + 32, 16), a_(CF, CF_SM + 16, 16),
                                 1.0 / HD, EPS_ATTN, ALU.mult, ALU.add)
            SE.wait_ge(sdv.h, sdv.inc(i))
            i = SE.activation(a_(CF, CF_SM + 48, 16), a_(CF, CF_SM + 32, 16), AF.Sqrt)
            VE.wait_ge(sac.h, sac.inc(i))
            i = VE.reciprocal(a_(CF, CF_SM + 64, 16), a_(CF, CF_SM + 48, 16))
            VE.wait_ge(sdv.h, sdv.inc(i))
            VE.wait_ge(spe.h, tr_pe_prev)
            in0, in1 = bass.broadcast_tensor_aps(g3(a_(PBF, 2048, 2048), 128),
                                                 g3(a_(CF, CF_SM + 64, 16), 1))
            i = VE.tensor_tensor(g3(a_(PBF, 6144, 2048), 128), in0, in1, op=ALU.mult)
            onrm_dv = sdv.inc(i)
            if "p2c1" in dump and ch == 1:
                VE.tensor_copy(a_(PF, 0, 2048), a_(PBF, 2048, 2048))
                VE.tensor_copy(a_(PF, 2048, 16), a_(CF, CF_SM + 16, 16))
                VE.tensor_copy(a_(PF, 2064, 16), a_(CF, CF_SM + 64, 16))
                i = VE.tensor_copy(a_(PF, 2080, 2016), a_(PBF, 6144, 2016))
                c1dv = sdv.inc(i)
                SY.wait_ge(sdv.h, c1dv)
                dS(ST("dbg2"), 0, dbg["p2c1"], PF[:])
            tr_ev = {}
            for grp in range(2):
                TE.wait_ge(sdv.h, onrm_dv)
                if ch == 0 and grp == 0:
                    p2_trev = {}
                if grp == 1:
                    TE.wait_ge(sdv.h, p2_trev[(ch, 0)])
                elif ch > 0:
                    TE.wait_ge(sdv.h, p2_trev[(ch - 1, 1)])
                for s8 in range(8):
                    hh = grp * 8 + s8
                    i = TE.transpose(PSTR[:, s8 * 128:(s8 + 1) * 128],
                                     a_(PBF, 6144 + hh * 128, 128), a_(CA, CA_I128, 128))
                spe.inc(i)
                VE.wait_ge(spe.h, spe.n)
                dst = A[:].rearrange("p (dt t) -> p dt t", t=2048)[
                    :, grp * 8:(grp + 1) * 8, ch * 128:(ch + 1) * 128]
                i = VE.tensor_copy(dst, g3(PSTR[:], 128))
                p2_trev[(ch, grp)] = sdv.inc(i)
            tr_pe_prev = spe.n
            o_pe_prev = su_pe
            if ch + 2 < NCH:
                SY.wait_ge(spe.h, su_pe)
                SY.wait_ge(sdv.h, odot_dv)
                SE.wait_ge(sdv.h, odot_dv)
                p2_ld[ch + 2] = p2_loads(ch + 2)
        p2_done = sdv.n

        p3_ev = {}
        p3_st = {}
        p3_ev_list = [0]
        p3_st_list = [None]
        wo_scaled = 0
        stWO = ST("wo", 2)
        stP3X = ST("p3x", 2)
        stP3T = ST("p3t", 2)
        for eh in range(2):
            GP.wait_ge(sdv.h, p2_done if eh == 0 else wo_scaled)
            for dt in range(16):
                wo_ld = dG(stWO, eh, a_(B, 16384 + dt * 1024, 1024),
                           wo_in[dt * 128:(dt + 1) * 128, eh * 1024:(eh + 1) * 1024])
            stWO.wait(VE, wo_ld)
            VE.wait_ge(spe.h, spe.n)
            for dt in range(16):
                i = VE.tensor_scalar(a_(B, dt * 1024, 1024), a_(B, 16384 + dt * 1024, 1024),
                                     a_(CF, CF_NW16 + dt, 1), None, ALU.mult)
            wo_scaled = sdv.inc(i)
            TE.wait_ge(sdv.h, wo_scaled)
            for tt in range(16):
                idx = eh * 16 + tt
                pso = (tt % 2) * 1024
                if idx >= 2:
                    TE.wait_ge(sdv.h, p3_ev_list[idx - 2 + 1])
                for ec in range(2):
                    for dt in range(16):
                        i = TE.matmul(PSW[:, pso + ec * 512:pso + (ec + 1) * 512],
                                      a_(A, dt * 2048 + tt * 128, 128),
                                      a_(B, dt * 1024 + ec * 512, 512),
                                      start=(dt == 0), stop=(dt == 15))
                spe.inc(i)
                if idx >= 2:
                    SE.wait_ge(sdv.h, p3_ev_list[idx - 2 + 1])
                xld = dA(stP3X, idx, a_(PF, (idx % 2) * 1024, 1024),
                         x_in[tt * 128:(tt + 1) * 128, eh * 1024:(eh + 1) * 1024])
                stP3X.wait(VE, xld)
                VE.wait_ge(spe.h, spe.n)
                if idx >= 2:
                    stP3T.wait(VE, p3_st_list[idx - 2 + 1])
                i = VE.tensor_add(a_(PF, 2048 + (idx % 2) * 1024, 1024),
                                  PSW[:, pso:pso + 1024], a_(PF, (idx % 2) * 1024, 1024))
                p3_ev_list.append(sdv.inc(i))
                SY.wait_ge(sdv.h, sdv.n)
                p3_st_list.append(dS(
                    stP3T, idx,
                    attnx_scr[tt * 128:(tt + 1) * 128, eh * 1024:(eh + 1) * 1024],
                    a_(PF, 2048 + (idx % 2) * 1024, 1024)))
        p3_pe = spe.n
        onormt_ld = None
        if "onormt" in dump:
            GP.wait_ge(spe.h, p3_pe)
            onormt_ld = dG(ST("dbg3"), 0, dbg["onormt"], A[:])

        stP3T.wait_all(SY)
        stXNT.wait_all(SY)
        xnT_ld2 = dS(stXNT, 0, B[:], xnt_scr)
        NSUP = 16
        SUPH = 512

        stSUPS = ST("sups", 2)
        stSUPA = ST("supa", 2)

        def p4_load_super(s, par, wait_pe):
            base = (par % 2) * 16384
            SY.wait_ge(spe.h, wait_pe)
            SE.wait_ge(spe.h, wait_pe)
            if onormt_ld is not None:
                ST("dbg3").wait(SY, onormt_ld)
                ST("dbg3").wait(SE, onormt_ld)
            for gup in range(2):
                coff = gup * HFF + s * SUPH
                boff = base + gup * 8192
                for dt in range(16):
                    dst = a_(A, boff + dt * 512, 512)
                    srcw = wgu_hat[dt * 128:(dt + 1) * 128, coff:coff + SUPH]
                    if dt % 2 == 0:
                        ls = dS(stSUPS, s, dst, srcw)
                    else:
                        l2 = dA(stSUPA, s, dst, srcw)
            return (ls, l2)

        SY.wait_ge(scc.h, scc.n)
        SE.wait_ge(scc.h, scc.n)
        sup_ld = {0: p4_load_super(0, 0, p3_pe), 1: p4_load_super(1, 1, p3_pe)}
        wa0_ld = None
        if "wA0" in dump:
            stSUPS.wait(GP, sup_ld[0][0])
            stSUPA.wait(GP, sup_ld[0][1])
            wa0_ld = dG(ST("dbg3"), 0, dbg["wA0"], a_(A, 0, 16384))
        if "xnt2" in dump:
            stXNT.wait(GP, xnT_ld2)
            dG(ST("dbg3"), 0, dbg["xnt2"], B[:])
        first_stats = True
        PP = [(PS0, PS1), (PSW[:, 0:512], PSW[:, 512:1024]),
              (PSW[:, 1024:1536], PSW[:, 1536:2048])]
        sil_done = [0] * 3
        h_done = [0] * 3
        hst_done = [None] * 3
        stH = ST("hst", 3)
        stats_q = []
        stats_pe = 0
        pidx = 0
        for s in range(NSUP):
            base = (s % 2) * 16384
            stSUPS.wait(TE, sup_ld[s][0])
            stSUPA.wait(TE, sup_ld[s][1])
            if s == 0:
                stXNT.wait(TE, xnT_ld2)
                TE.wait_ge(sdv.h, p3_ev_list[-1])
            for pb in range(4):
                for tc in range(4):
                    par = pidx % 3
                    pidx += 1
                    psg, psu = PP[par]
                    TE.wait_ge(sac.h, sil_done[par])
                    TE.wait_ge(sdv.h, h_done[par])
                    for dt in range(16):
                        TE.matmul(psg[:], a_(A, base + dt * 512 + pb * 128, 128),
                                  a_(B, dt * 2048 + tc * 512, 512),
                                  start=(dt == 0), stop=(dt == 15))
                    for dt in range(16):
                        i = TE.matmul(psu[:], a_(A, base + 8192 + dt * 512 + pb * 128, 128),
                                      a_(B, dt * 2048 + tc * 512, 512),
                                      start=(dt == 0), stop=(dt == 15))
                    gu_pe = spe.inc(i)
                    if "gu0" in dump and s == 0 and pb == 0 and tc == 0:
                        VE.wait_ge(spe.h, gu_pe)
                        VE.tensor_copy(a_(PF, 0, 512), psg[:])
                        i = VE.tensor_copy(a_(PF, 512, 512), psu[:])
                        gu0dv = sdv.inc(i)
                    SE.wait_ge(spe.h, gu_pe)
                    SE.wait_ge(sdv.h, h_done[par])
                    i = SE.activation(a_(PBF, par * 1536, 512), psg[:], AF.Silu)
                    sil_done[par] = sac.inc(i)
                    VE.wait_ge(sac.h, sil_done[par])
                    if hst_done[par]:
                        stH.wait(VE, hst_done[par])
                    VE.wait_ge(spe.h, stats_pe)
                    i = VE.tensor_mul(a_(PBF, 512 + par * 1536, 512),
                                      a_(PBF, par * 1536, 512), psu[:])
                    h_done[par] = sdv.inc(i)
                    SE.wait_ge(sdv.h, h_done[par])
                    SE.wait_ge(spe.h, stats_pe)
                    i = SE.activation(a_(PBF, 1024 + par * 1536, 512),
                                      a_(PBF, 512 + par * 1536, 512), AF.Square)
                    hsq_ac = sac.inc(i)
                    stats_q.append((par, tc, hsq_ac))
                    if len(stats_q) >= 3:
                        spar, stc, sac_v = stats_q.pop(0)
                        TE.wait_ge(sac.h, sac_v)
                        for ts in range(4):
                            gsub = stc * 4 + ts
                            i = TE.matmul(PS2[:, 0:16],
                                          a_(PBF, 1024 + spar * 1536 + ts * 128, 128),
                                          a_(CA, CA_OH + gsub * 16, 16),
                                          start=first_stats, stop=False,
                                          skip_group_check=True)
                            first_stats = False
                        stats_pe = spe.inc(i)
                    if "gu0" in dump and s == 0 and pb == 0 and tc == 0:
                        VE.wait_ge(sac.h, sil_done[par])
                        VE.tensor_copy(a_(PF, 1024, 512), a_(PBF, par * 1536, 512))
                        VE.wait_ge(sdv.h, h_done[par])
                        i = VE.tensor_copy(a_(PF, 1536, 512), a_(PBF, 512 + par * 1536, 512))
                        gu0dv = sdv.inc(i)
                        SY.wait_ge(sdv.h, gu0dv)
                        dS(ST("dbg2"), 0, dbg["gu0"], PF[:, 0:2048])
                    SY.wait_ge(sdv.h, h_done[par])
                    hb = s * 4 + pb
                    hst_done[par] = dS(
                        stH, par,
                        h_scr5[tc * 2:(tc + 1) * 2, hb, :, :].rearrange("a p t -> p a t"),
                        g3(a_(PBF, 512 + par * 1536, 512), 256))
            if s + 2 < NSUP:
                if wa0_ld is not None and s == 0:
                    ST("dbg3").wait(SY, wa0_ld)
                    ST("dbg3").wait(SE, wa0_ld)
                sup_ld[s + 2] = p4_load_super(s + 2, s, spe.n)
        while stats_q:
            spar, stc, sac_v = stats_q.pop(0)
            TE.wait_ge(sac.h, sac_v)
            for ts in range(4):
                gsub = stc * 4 + ts
                i = TE.matmul(PS2[:, 0:16],
                              a_(PBF, 1024 + spar * 1536 + ts * 128, 128),
                              a_(CA, CA_OH + gsub * 16, 16),
                              start=first_stats, stop=False, skip_group_check=True)
                first_stats = False
            stats_pe = spe.inc(i)
        p4_pe = spe.n

        VE.wait_ge(spe.h, p4_pe)
        i = VE.tensor_scalar(a_(CF, CF_STATS, 16), PS2[:, 0:16], 1.0 / HFF, EPS_FFN,
                             ALU.mult, ALU.add)
        SE.wait_ge(sdv.h, sdv.inc(i))
        i = SE.activation(a_(CF, CF_STATS, 16), a_(CF, CF_STATS, 16), AF.Sqrt)
        VE.wait_ge(sac.h, sac.inc(i))
        i = VE.reciprocal(a_(CF, CF_INVF, 16), a_(CF, CF_STATS, 16))
        invf_dv = sdv.inc(i)

        p5_ev_list = [0, 0, 0, 0]
        p5_st_list = [None] * 4
        p5_pe_pair = [0, 0]
        stWP5 = ST("wp5", 2)
        stWP5A = ST("wp5a", 2)
        stHT = ST("ht", 2)
        stHT2 = ST("ht2", 2)
        stAX = ST("ax", 4)
        stOUT = ST("out", 4)
        idx = 0
        last_q_pe = p4_pe
        for dq in range(4):
            SY.wait_ge(spe.h, last_q_pe)
            SE.wait_ge(spe.h, last_q_pe)
            if dq == 0:
                stP3T.wait_all(SE)
                TE.wait_ge(sac.h, sac.n)
                TE.wait_ge(sdv.h, invf_dv)
            for hb in range(64):
                dst = a_(A, hb * 512, 512)
                srcw = wp_hat[hb * 128:(hb + 1) * 128, dq * 512:(dq + 1) * 512]
                if hb % 2 == 0:
                    wp_s = dS(stWP5, dq, dst, srcw)
                else:
                    wp_2 = dA(stWP5A, dq, dst, srcw)
            stWP5.wait(TE, wp_s)
            stWP5A.wait(TE, wp_2)
            for ttp in range(8):
                bs = a_(B, (ttp % 2) * 16384, 16384)
                SY.wait_ge(spe.h, p5_pe_pair[ttp % 2])
                if dq == 0 and ttp < 2:
                    stH.wait_all(SY)
                SE.wait_ge(spe.h, p5_pe_pair[ttp % 2])
                for hb8 in range(8):
                    dstp = g3(bs, 256)[:, hb8 * 8:(hb8 + 1) * 8, :]
                    srcp = h_scr5[ttp][hb8 * 8:(hb8 + 1) * 8].rearrange("a p t -> p a t")
                    if hb8 % 2 == 0:
                        ld = dS(stHT, ttp, dstp, srcp)
                    else:
                        ld2 = dA(stHT2, ttp, dstp, srcp)
                stHT.wait(TE, ld)
                stHT2.wait(TE, ld2)
                for sub in range(2):
                    tt = ttp * 2 + sub
                    q4 = idx % 4
                    psum = PSW[:, q4 * 512:(q4 + 1) * 512]
                    if idx >= 4:
                        TE.wait_ge(sdv.h, p5_ev_list[idx - 4 + 4])
                    for hb in range(64):
                        i = TE.matmul(psum, bs[:, hb * 256 + sub * 128:
                                               hb * 256 + sub * 128 + 128],
                                      a_(A, hb * 512, 512),
                                      start=(hb == 0), stop=(hb == 63))
                    mm_pe = spe.inc(i)
                    p5_pe_pair[ttp % 2] = mm_pe
                    last_q_pe = mm_pe
                    if idx >= 4:
                        SE.wait_ge(sdv.h, p5_ev_list[idx - 4 + 4])
                    axld = dA(stAX, idx, a_(PF, q4 * 512, 512),
                              attnx_scr[tt * 128:(tt + 1) * 128,
                                        dq * 512:(dq + 1) * 512])
                    stAX.wait(VE, axld)
                    VE.wait_ge(spe.h, mm_pe)
                    if idx == 0:
                        VE.wait_ge(sdv.h, invf_dv)
                    if idx >= 4 and p5_st_list[idx - 4 + 4]:
                        stOUT.wait(VE, p5_st_list[idx - 4 + 4])
                    i = VE.tensor_scalar(a_(PF, 2048 + q4 * 512, 512), psum,
                                         a_(CF, CF_INVF + tt, 1), None, ALU.mult)
                    VE.wait_ge(sdv.h, sdv.inc(i))
                    i = VE.tensor_add(a_(PF, 2048 + q4 * 512, 512),
                                      a_(PF, 2048 + q4 * 512, 512),
                                      a_(PF, q4 * 512, 512))
                    p5_ev_list.append(sdv.inc(i))
                    SY.wait_ge(sdv.h, sdv.n)
                    i = SY.dma_start(out=out_t[tt * 128:(tt + 1) * 128,
                                               dq * 512:(dq + 1) * 512],
                                     in_=a_(PF, 2048 + q4 * 512, 512))
                    p5_st_list.append(stOUT.inc(i, idx))
                    idx += 1
        stOUT.wait_all(SY)

        stDBG = ST("dbg")
        for name in dump:
            if name.startswith("_") or name in taps:
                continue
            ap0, shp, dt = scratches[name]
            stOUT.wait_all(GP)
            d_ = dG(stDBG, 0, dbg[name], ap0)
        if dump:
            stDBG.wait_all(GP)
            ST("dbg2").wait_all(GP)

    return nc


def _host_consts():
    idx = np.arange(128)
    diff = idx[None, :] - idx[:, None]
    diffT = np.where(diff >= 0, diff, 1e5).astype(np.float32)
    cplus1 = np.broadcast_to((idx + 1)[None, :], (128, 128)).astype(np.float32).copy()
    c127 = 127 - (np.arange(512) % 128)
    c127rep = np.broadcast_to(c127[None, :], (128, 512)).astype(np.float32).copy()
    ident = np.eye(128, dtype=np.float32)
    onehot = np.zeros((128, 256), dtype=np.float32)
    for j in range(16):
        onehot[:, j * 16 + j] = 1.0
    return diffT, cplus1, c127rep, ident, onehot


def kernel(L0_attn_x, L0_attn_log_decay, L0_attn_Wq, L0_attn_Wk, L0_attn_Wv,
           L0_attn_Wo, L0_attn_Wg1, L0_attn_Wg2, L0_attn_norm_w,
           L0_ffn_w_gate_up, L0_ffn_w_proj, _dump=(), _nc=None):
    f32 = np.float32
    x = np.ascontiguousarray(np.asarray(L0_attn_x, dtype=f32))
    ld = np.asarray(L0_attn_log_decay, dtype=f32)
    Wq = np.ascontiguousarray(np.asarray(L0_attn_Wq, dtype=f32))
    Wk = np.ascontiguousarray(np.asarray(L0_attn_Wk, dtype=f32))
    Wv = np.ascontiguousarray(np.asarray(L0_attn_Wv, dtype=f32))
    Wo = np.ascontiguousarray(np.asarray(L0_attn_Wo, dtype=f32))
    Wg1 = np.ascontiguousarray(np.asarray(L0_attn_Wg1, dtype=f32))
    Wg2 = np.ascontiguousarray(np.asarray(L0_attn_Wg2, dtype=f32))
    nw = np.asarray(L0_attn_norm_w, dtype=f32)
    Wgu = np.ascontiguousarray(np.asarray(L0_ffn_w_gate_up, dtype=f32))
    Wp = np.ascontiguousarray(np.asarray(L0_ffn_w_proj, dtype=f32))

    diffT, cplus1, c127rep, ident, onehot = _host_consts()
    nw16 = np.ascontiguousarray(nw.reshape(16, 128).T)
    ldbc = np.ascontiguousarray(np.broadcast_to(ld[None, :], (128, 16)))

    nc = _nc if _nc is not None else build_program(dump=_dump)
    in_maps = []
    for c in range(NCORE):
        b, hf = c // 2, c % 2
        m = {
            "x": np.ascontiguousarray(x[b, hf * T:(hf + 1) * T, :]),
            "Wq": Wq, "Wk": Wk, "Wv": Wv, "Wo": Wo, "Wg1": Wg1, "Wg2": Wg2,
            "nw16": nw16, "ldbc": ldbc,
            "wgu_sh": np.ascontiguousarray(Wgu[:, c * 256:(c + 1) * 256]),
            "wp_sh": np.ascontiguousarray(Wp[:, c * 1024:(c + 1) * 1024]),
            "flag": np.full((128, 1), float(hf), dtype=f32),
            "diffT": diffT, "cplus1": cplus1, "c127rep": c127rep,
            "ident": ident, "onehot": onehot,
        }
        in_maps.append(m)
    res = run_bass_kernel_spmd(nc, in_maps, core_ids=list(range(NCORE)))
    out = np.empty((4, 4096, 2048), dtype=f32)
    for c in range(NCORE):
        b, hf = c // 2, c % 2
        out[b, hf * T:(hf + 1) * T, :] = res.results[c]["out"]
    if _dump:
        return out, res
    return out


# version 04
# speedup vs baseline: 1.2032x; 1.0068x over previous
import numpy as np
import ml_dtypes
import concourse.bass as bass
import concourse.mybir as mybir
from concourse.bass_utils import run_bass_kernel_spmd

F32 = mybir.dt.float32
BF16 = mybir.dt.bfloat16
AF = mybir.ActivationFunctionType
ALU = mybir.AluOpType
AX = mybir.AxisListType
bf16 = ml_dtypes.bfloat16

T = 2048
D = 2048
H = 16
HD = 128
C = 128
NCH = T // C
HFF = 8192
NCORE = 8
EPS_ATTN = 1e-5
EPS_LN = 1e-5
EPS_FFN = 1.1920929e-07


def build_program(dump=()):
    nc = bass.Bass("TRN2", target_bir_lowering=False)

    x_in = nc.dram_tensor("x", [T, D], F32, kind="ExternalInput").ap()
    wq_in = nc.dram_tensor("Wq", [D, D], F32, kind="ExternalInput").ap()
    wk_in = nc.dram_tensor("Wk", [D, D], F32, kind="ExternalInput").ap()
    wv_in = nc.dram_tensor("Wv", [D, D], F32, kind="ExternalInput").ap()
    wo_in = nc.dram_tensor("Wo", [D, D], F32, kind="ExternalInput").ap()
    wg1_in = nc.dram_tensor("Wg1", [D, HD], F32, kind="ExternalInput").ap()
    wg2_in = nc.dram_tensor("Wg2", [HD, D], F32, kind="ExternalInput").ap()
    nw16_in = nc.dram_tensor("nw16", [128, 16], F32, kind="ExternalInput").ap()
    ldbc_in = nc.dram_tensor("ldbc", [128, 16], F32, kind="ExternalInput").ap()
    wgu_sh_in = nc.dram_tensor("wgu_sh", [2 * HFF, D // NCORE], F32, kind="ExternalInput").ap()
    wp_sh_in = nc.dram_tensor("wp_sh", [D, HFF // NCORE], F32, kind="ExternalInput").ap()
    flag_in = nc.dram_tensor("flag", [128, 1], F32, kind="ExternalInput").ap()
    diffT_in = nc.dram_tensor("diffT", [128, 128], F32, kind="ExternalInput").ap()
    cplus1_in = nc.dram_tensor("cplus1", [128, 128], F32, kind="ExternalInput").ap()
    c127rep_in = nc.dram_tensor("c127rep", [128, 512], F32, kind="ExternalInput").ap()
    ident_in = nc.dram_tensor("ident", [128, 128], F32, kind="ExternalInput").ap()
    onehot_in = nc.dram_tensor("onehot", [128, 256], F32, kind="ExternalInput").ap()
    out_t = nc.dram_tensor("out", [T, D], F32, kind="ExternalOutput").ap()

    xn_scr = nc.dram_tensor("xn_scr", [T, D], BF16, kind="Internal").ap()
    q_scr = nc.dram_tensor("q_scr", [D, T], BF16, kind="Internal").ap()
    k_scr = nc.dram_tensor("k_scr", [D, T], BF16, kind="Internal").ap()
    ktd_scr = nc.dram_tensor("ktd_scr", [D, T], BF16, kind="Internal").ap()
    v_scr = nc.dram_tensor("v_scr", [T, D], BF16, kind="Internal").ap()
    sg_scr = nc.dram_tensor("sg_scr", [T, D], BF16, kind="Internal").ap()
    h_scr = nc.dram_tensor("h_scr", [8 * 64 * 128, 256], BF16, kind="Internal").ap()
    h_scr5 = h_scr.rearrange("(tp hb p) t -> tp hb p t", hb=64, p=128)
    attnx_scr = nc.dram_tensor("attnx_scr", [T, D], F32, kind="Internal").ap()
    wgu_my = nc.dram_tensor("wgu_my", [D // NCORE, 2 * HFF], BF16, kind="Internal").ap()
    wp_my = nc.dram_tensor("wp_my", [HFF // NCORE, D], BF16, kind="Internal").ap()
    wgu_hat = nc.dram_tensor("wgu_hat", [D, 2 * HFF], BF16, kind="Internal").ap()
    wp_hat = nc.dram_tensor("wp_hat", [HFF, D], BF16, kind="Internal").ap()
    xnt_scr = nc.dram_tensor("xnt_scr", [128, 32768], BF16, kind="Internal").ap()
    s_end = nc.dram_tensor("s_end", [128, H * HD], BF16, kind="Internal").ap()
    s_gath = nc.dram_tensor("s_gath", [256, H * HD], BF16, kind="Internal").ap()
    scratches = {"xn_scr": (xn_scr, [T, D], BF16), "q_scr": (q_scr, [D, T], BF16),
                 "k_scr": (k_scr, [D, T], BF16), "ktd_scr": (ktd_scr, [D, T], BF16),
                 "v_scr": (v_scr, [T, D], BF16), "sg_scr": (sg_scr, [T, D], BF16),
                 "attnx_scr": (attnx_scr, [T, D], F32),
                 "wgu_hat": (wgu_hat, [D, 2 * HFF], BF16),
                 "wp_hat": (wp_hat, [HFF, D], BF16),
                 "s_gath": (s_gath, [256, H * HD], BF16),
                 "s_end": (s_end, [128, H * HD], BF16),
                 "h_scr": (h_scr, [8 * 64 * 128, 256], BF16)}
    taps = {"onormt": ([128, 32768], BF16), "wA0": ([128, 16384], BF16),
            "xnt2": ([128, 32768], BF16), "gu0": ([128, 2048], F32),
            "p2c1": ([128, 4096], F32), "p2o1": ([128, 4096], F32),
            "ktd15": ([128, 4096], F32)}
    dbg = {}
    for name in dump:
        if name.startswith("_"):
            continue
        if name in taps:
            shp, dt = taps[name]
        else:
            ap0, shp, dt = scratches[name]
        dbg[name] = nc.dram_tensor("dbg_" + name, shp, dt, kind="ExternalOutput").ap()

    A = nc.alloc_sbuf_tensor("A", [128, 32768], BF16)
    B = nc.alloc_sbuf_tensor("B", [128, 32768], BF16)
    PBF = nc.alloc_sbuf_tensor("PBF", [128, 12288], BF16)
    PF = nc.alloc_sbuf_tensor("PF", [128, 4096], F32)
    CA = nc.alloc_sbuf_tensor("CA", [128, 8192], BF16)
    CF = nc.alloc_sbuf_tensor("CF", [128, 1664], F32)
    QB = nc.alloc_sbuf_tensor("QB", [128, 7168], BF16)

    CF_DIFFT, CF_CP1, CF_C127, CF_LDBC, CF_NW16, CF_FLAG = 0, 128, 256, 768, 784, 800
    CF_KD, CF_CD, CF_IDENT, CF_OH, CF_STATS, CF_INVF, CF_SM = 801, 817, 833, 961, 1217, 1233, 1280
    CA_DT, CA_QD, CA_CDI, CA_I128, CA_OH = 0, 2048, 4096, 6144, 6272

    PSW = nc.alloc_psum_tensor("PSW", [128, 2048], F32)
    PS0 = nc.alloc_psum_tensor("PS0", [128, 512], F32)
    PS1 = nc.alloc_psum_tensor("PS1", [128, 512], F32)
    PS2 = nc.alloc_psum_tensor("PS2", [128, 512], F32)
    PSTR = nc.alloc_psum_tensor("PSTR", [128, 1024], BF16)

    class Sem:
        def __init__(self, name):
            self.h = nc.alloc_semaphore(name)
            self.n = 0

        def inc(self, inst, k=1):
            inst.then_inc(self.h, k)
            self.n += k
            return self.n

    scc = Sem("scc")
    spe = Sem("spe")
    sdv = Sem("sdv")
    sac = Sem("sac")

    class DStream:

        def __init__(self, name, k):
            self.sems = [nc.alloc_semaphore(f"{name}{j}") for j in range(k)]
            self.cnt = [0] * k
            self.k = k

        def inc(self, inst, slot=0):
            j = slot % self.k
            inst.then_inc(self.sems[j], 16)
            self.cnt[j] += 16
            return (j, self.cnt[j])

        def tok(self, slot=0):
            j = slot % self.k
            return (j, self.cnt[j])

        def wait(self, eng, token):
            j, v = token
            eng.wait_ge(self.sems[j], v)

        def wait_all(self, eng):
            for j in range(self.k):
                eng.wait_ge(self.sems[j], self.cnt[j])

    def a_(t, off, w):
        return t[:, off:off + w]

    def g3(ap, k):
        return ap.rearrange("p (g k) -> p g k", k=k)

    STREAMS = {}

    with nc.Block() as blk:
        TE, VE, SE, SY, GP = nc.tensor, nc.vector, nc.scalar, nc.sync, nc.gpsimd

        def ST(name, k=1):
            if name not in STREAMS:
                STREAMS[name] = DStream(name, k)
            return STREAMS[name]

        def dS(st, slot, dst, src):
            return st.inc(SY.dma_start(out=dst, in_=src), slot)

        def dTS(st, slot, dst, src):
            return st.inc(SY.dma_start_transpose(dst, src), slot)

        def dA(st, slot, dst, src):
            return st.inc(SE.dma_start(out=dst, in_=src), slot)

        def dG(st, slot, dst, src):
            return st.inc(GP.dma_start(out=dst, in_=src), slot)

        stC = ST("c")
        dS(stC, 0, a_(CF, CF_DIFFT, 128), diffT_in)
        dS(stC, 0, a_(CF, CF_CP1, 128), cplus1_in)
        dS(stC, 0, a_(CF, CF_C127, 512), c127rep_in)
        dS(stC, 0, a_(CF, CF_LDBC, 16), ldbc_in)
        dS(stC, 0, a_(CF, CF_NW16, 16), nw16_in)
        dS(stC, 0, a_(CF, CF_FLAG, 1), flag_in)
        dS(stC, 0, a_(CF, CF_IDENT, 128), ident_in)
        dS(stC, 0, a_(CF, CF_OH, 256), onehot_in)

        stC.wait_all(SE)
        for h in range(H):
            SE.activation(a_(CA, CA_DT + h * 128, 128), a_(CF, CF_DIFFT, 128), AF.Exp,
                          scale=a_(CF, CF_LDBC + h, 1))
            SE.activation(a_(CA, CA_QD + h * 128, 128), a_(CF, CF_CP1, 128), AF.Exp,
                          scale=a_(CF, CF_LDBC + h, 1))
            i = SE.activation(a_(CF, CF_KD + h, 1), a_(CF, CF_C127, 1), AF.Exp,
                              scale=a_(CF, CF_LDBC + h, 1))
        i = SE.activation(a_(CF, CF_CD, 16), a_(CF, CF_LDBC, 16), AF.Exp, scale=float(C))
        consts_act = sac.inc(i)
        stC.wait_all(VE)
        VE.tensor_copy(a_(CA, CA_I128, 128), a_(CF, CF_IDENT, 128))
        i = VE.tensor_copy(a_(CA, CA_OH, 256), a_(CF, CF_OH, 256))
        VE.wait_ge(sdv.h, sdv.inc(i))
        VE.wait_ge(sac.h, consts_act)
        for h in range(H):
            i = VE.tensor_scalar(a_(CA, CA_CDI + h * 128, 128), a_(CA, CA_I128, 128),
                                 a_(CF, CF_CD + h, 1), None, ALU.mult)
        consts_cdi = sdv.inc(i)

        q_tiles = []
        for r in range(32):
            q_tiles.append(("gu", r))
        for r in range(16):
            q_tiles.append(("wp", r))
        if "_noq" in dump:
            q_tiles = []
        qp1 = a_(QB, 2048, 1024); qp2 = a_(QB, 3072, 1024)
        qtq = a_(QB, 4096, 1024); qwh = a_(QB, 5120, 1024)
        redq = a_(CF, CF_SM + 96, 16)
        sbfq = a_(CA, 6528, 16); hsqq = a_(CA, 6560, 16); nhsq = a_(CA, 6592, 16)
        qstate = {"dv": {}, "tr": 0, "ev": 0, "stg": [None, None], "next": 0,
                  "seeded": False}
        stQL = ST("ql", 2)
        stQS = ST("qs", 2)

        def emit_q_tile():
            it = qstate["next"]
            if it >= len(q_tiles):
                return False
            qstate["next"] += 1
            kind, r = q_tiles[it]
            wbf = a_(QB, (it % 2) * 1024, 1024)
            if it >= 2:
                GP.wait_ge(sdv.h, qstate["dv"][it - 2])
            if kind == "gu":
                qld = dG(stQL, it, g3(wbf, 256),
                         wgu_sh_in[r * 512:(r + 1) * 512, :]
                         .rearrange("(a p) d -> p a d", p=128))
            else:
                qld = dG(stQL, it, wbf, wp_sh_in[r * 128:(r + 1) * 128, :])
            stQL.wait(VE, qld)
            i = VE.tensor_reduce(redq, g3(wbf, 64), axis=AX.X, op=ALU.add,
                                 apply_absolute_value=True)
            VE.wait_ge(sdv.h, sdv.inc(i))
            i = VE.tensor_scalar(sbfq, redq, 1.0 / 64.0, 1e-8, ALU.mult, ALU.max)
            VE.wait_ge(sdv.h, sdv.inc(i))
            VE.tensor_scalar(hsqq, sbfq, 0.5, None, ALU.mult)
            i = VE.tensor_scalar(nhsq, sbfq, -0.5, None, ALU.mult)
            VE.wait_ge(sdv.h, sdv.inc(i))
            a0, a1 = bass.broadcast_tensor_aps(g3(wbf, 64), g3(hsqq, 1))
            VE.tensor_tensor(g3(qp1, 64), a0, a1, op=ALU.is_gt)
            b0, b1 = bass.broadcast_tensor_aps(g3(wbf, 64), g3(nhsq, 1))
            i = VE.tensor_tensor(g3(qp2, 64), b0, b1, op=ALU.is_lt)
            qstate["dv"][it] = sdv.inc(i)
            VE.wait_ge(sdv.h, qstate["dv"][it])
            i = VE.tensor_sub(qtq, qp1, qp2)
            VE.wait_ge(sdv.h, sdv.inc(i))
            VE.wait_ge(spe.h, qstate["tr"])
            c0, c1 = bass.broadcast_tensor_aps(g3(qtq, 64), g3(sbfq, 1))
            i = VE.tensor_tensor(g3(qwh, 64), c0, c1, op=ALU.mult)
            wh_dv = sdv.inc(i)
            TE.wait_ge(sdv.h, wh_dv)
            if not qstate["seeded"]:
                qstate["ev"] = xnT_dv
                qstate["seeded"] = True
            if kind == "gu":
                for dhalf in range(2):
                    par = (2 * it + dhalf) % 2
                    stg = a_(QB, 6144 + par * 512, 512)
                    TE.wait_ge(sdv.h, qstate["ev"])
                    for a4 in range(4):
                        i = TE.transpose(PSTR[:, a4 * 128:(a4 + 1) * 128],
                                         qwh[:, a4 * 256 + dhalf * 128:
                                             a4 * 256 + dhalf * 128 + 128],
                                         a_(CA, CA_I128, 128))
                    qstate["tr"] = spe.inc(i)
                    VE.wait_ge(spe.h, qstate["tr"])
                    if qstate["stg"][par]:
                        stQS.wait(VE, qstate["stg"][par])
                    i = VE.tensor_copy(stg, PSTR[:, 0:512])
                    qstate["ev"] = sdv.inc(i)
                    SY.wait_ge(sdv.h, qstate["ev"])
                    qstate["stg"][par] = dS(
                        stQS, par,
                        wgu_my[dhalf * 128:(dhalf + 1) * 128, r * 512:(r + 1) * 512],
                        stg)
            else:
                for half in range(2):
                    par = (2 * it + half) % 2
                    stg = a_(QB, 6144 + par * 512, 512)
                    TE.wait_ge(sdv.h, qstate["ev"])
                    for a4 in range(4):
                        i = TE.transpose(PSTR[:, a4 * 128:(a4 + 1) * 128],
                                         qwh[:, (half * 4 + a4) * 128:
                                             (half * 4 + a4) * 128 + 128],
                                         a_(CA, CA_I128, 128))
                    qstate["tr"] = spe.inc(i)
                    VE.wait_ge(spe.h, qstate["tr"])
                    if qstate["stg"][par]:
                        stQS.wait(VE, qstate["stg"][par])
                    i = VE.tensor_copy(stg, PSTR[:, 0:512])
                    qstate["ev"] = sdv.inc(i)
                    SY.wait_ge(sdv.h, qstate["ev"])
                    dstw = wp_my.rearrange("(a p) d -> p a d", p=128)[
                        :, half * 4:(half + 1) * 4, r * 128:(r + 1) * 128]
                    qstate["stg"][par] = dS(stQS, par, dstw, g3(stg, 128))
            return True

        def emit_q_gathers():
            stQS.wait_all(GP)
            stQL.wait_all(GP)
            if "_nocc" in dump:
                for rr in range(NCORE):
                    i = GP.dma_start(out=wgu_hat[rr * (D // NCORE):(rr + 1) * (D // NCORE), :],
                                     in_=wgu_my)
                    i.then_inc(scc.h, 16); scc.n += 16
                    i = GP.dma_start(out=wp_hat[rr * (HFF // NCORE):(rr + 1) * (HFF // NCORE), :],
                                     in_=wp_my)
                    i.then_inc(scc.h, 16); scc.n += 16
                GP.wait_ge(scc.h, scc.n)
            else:
                GP.collective_compute("AllGather", ALU.bypass,
                                      replica_groups=[list(range(NCORE))],
                                      ins=[wgu_my.opt()], outs=[wgu_hat.opt()]).then_inc(scc.h, 1)
                scc.n += 1
                GP.collective_compute("AllGather", ALU.bypass,
                                      replica_groups=[list(range(NCORE))],
                                      ins=[wp_my.opt()], outs=[wp_hat.opt()]).then_inc(scc.h, 1)
                scc.n += 1


        p0_act = [0] * 16
        p0_dv = [0] * 16
        p0_tr = [0] * 16
        xnT_evac = 0
        for tt in range(16):
            if tt >= 2:
                SY.wait_ge(sac.h, p0_act[tt - 2])
                SY.wait_ge(sdv.h, p0_dv[tt - 2])
            stX = ST("x", 2)
            xl = dS(stX, tt, a_(PF, (tt % 2) * 2048, 2048), x_in[tt * 128:(tt + 1) * 128, :])
            xt = a_(PF, (tt % 2) * 2048, 2048)
            sums = a_(CF, CF_SM, 1); sums2 = a_(CF, CF_SM + 1, 1)
            m_ = a_(CF, CF_SM + 2, 1); var_ = a_(CF, CF_SM + 3, 1)
            sd_ = a_(CF, CF_SM + 4, 1); rstd_ = a_(CF, CF_SM + 5, 1)
            msq_ = a_(CF, CF_SM + 6, 1)
            stX.wait(SE, xl)
            if tt >= 1:
                SE.wait_ge(sdv.h, p0_dv[tt - 1])
            i = SE.activation(PSW[:], xt, AF.Copy, accum_out=sums)
            SE.wait_ge(sac.h, sac.inc(i))
            i = SE.activation(PSW[:], xt, AF.Square, accum_out=sums2)
            p0_act[tt] = sac.inc(i)
            VE.wait_ge(sac.h, p0_act[tt])
            VE.tensor_scalar(m_, sums, 1.0 / D, None, ALU.mult)
            i = VE.tensor_scalar(var_, sums2, 1.0 / D, None, ALU.mult)
            VE.wait_ge(sdv.h, sdv.inc(i))
            i = VE.tensor_tensor(msq_, m_, m_, op=ALU.mult)
            VE.wait_ge(sdv.h, sdv.inc(i))
            i = VE.tensor_tensor(var_, var_, msq_, op=ALU.subtract)
            VE.wait_ge(sdv.h, sdv.inc(i))
            i = VE.tensor_scalar(var_, var_, 1.0, EPS_LN, ALU.mult, ALU.add)
            SE.wait_ge(sdv.h, sdv.inc(i))
            i = SE.activation(sd_, var_, AF.Sqrt)
            VE.wait_ge(sac.h, sac.inc(i))
            i = VE.reciprocal(rstd_, sd_)
            VE.wait_ge(sdv.h, sdv.inc(i))
            if tt >= 2:
                VE.wait_ge(spe.h, p0_tr[tt - 2])
            i = VE.tensor_scalar(a_(PBF, (tt % 2) * 2048, 2048), xt, m_, rstd_,
                                 ALU.subtract, ALU.mult)
            p0_dv[tt] = sdv.inc(i)
            TE.wait_ge(sdv.h, p0_dv[tt])
            for rnd in range(2):
                TE.wait_ge(sdv.h, xnT_evac)
                for a8 in range(8):
                    dt = rnd * 8 + a8
                    i = TE.transpose(PSTR[:, a8 * 128:(a8 + 1) * 128],
                                     a_(PBF, (tt % 2) * 2048 + dt * 128, 128),
                                     a_(CA, CA_I128, 128))
                p0_tr[tt] = spe.inc(i)
                VE.wait_ge(spe.h, p0_tr[tt])
                dstB = B[:].rearrange("p (dt t) -> p dt t", t=2048)[
                    :, rnd * 8:(rnd + 1) * 8, tt * 128:(tt + 1) * 128]
                i = VE.tensor_copy(dstB, g3(PSTR[:], 128))
                xnT_evac = sdv.inc(i)
        xnT_dv = xnT_evac
        stXNT = ST("xnt")
        SY.wait_ge(sdv.h, xnT_dv)
        xnt_st = dS(stXNT, 0, xnt_scr, B[:])

        stWG1 = ST("wg1")
        for dt in range(16):
            wg1_ld = dG(stWG1, 0, a_(PBF, 10240 + dt * 128, 128),
                        wg1_in[dt * 128:(dt + 1) * 128, :])
        stWG1.wait(TE, wg1_ld)
        TE.wait_ge(sdv.h, xnT_dv)
        g1_ev = [0] * 4
        for tc in range(4):
            ps = (PS0, PS1)[tc % 2]
            if tc >= 2:
                TE.wait_ge(sdv.h, g1_ev[tc - 2])
            for dt in range(16):
                i = TE.matmul(ps[:], a_(PBF, 10240 + dt * 128, 128),
                              a_(B, dt * 2048 + tc * 512, 512),
                              start=(dt == 0), stop=(dt == 15))
            spe.inc(i)
            VE.wait_ge(spe.h, spe.n)
            i = VE.tensor_copy(a_(PBF, tc * 512, 512), ps[:])
            g1_ev[tc] = sdv.inc(i)
        g1_done = sdv.n
        g1_pe = spe.n

        stWH = ST("wh2", 2)
        whalf_idx = [0]

        def load_w_half(w_in, half, base, wait_pe):
            GP.wait_ge(spe.h, wait_pe)
            slot = whalf_idx[0]
            whalf_idx[0] += 1
            for dt in range(16):
                r = dG(stWH, slot, a_(A, base + dt * 1024, 1024),
                       w_in[dt * 128:(dt + 1) * 128, half * 1024:(half + 1) * 1024])
            return r

        state = {"qrow_free": None, "krow_free": None, "ktdrow_free": None, "kdrep_rd": 0}

        def proj_lhsT(w_in, dst_scr, decay_scr, wait_pe0):
            h0 = load_w_half(w_in, 0, 0, wait_pe0)
            h1 = load_w_half(w_in, 1, 16384, wait_pe0)
            half_ld = (h0, h1)
            for et in range(16):
                emit_q_tile()
                half = et // 8
                hd_ = et % 8
                base = 16384 * half
                stWH.wait(TE, half_ld[half])
                row = a_(PBF, 2048, 2048) if decay_scr is None else a_(PBF, 4096, 2048)
                stR = ST("rq") if decay_scr is None else ST("rk")
                stRT = ST("rkt")
                row_free = state["qrow_free"] if decay_scr is None else state["krow_free"]
                for tc in range(4):
                    ps = (PS0, PS1)[tc % 2]
                    TE.wait_ge(sac.h, sac.n)
                    for dt in range(16):
                        i = TE.matmul(ps[:], a_(A, base + dt * 1024 + hd_ * 128, 128),
                                      a_(B, dt * 2048 + tc * 512, 512),
                                      start=(dt == 0), stop=(dt == 15))
                    spe.inc(i)
                    SE.wait_ge(spe.h, spe.n)
                    if tc == 0 and row_free:
                        stR.wait(SE, row_free)
                    i = SE.activation(row[:, tc * 512:(tc + 1) * 512], ps[:], AF.Silu)
                    sac.inc(i)
                act_done = sac.n
                if decay_scr is not None:
                    SE.wait_ge(sdv.h, state["kdrep_rd"])
                    i = SE.activation(a_(PBF, 10240, 512), a_(CF, CF_C127, 512), AF.Exp,
                                      scale=a_(CF, CF_LDBC + et, 1))
                    sac.inc(i)
                    VE.wait_ge(sac.h, sac.n)
                    if state["ktdrow_free"]:
                        stRT.wait(VE, state["ktdrow_free"])
                    r0, r1 = bass.broadcast_tensor_aps(
                        row.rearrange("p (r k) -> p r k", k=512),
                        a_(PBF, 10240, 512).rearrange("p (r k) -> p r k", r=1))
                    i = VE.tensor_tensor(
                        a_(PBF, 6144, 2048).rearrange("p (r k) -> p r k", k=512),
                        r0, r1, op=ALU.mult)
                    state["kdrep_rd"] = sdv.inc(i)
                    if "ktd15" in dump and et == 15:
                        VE.wait_ge(sdv.h, state["kdrep_rd"])
                        VE.tensor_copy(a_(PF, 0, 2048), a_(PBF, 6144, 2048))
                        VE.tensor_copy(a_(PF, 2048, 512), a_(PBF, 10240, 512))
                        i = VE.tensor_copy(a_(PF, 2560, 1024), a_(PBF, 4096, 1024))
                        k15dv = sdv.inc(i)
                        SY.wait_ge(sdv.h, k15dv)
                        dS(ST("dbg2"), 0, dbg["ktd15"], PF[:])
                    SY.wait_ge(sdv.h, state["kdrep_rd"])
                    state["ktdrow_free"] = dS(stRT, 0, decay_scr[et * 128:(et + 1) * 128, :],
                                              a_(PBF, 6144, 2048))
                SY.wait_ge(sac.h, act_done)
                st = dS(stR, 0, dst_scr[et * 128:(et + 1) * 128, :], row)
                if decay_scr is None:
                    state["qrow_free"] = st
                else:
                    state["krow_free"] = st

        TE.wait_ge(sdv.h, g1_done)
        proj_lhsT(wq_in, q_scr, None, 0)
        proj_lhsT(wk_in, k_scr, ktd_scr, spe.n)

        vh0 = load_w_half(wv_in, 0, 0, spe.n)
        vh1 = load_w_half(wv_in, 1, 16384, spe.n)
        vrow_free = {}
        stRV = ST("rv", 2)
        for tt in range(16):
            for half in range(2):
                emit_q_tile()
                emit_q_tile()
                base = 16384 * half
                stWH.wait(TE, (vh0, vh1)[half])
                vr = a_(PBF, 8192 + ((tt * 2 + half) % 2) * 1024, 1024)
                for ec in range(2):
                    ps = (PS0, PS1)[ec]
                    TE.wait_ge(sac.h, sac.n)
                    for dt in range(16):
                        i = TE.matmul(ps[:], a_(B, dt * 2048 + tt * 128, 128),
                                      a_(A, base + dt * 1024 + ec * 512, 512),
                                      start=(dt == 0), stop=(dt == 15))
                    spe.inc(i)
                    SE.wait_ge(spe.h, spe.n)
                    if ec == 0 and vrow_free.get((tt * 2 + half) % 2):
                        stRV.wait(SE, vrow_free[(tt * 2 + half) % 2])
                    i = SE.activation(vr[:, ec * 512:(ec + 1) * 512], ps[:], AF.Silu)
                    sac.inc(i)
                SY.wait_ge(sac.h, sac.n)
                vrow_free[(tt * 2 + half) % 2] = dS(
                    stRV, (tt * 2 + half),
                    v_scr[tt * 128:(tt + 1) * 128, half * 1024:(half + 1) * 1024], vr)
        p1_pe = spe.n

        while emit_q_tile():
            pass
        emit_q_gathers()

        VE.wait_ge(sdv.h, consts_cdi)
        for s_ in ("rq", "rk", "rkt", "rv"):
            ST(s_).wait_all(VE)
        i = VE.memset(a_(QB, 0, 2048), 0.0)
        a_sprev = sdv.inc(i)
        stACH = ST("ach", 2)
        for s_ in ("rq", "rk", "rkt", "rv"):
            ST(s_).wait_all(SY)

        def a_slots(ch):
            return (a_(PBF, 2048 + (ch % 2) * 4096, 2048),
                    a_(PBF, 4096 + (ch % 2) * 4096, 2048))

        def a_loads(ch):
            kt_, v_ = a_slots(ch)
            dTS(stACH, ch, kt_, ktd_scr[:, ch * 128:(ch + 1) * 128])
            return dS(stACH, ch, v_, v_scr[ch * 128:(ch + 1) * 128, :])

        a_ld = {0: a_loads(0), 1: a_loads(1)}
        a_mm = {}
        for ch in range(NCH):
            scur = a_(QB, (ch % 2) * 2048, 2048)
            snxt = a_(QB, ((ch + 1) % 2) * 2048, 2048)
            ktb, vb = a_slots(ch)
            stACH.wait(TE, a_ld[ch])
            TE.wait_ge(sdv.h, a_sprev)
            if ch == 0:
                TE.wait_ge(sac.h, sac.n)
            for q in range(4):
                ps = (PS0, PS1, PS2, PSW[:, 0:512])[q]
                for hh in range(4 * q, 4 * q + 4):
                    TE.matmul(ps[:, (hh % 4) * 128:(hh % 4 + 1) * 128],
                              ktb[:, hh * 128:(hh + 1) * 128],
                              vb[:, hh * 128:(hh + 1) * 128],
                              start=(hh % 4 == 0), stop=False, skip_group_check=True)
                for hh in range(4 * q, 4 * q + 4):
                    i = TE.matmul(ps[:, (hh % 4) * 128:(hh % 4 + 1) * 128],
                                  a_(CA, CA_CDI + hh * 128, 128),
                                  scur[:, hh * 128:(hh + 1) * 128],
                                  start=False, stop=(hh % 4 == 3), skip_group_check=True)
                spe.inc(i)
                VE.wait_ge(spe.h, spe.n)
                i = VE.tensor_copy(snxt[:, q * 512:(q + 1) * 512], ps[:])
                sdv.inc(i)
            a_sprev = sdv.n
            a_mm[ch] = spe.n
            if ch + 2 < NCH:
                SY.wait_ge(spe.h, a_mm[ch])
                a_ld[ch + 2] = a_loads(ch + 2)
        s_final = a_(QB, (NCH % 2) * 2048, 2048)
        stSE = ST("se")
        SY.wait_ge(sdv.h, a_sprev)
        se_st = dS(stSE, 0, s_end, s_final)
        stSE.wait(GP, se_st)
        if "_nocc" in dump:
            i = GP.dma_start(out=s_gath[0:128, :], in_=s_end)
            i.then_inc(scc.h, 16); scc.n += 16
        else:
            GP.collective_compute("AllGather", ALU.bypass,
                                  replica_groups=[[0, 1], [2, 3], [4, 5], [6, 7]],
                                  ins=[s_end.opt()], outs=[s_gath.opt()]).then_inc(scc.h, 1)
            scc.n += 1
        cc_s = scc.n

        stWG2 = ST("wg2")
        wg2_ld = dG(stWG2, 0, a_(PBF, 2048, 2048), wg2_in)
        stSG = ST("sgst", 2)
        sg_st = [None, None]
        g_idx = 0
        for tt in range(16):
            for ec in range(4):
                ps = (PS0, PS1)[g_idx % 2]
                if g_idx == 0:
                    stWG2.wait(TE, wg2_ld)
                    TE.wait_ge(sdv.h, sdv.n)
                TE.wait_ge(sac.h, sac.n)
                i = TE.matmul(ps[:], a_(PBF, tt * 128, 128),
                              a_(PBF, 2048 + ec * 512, 512), start=True, stop=True)
                spe.inc(i)
                SE.wait_ge(spe.h, spe.n)
                if sg_st[g_idx % 2]:
                    stSG.wait(SE, sg_st[g_idx % 2])
                i = SE.activation(a_(PBF, 4096 + (g_idx % 2) * 512, 512), ps[:], AF.Sigmoid)
                sac.inc(i)
                SY.wait_ge(sac.h, sac.n)
                sg_st[g_idx % 2] = dS(
                    stSG, g_idx,
                    sg_scr[tt * 128:(tt + 1) * 128, ec * 512:(ec + 1) * 512],
                    a_(PBF, 4096 + (g_idx % 2) * 512, 512))
                g_idx += 1
        g_pe = spe.n

        def bslot(stream, ch):
            offs = {"q": 0, "k": 4096, "kt": 8192, "v": 12288, "sg": 16384}
            return a_(B, offs[stream] + (ch % 2) * 2048, 2048)

        stSI = ST("si")
        SY.wait_ge(scc.h, scc.n)
        stXNT.wait(SY, xnt_st)
        si_ld = dS(stSI, 0, a_(B, 20480, 2048), s_gath[0:128, :])
        stSI.wait(VE, si_ld)
        i = VE.tensor_scalar(a_(QB, 0, 2048), a_(B, 20480, 2048),
                             a_(CF, CF_FLAG, 1), None, ALU.mult)
        s_ready = sdv.inc(i)

        stC2S = ST("c2s", 2)
        stC2A = ST("c2a", 2)

        def p2_loads(ch):
            dS(stC2S, ch, g3(bslot("q", ch), 128),
               q_scr[:, ch * 128:(ch + 1) * 128].rearrange("(a p) t -> p a t", p=128))
            dS(stC2S, ch, g3(bslot("k", ch), 128),
               k_scr[:, ch * 128:(ch + 1) * 128].rearrange("(a p) t -> p a t", p=128))
            l1 = dTS(stC2S, ch, bslot("kt", ch), ktd_scr[:, ch * 128:(ch + 1) * 128])
            dA(stC2A, ch, bslot("v", ch), v_scr[ch * 128:(ch + 1) * 128, :])
            l2 = dA(stC2A, ch, bslot("sg", ch), sg_scr[ch * 128:(ch + 1) * 128, :])
            return (l1, l2)

        ST("rq").wait_all(SY)
        ST("rk").wait_all(SY)
        ST("sgst", 2).wait_all(SY)
        ST("rq").wait_all(SE)
        ST("rv", 2).wait_all(SE)
        ST("sgst", 2).wait_all(SE)
        stXNT.wait(SE, xnt_st)
        p2_ld = {0: p2_loads(0), 1: p2_loads(1)}
        sct_ev = {}
        o_pe_prev = g_pe
        su_pe_prev = g_pe
        odot_prev = 0
        tr_pe_prev = 0
        p2_sdone = s_ready
        for ch in range(NCH):
            scur = a_(QB, (ch % 2) * 2048, 2048)
            snxt = a_(QB, ((ch + 1) % 2) * 2048, 2048)
            stC2S.wait(TE, p2_ld[ch][0])
            stC2A.wait(TE, p2_ld[ch][1])
            TE.wait_ge(sdv.h, p2_sdone)
            if ch == 0:
                TE.wait_ge(sac.h, sac.n)
            VE.wait_ge(spe.h, o_pe_prev)
            VE.wait_ge(sac.h, sac.n)
            for hq in range(4):
                ps = (PS0, PS1)[hq % 2]
                TE.wait_ge(sdv.h, sct_ev.get((ch, hq - 2), sct_ev.get((ch - 1, hq + 2), 0)))
                for hh in range(4 * hq, 4 * hq + 4):
                    i = TE.matmul(ps[:, (hh % 4) * 128:(hh % 4 + 1) * 128],
                                  bslot("k", ch)[:, hh * 128:(hh + 1) * 128],
                                  bslot("q", ch)[:, hh * 128:(hh + 1) * 128],
                                  start=(hh % 4 == 0), stop=(hh % 4 == 3),
                                  skip_group_check=True)
                spe.inc(i)
                VE.wait_ge(spe.h, spe.n)
                i = VE.tensor_tensor(a_(PBF, hq * 512, 512), ps[:],
                                     a_(CA, CA_DT + hq * 512, 512), op=ALU.mult)
                sct_ev[(ch, hq)] = sdv.inc(i)
            i = VE.tensor_mul(a_(PBF, 2048, 2048), bslot("q", ch), a_(CA, CA_QD, 2048))
            qtl_dv = sdv.inc(i)
            TE.wait_ge(sdv.h, qtl_dv)
            TE.wait_ge(sdv.h, odot_prev)
            for hh in range(H):
                TE.wait_ge(sdv.h, sct_ev[(ch, hh // 4)]) if hh % 4 == 0 else None
                TE.matmul(PSW[:, hh * 128:(hh + 1) * 128],
                          a_(PBF, hh * 128, 128),
                          bslot("v", ch)[:, hh * 128:(hh + 1) * 128],
                          start=(hh % 4 == 0), stop=False, skip_group_check=True)
                i = TE.matmul(PSW[:, hh * 128:(hh + 1) * 128],
                              a_(PBF, 2048 + hh * 128, 128),
                              scur[:, hh * 128:(hh + 1) * 128],
                              start=False, stop=(hh % 4 == 3), skip_group_check=True)
            o_pe = spe.inc(i)
            su_ev = 0
            for q in range(4):
                if q > 0:
                    TE.wait_ge(sdv.h, su_ev)
                for hh in range(4 * q, 4 * q + 4):
                    TE.matmul(PS2[:, (hh % 4) * 128:(hh % 4 + 1) * 128],
                              bslot("kt", ch)[:, hh * 128:(hh + 1) * 128],
                              bslot("v", ch)[:, hh * 128:(hh + 1) * 128],
                              start=(hh % 4 == 0), stop=False, skip_group_check=True)
                for hh in range(4 * q, 4 * q + 4):
                    i = TE.matmul(PS2[:, (hh % 4) * 128:(hh % 4 + 1) * 128],
                                  a_(CA, CA_CDI + hh * 128, 128),
                                  scur[:, hh * 128:(hh + 1) * 128],
                                  start=False, stop=(hh % 4 == 3), skip_group_check=True)
                spe.inc(i)
                VE.wait_ge(spe.h, spe.n)
                i = VE.tensor_copy(snxt[:, q * 512:(q + 1) * 512], PS2[:])
                su_ev = sdv.inc(i)
            p2_sdone = su_ev
            su_pe = spe.n
            VE.wait_ge(spe.h, o_pe)
            if "p2o1" in dump and ch == 1:
                VE.tensor_copy(a_(PF, 0, 2048), PSW[:])
                VE.tensor_copy(a_(PF, 2048, 512), scur[:, 1536:2048])
                VE.tensor_copy(a_(PF, 2560, 128), a_(PBF, 2048 + 15 * 128, 128))
                VE.tensor_copy(a_(PF, 2688, 128), a_(PBF, 15 * 128, 128))
                i = VE.tensor_copy(a_(PF, 2816, 128), bslot("v", ch)[:, 15 * 128:16 * 128])
                o1dv = sdv.inc(i)
                SY.wait_ge(sdv.h, o1dv)
                dS(ST("dbg2"), 0, dbg["p2o1"], PF[:])
            i = VE.tensor_mul(a_(PBF, 2048, 2048), PSW[:], bslot("sg", ch))
            odot_dv = sdv.inc(i)
            odot_prev = odot_dv
            SE.wait_ge(sdv.h, odot_dv)
            i = SE.activation(a_(PBF, 4096, 2048), a_(PBF, 2048, 2048), AF.Square)
            osq_ac = sac.inc(i)
            VE.wait_ge(sac.h, osq_ac)
            i = VE.tensor_reduce(a_(CF, CF_SM + 16, 16), g3(a_(PBF, 4096, 2048), 128),
                                 axis=AX.X, op=ALU.add)
            VE.wait_ge(sdv.h, sdv.inc(i))
            i = VE.tensor_scalar(a_(CF, CF_SM + 32, 16), a_(CF, CF_SM + 16, 16),
                                 1.0 / HD, EPS_ATTN, ALU.mult, ALU.add)
            SE.wait_ge(sdv.h, sdv.inc(i))
            i = SE.activation(a_(CF, CF_SM + 48, 16), a_(CF, CF_SM + 32, 16), AF.Sqrt)
            VE.wait_ge(sac.h, sac.inc(i))
            i = VE.reciprocal(a_(CF, CF_SM + 64, 16), a_(CF, CF_SM + 48, 16))
            VE.wait_ge(sdv.h, sdv.inc(i))
            VE.wait_ge(spe.h, tr_pe_prev)
            in0, in1 = bass.broadcast_tensor_aps(g3(a_(PBF, 2048, 2048), 128),
                                                 g3(a_(CF, CF_SM + 64, 16), 1))
            i = VE.tensor_tensor(g3(a_(PBF, 6144, 2048), 128), in0, in1, op=ALU.mult)
            onrm_dv = sdv.inc(i)
            if "p2c1" in dump and ch == 1:
                VE.tensor_copy(a_(PF, 0, 2048), a_(PBF, 2048, 2048))
                VE.tensor_copy(a_(PF, 2048, 16), a_(CF, CF_SM + 16, 16))
                VE.tensor_copy(a_(PF, 2064, 16), a_(CF, CF_SM + 64, 16))
                i = VE.tensor_copy(a_(PF, 2080, 2016), a_(PBF, 6144, 2016))
                c1dv = sdv.inc(i)
                SY.wait_ge(sdv.h, c1dv)
                dS(ST("dbg2"), 0, dbg["p2c1"], PF[:])
            tr_ev = {}
            for grp in range(2):
                TE.wait_ge(sdv.h, onrm_dv)
                if ch == 0 and grp == 0:
                    p2_trev = {}
                if grp == 1:
                    TE.wait_ge(sdv.h, p2_trev[(ch, 0)])
                elif ch > 0:
                    TE.wait_ge(sdv.h, p2_trev[(ch - 1, 1)])
                for s8 in range(8):
                    hh = grp * 8 + s8
                    i = TE.transpose(PSTR[:, s8 * 128:(s8 + 1) * 128],
                                     a_(PBF, 6144 + hh * 128, 128), a_(CA, CA_I128, 128))
                spe.inc(i)
                VE.wait_ge(spe.h, spe.n)
                dst = A[:].rearrange("p (dt t) -> p dt t", t=2048)[
                    :, grp * 8:(grp + 1) * 8, ch * 128:(ch + 1) * 128]
                i = VE.tensor_copy(dst, g3(PSTR[:], 128))
                p2_trev[(ch, grp)] = sdv.inc(i)
            tr_pe_prev = spe.n
            o_pe_prev = su_pe
            if ch + 2 < NCH:
                SY.wait_ge(spe.h, su_pe)
                SY.wait_ge(sdv.h, odot_dv)
                SE.wait_ge(sdv.h, odot_dv)
                p2_ld[ch + 2] = p2_loads(ch + 2)
        p2_done = sdv.n

        p3_ev = {}
        p3_st = {}
        p3_ev_list = [0]
        p3_st_list = [None]
        wo_scaled = 0
        stWO = ST("wo", 2)
        stP3X = ST("p3x", 2)
        stP3T = ST("p3t", 2)
        for eh in range(2):
            GP.wait_ge(sdv.h, p2_done if eh == 0 else wo_scaled)
            for dt in range(16):
                wo_ld = dG(stWO, eh, a_(B, 16384 + dt * 1024, 1024),
                           wo_in[dt * 128:(dt + 1) * 128, eh * 1024:(eh + 1) * 1024])
            stWO.wait(VE, wo_ld)
            VE.wait_ge(spe.h, spe.n)
            for dt in range(16):
                i = VE.tensor_scalar(a_(B, dt * 1024, 1024), a_(B, 16384 + dt * 1024, 1024),
                                     a_(CF, CF_NW16 + dt, 1), None, ALU.mult)
            wo_scaled = sdv.inc(i)
            TE.wait_ge(sdv.h, wo_scaled)
            for tt in range(16):
                idx = eh * 16 + tt
                pso = (tt % 2) * 1024
                if idx >= 2:
                    TE.wait_ge(sdv.h, p3_ev_list[idx - 2 + 1])
                for ec in range(2):
                    for dt in range(16):
                        i = TE.matmul(PSW[:, pso + ec * 512:pso + (ec + 1) * 512],
                                      a_(A, dt * 2048 + tt * 128, 128),
                                      a_(B, dt * 1024 + ec * 512, 512),
                                      start=(dt == 0), stop=(dt == 15))
                spe.inc(i)
                if idx >= 2:
                    SE.wait_ge(sdv.h, p3_ev_list[idx - 2 + 1])
                xld = dA(stP3X, idx, a_(PF, (idx % 2) * 1024, 1024),
                         x_in[tt * 128:(tt + 1) * 128, eh * 1024:(eh + 1) * 1024])
                stP3X.wait(VE, xld)
                VE.wait_ge(spe.h, spe.n)
                if idx >= 2:
                    stP3T.wait(VE, p3_st_list[idx - 2 + 1])
                i = VE.tensor_add(a_(PF, 2048 + (idx % 2) * 1024, 1024),
                                  PSW[:, pso:pso + 1024], a_(PF, (idx % 2) * 1024, 1024))
                p3_ev_list.append(sdv.inc(i))
                SY.wait_ge(sdv.h, sdv.n)
                p3_st_list.append(dS(
                    stP3T, idx,
                    attnx_scr[tt * 128:(tt + 1) * 128, eh * 1024:(eh + 1) * 1024],
                    a_(PF, 2048 + (idx % 2) * 1024, 1024)))
        p3_pe = spe.n
        onormt_ld = None
        if "onormt" in dump:
            GP.wait_ge(spe.h, p3_pe)
            onormt_ld = dG(ST("dbg3"), 0, dbg["onormt"], A[:])

        stP3T.wait_all(SY)
        stXNT.wait_all(SY)
        xnT_ld2 = dS(stXNT, 0, B[:], xnt_scr)
        NSUP = 16
        SUPH = 512

        stSUPS = ST("sups", 2)
        stSUPA = ST("supa", 2)

        def p4_load_super(s, par, wait_pe):
            base = (par % 2) * 16384
            SY.wait_ge(spe.h, wait_pe)
            SE.wait_ge(spe.h, wait_pe)
            if onormt_ld is not None:
                ST("dbg3").wait(SY, onormt_ld)
                ST("dbg3").wait(SE, onormt_ld)
            for gup in range(2):
                coff = gup * HFF + s * SUPH
                boff = base + gup * 8192
                for dt in range(16):
                    dst = a_(A, boff + dt * 512, 512)
                    srcw = wgu_hat[dt * 128:(dt + 1) * 128, coff:coff + SUPH]
                    if dt % 2 == 0:
                        ls = dS(stSUPS, s, dst, srcw)
                    else:
                        l2 = dA(stSUPA, s, dst, srcw)
            return (ls, l2)

        SY.wait_ge(scc.h, scc.n)
        SE.wait_ge(scc.h, scc.n)
        sup_ld = {0: p4_load_super(0, 0, p3_pe), 1: p4_load_super(1, 1, p3_pe)}
        wa0_ld = None
        if "wA0" in dump:
            stSUPS.wait(GP, sup_ld[0][0])
            stSUPA.wait(GP, sup_ld[0][1])
            wa0_ld = dG(ST("dbg3"), 0, dbg["wA0"], a_(A, 0, 16384))
        if "xnt2" in dump:
            stXNT.wait(GP, xnT_ld2)
            dG(ST("dbg3"), 0, dbg["xnt2"], B[:])
        first_stats = True
        PP = [(PS0, PS1), (PSW[:, 0:512], PSW[:, 512:1024]),
              (PSW[:, 1024:1536], PSW[:, 1536:2048])]
        sil_done = [0] * 3
        h_done = [0] * 3
        hst_done = [None] * 3
        stH = ST("hst", 3)
        stats_q = []
        stats_pe = 0
        pidx = 0
        for s in range(NSUP):
            base = (s % 2) * 16384
            stSUPS.wait(TE, sup_ld[s][0])
            stSUPA.wait(TE, sup_ld[s][1])
            if s == 0:
                stXNT.wait(TE, xnT_ld2)
                TE.wait_ge(sdv.h, p3_ev_list[-1])
            for pb in range(4):
                for tc in range(4):
                    par = pidx % 3
                    pidx += 1
                    psg, psu = PP[par]
                    TE.wait_ge(sac.h, sil_done[par])
                    TE.wait_ge(sdv.h, h_done[par])
                    for dt in range(16):
                        TE.matmul(psg[:], a_(A, base + dt * 512 + pb * 128, 128),
                                  a_(B, dt * 2048 + tc * 512, 512),
                                  start=(dt == 0), stop=(dt == 15))
                    for dt in range(16):
                        i = TE.matmul(psu[:], a_(A, base + 8192 + dt * 512 + pb * 128, 128),
                                      a_(B, dt * 2048 + tc * 512, 512),
                                      start=(dt == 0), stop=(dt == 15))
                    gu_pe = spe.inc(i)
                    if "gu0" in dump and s == 0 and pb == 0 and tc == 0:
                        VE.wait_ge(spe.h, gu_pe)
                        VE.tensor_copy(a_(PF, 0, 512), psg[:])
                        i = VE.tensor_copy(a_(PF, 512, 512), psu[:])
                        gu0dv = sdv.inc(i)
                    SE.wait_ge(spe.h, gu_pe)
                    SE.wait_ge(sdv.h, h_done[par])
                    i = SE.activation(a_(PBF, par * 1536, 512), psg[:], AF.Silu)
                    sil_done[par] = sac.inc(i)
                    VE.wait_ge(sac.h, sil_done[par])
                    if hst_done[par]:
                        stH.wait(VE, hst_done[par])
                    VE.wait_ge(spe.h, stats_pe)
                    i = VE.tensor_mul(a_(PBF, 512 + par * 1536, 512),
                                      a_(PBF, par * 1536, 512), psu[:])
                    h_done[par] = sdv.inc(i)
                    SE.wait_ge(sdv.h, h_done[par])
                    SE.wait_ge(spe.h, stats_pe)
                    i = SE.activation(a_(PBF, 1024 + par * 1536, 512),
                                      a_(PBF, 512 + par * 1536, 512), AF.Square)
                    hsq_ac = sac.inc(i)
                    stats_q.append((par, tc, hsq_ac))
                    if len(stats_q) >= 3:
                        spar, stc, sac_v = stats_q.pop(0)
                        TE.wait_ge(sac.h, sac_v)
                        for ts in range(4):
                            gsub = stc * 4 + ts
                            i = TE.matmul(PS2[:, 0:16],
                                          a_(PBF, 1024 + spar * 1536 + ts * 128, 128),
                                          a_(CA, CA_OH + gsub * 16, 16),
                                          start=first_stats, stop=False,
                                          skip_group_check=True)
                            first_stats = False
                        stats_pe = spe.inc(i)
                    if "gu0" in dump and s == 0 and pb == 0 and tc == 0:
                        VE.wait_ge(sac.h, sil_done[par])
                        VE.tensor_copy(a_(PF, 1024, 512), a_(PBF, par * 1536, 512))
                        VE.wait_ge(sdv.h, h_done[par])
                        i = VE.tensor_copy(a_(PF, 1536, 512), a_(PBF, 512 + par * 1536, 512))
                        gu0dv = sdv.inc(i)
                        SY.wait_ge(sdv.h, gu0dv)
                        dS(ST("dbg2"), 0, dbg["gu0"], PF[:, 0:2048])
                    SY.wait_ge(sdv.h, h_done[par])
                    hb = s * 4 + pb
                    hst_done[par] = dS(
                        stH, par,
                        h_scr5[tc * 2:(tc + 1) * 2, hb, :, :].rearrange("a p t -> p a t"),
                        g3(a_(PBF, 512 + par * 1536, 512), 256))
            if s + 2 < NSUP:
                if wa0_ld is not None and s == 0:
                    ST("dbg3").wait(SY, wa0_ld)
                    ST("dbg3").wait(SE, wa0_ld)
                sup_ld[s + 2] = p4_load_super(s + 2, s, spe.n)
        while stats_q:
            spar, stc, sac_v = stats_q.pop(0)
            TE.wait_ge(sac.h, sac_v)
            for ts in range(4):
                gsub = stc * 4 + ts
                i = TE.matmul(PS2[:, 0:16],
                              a_(PBF, 1024 + spar * 1536 + ts * 128, 128),
                              a_(CA, CA_OH + gsub * 16, 16),
                              start=first_stats, stop=False, skip_group_check=True)
                first_stats = False
            stats_pe = spe.inc(i)
        p4_pe = spe.n

        VE.wait_ge(spe.h, p4_pe)
        i = VE.tensor_scalar(a_(CF, CF_STATS, 16), PS2[:, 0:16], 1.0 / HFF, EPS_FFN,
                             ALU.mult, ALU.add)
        SE.wait_ge(sdv.h, sdv.inc(i))
        i = SE.activation(a_(CF, CF_STATS, 16), a_(CF, CF_STATS, 16), AF.Sqrt)
        VE.wait_ge(sac.h, sac.inc(i))
        i = VE.reciprocal(a_(CF, CF_INVF, 16), a_(CF, CF_STATS, 16))
        invf_dv = sdv.inc(i)

        p5_ev_list = [0, 0, 0, 0]
        p5_st_list = [None] * 4
        p5_pe_pair = [0, 0]
        stWP5 = ST("wp5", 2)
        stWP5A = ST("wp5a", 2)
        stHT = ST("ht", 2)
        stHT2 = ST("ht2", 2)
        stHT3 = ST("ht3", 2)
        stAX = ST("ax", 4)
        stOUT = ST("out", 4)
        idx = 0
        last_q_pe = p4_pe
        for dq in range(4):
            SY.wait_ge(spe.h, last_q_pe)
            SE.wait_ge(spe.h, last_q_pe)
            GP.wait_ge(spe.h, last_q_pe)
            if dq == 0:
                stP3T.wait_all(SE)
                TE.wait_ge(sac.h, sac.n)
                TE.wait_ge(sdv.h, invf_dv)
            for hb in range(64):
                dst = a_(A, hb * 512, 512)
                srcw = wp_hat[hb * 128:(hb + 1) * 128, dq * 512:(dq + 1) * 512]
                if hb % 2 == 0:
                    wp_s = dS(stWP5, dq, dst, srcw)
                else:
                    wp_2 = dA(stWP5A, dq, dst, srcw)
            stWP5.wait(TE, wp_s)
            stWP5A.wait(TE, wp_2)
            for ttp in range(8):
                bs = a_(B, (ttp % 2) * 16384, 16384)
                SY.wait_ge(spe.h, p5_pe_pair[ttp % 2])
                if dq == 0 and ttp < 2:
                    stH.wait_all(SY)
                SE.wait_ge(spe.h, p5_pe_pair[ttp % 2])
                GP.wait_ge(spe.h, p5_pe_pair[ttp % 2])
                for hb8 in range(8):
                    dstp = g3(bs, 256)[:, hb8 * 8:(hb8 + 1) * 8, :]
                    srcp = h_scr5[ttp][hb8 * 8:(hb8 + 1) * 8].rearrange("a p t -> p a t")
                    if hb8 % 3 == 0:
                        ld = dS(stHT, ttp, dstp, srcp)
                    elif hb8 % 3 == 1:
                        ld2 = dA(stHT2, ttp, dstp, srcp)
                    else:
                        ld3 = dG(stHT3, ttp, dstp, srcp)
                stHT.wait(TE, ld)
                stHT2.wait(TE, ld2)
                stHT3.wait(TE, ld3)
                for sub in range(2):
                    tt = ttp * 2 + sub
                    q4 = idx % 4
                    psum = PSW[:, q4 * 512:(q4 + 1) * 512]
                    if idx >= 4:
                        TE.wait_ge(sdv.h, p5_ev_list[idx - 4 + 4])
                    for hb in range(64):
                        i = TE.matmul(psum, bs[:, hb * 256 + sub * 128:
                                               hb * 256 + sub * 128 + 128],
                                      a_(A, hb * 512, 512),
                                      start=(hb == 0), stop=(hb == 63))
                    mm_pe = spe.inc(i)
                    p5_pe_pair[ttp % 2] = mm_pe
                    last_q_pe = mm_pe
                    if idx >= 4:
                        SE.wait_ge(sdv.h, p5_ev_list[idx - 4 + 4])
                    axld = dA(stAX, idx, a_(PF, q4 * 512, 512),
                              attnx_scr[tt * 128:(tt + 1) * 128,
                                        dq * 512:(dq + 1) * 512])
                    stAX.wait(VE, axld)
                    VE.wait_ge(spe.h, mm_pe)
                    if idx == 0:
                        VE.wait_ge(sdv.h, invf_dv)
                    if idx >= 4 and p5_st_list[idx - 4 + 4]:
                        stOUT.wait(VE, p5_st_list[idx - 4 + 4])
                    i = VE.tensor_scalar(a_(PF, 2048 + q4 * 512, 512), psum,
                                         a_(CF, CF_INVF + tt, 1), None, ALU.mult)
                    VE.wait_ge(sdv.h, sdv.inc(i))
                    i = VE.tensor_add(a_(PF, 2048 + q4 * 512, 512),
                                      a_(PF, 2048 + q4 * 512, 512),
                                      a_(PF, q4 * 512, 512))
                    p5_ev_list.append(sdv.inc(i))
                    SY.wait_ge(sdv.h, sdv.n)
                    i = SY.dma_start(out=out_t[tt * 128:(tt + 1) * 128,
                                               dq * 512:(dq + 1) * 512],
                                     in_=a_(PF, 2048 + q4 * 512, 512))
                    p5_st_list.append(stOUT.inc(i, idx))
                    idx += 1
        stOUT.wait_all(SY)

        stDBG = ST("dbg")
        for name in dump:
            if name.startswith("_") or name in taps:
                continue
            ap0, shp, dt = scratches[name]
            stOUT.wait_all(GP)
            d_ = dG(stDBG, 0, dbg[name], ap0)
        if dump:
            stDBG.wait_all(GP)
            ST("dbg2").wait_all(GP)

    return nc


def _host_consts():
    idx = np.arange(128)
    diff = idx[None, :] - idx[:, None]
    diffT = np.where(diff >= 0, diff, 1e5).astype(np.float32)
    cplus1 = np.broadcast_to((idx + 1)[None, :], (128, 128)).astype(np.float32).copy()
    c127 = 127 - (np.arange(512) % 128)
    c127rep = np.broadcast_to(c127[None, :], (128, 512)).astype(np.float32).copy()
    ident = np.eye(128, dtype=np.float32)
    onehot = np.zeros((128, 256), dtype=np.float32)
    for j in range(16):
        onehot[:, j * 16 + j] = 1.0
    return diffT, cplus1, c127rep, ident, onehot


def kernel(L0_attn_x, L0_attn_log_decay, L0_attn_Wq, L0_attn_Wk, L0_attn_Wv,
           L0_attn_Wo, L0_attn_Wg1, L0_attn_Wg2, L0_attn_norm_w,
           L0_ffn_w_gate_up, L0_ffn_w_proj, _dump=(), _nc=None):
    f32 = np.float32
    x = np.ascontiguousarray(np.asarray(L0_attn_x, dtype=f32))
    ld = np.asarray(L0_attn_log_decay, dtype=f32)
    Wq = np.ascontiguousarray(np.asarray(L0_attn_Wq, dtype=f32))
    Wk = np.ascontiguousarray(np.asarray(L0_attn_Wk, dtype=f32))
    Wv = np.ascontiguousarray(np.asarray(L0_attn_Wv, dtype=f32))
    Wo = np.ascontiguousarray(np.asarray(L0_attn_Wo, dtype=f32))
    Wg1 = np.ascontiguousarray(np.asarray(L0_attn_Wg1, dtype=f32))
    Wg2 = np.ascontiguousarray(np.asarray(L0_attn_Wg2, dtype=f32))
    nw = np.asarray(L0_attn_norm_w, dtype=f32)
    Wgu = np.ascontiguousarray(np.asarray(L0_ffn_w_gate_up, dtype=f32))
    Wp = np.ascontiguousarray(np.asarray(L0_ffn_w_proj, dtype=f32))

    diffT, cplus1, c127rep, ident, onehot = _host_consts()
    nw16 = np.ascontiguousarray(nw.reshape(16, 128).T)
    ldbc = np.ascontiguousarray(np.broadcast_to(ld[None, :], (128, 16)))

    nc = _nc if _nc is not None else build_program(dump=_dump)
    in_maps = []
    for c in range(NCORE):
        b, hf = c // 2, c % 2
        m = {
            "x": np.ascontiguousarray(x[b, hf * T:(hf + 1) * T, :]),
            "Wq": Wq, "Wk": Wk, "Wv": Wv, "Wo": Wo, "Wg1": Wg1, "Wg2": Wg2,
            "nw16": nw16, "ldbc": ldbc,
            "wgu_sh": np.ascontiguousarray(Wgu[:, c * 256:(c + 1) * 256]),
            "wp_sh": np.ascontiguousarray(Wp[:, c * 1024:(c + 1) * 1024]),
            "flag": np.full((128, 1), float(hf), dtype=f32),
            "diffT": diffT, "cplus1": cplus1, "c127rep": c127rep,
            "ident": ident, "onehot": onehot,
        }
        in_maps.append(m)
    res = run_bass_kernel_spmd(nc, in_maps, core_ids=list(range(NCORE)))
    out = np.empty((4, 4096, 2048), dtype=f32)
    for c in range(NCORE):
        b, hf = c // 2, c % 2
        out[b, hf * T:(hf + 1) * T, :] = res.results[c]["out"]
    if _dump:
        return out, res
    return out


# version 06 (final)
# speedup vs baseline: 1.2605x; 1.0476x over previous
import numpy as np
import ml_dtypes
import concourse.bass as bass
import concourse.mybir as mybir
from concourse.bass_utils import run_bass_kernel_spmd

F32 = mybir.dt.float32
BF16 = mybir.dt.bfloat16
AF = mybir.ActivationFunctionType
ALU = mybir.AluOpType
AX = mybir.AxisListType
bf16 = ml_dtypes.bfloat16

T = 2048
D = 2048
H = 16
HD = 128
C = 128
NCH = T // C
HFF = 8192
NCORE = 8
EPS_ATTN = 1e-5
EPS_LN = 1e-5
EPS_FFN = 1.1920929e-07


def build_program(dump=()):
    nc = bass.Bass("TRN2", target_bir_lowering=False)

    x_in = nc.dram_tensor("x", [T, D], F32, kind="ExternalInput").ap()
    wq_in = nc.dram_tensor("Wq", [D, D], F32, kind="ExternalInput").ap()
    wk_in = nc.dram_tensor("Wk", [D, D], F32, kind="ExternalInput").ap()
    wv_in = nc.dram_tensor("Wv", [D, D], F32, kind="ExternalInput").ap()
    wo_in = nc.dram_tensor("Wo", [D, D], F32, kind="ExternalInput").ap()
    wg1_in = nc.dram_tensor("Wg1", [D, HD], F32, kind="ExternalInput").ap()
    wg2_in = nc.dram_tensor("Wg2", [HD, D], F32, kind="ExternalInput").ap()
    nw16_in = nc.dram_tensor("nw16", [128, 16], F32, kind="ExternalInput").ap()
    ldbc_in = nc.dram_tensor("ldbc", [128, 16], F32, kind="ExternalInput").ap()
    wgu_sh_in = nc.dram_tensor("wgu_sh", [2 * HFF, D // NCORE], F32, kind="ExternalInput").ap()
    wp_sh_in = nc.dram_tensor("wp_sh", [D, HFF // NCORE], F32, kind="ExternalInput").ap()
    flag_in = nc.dram_tensor("flag", [128, 1], F32, kind="ExternalInput").ap()
    diffT_in = nc.dram_tensor("diffT", [128, 128], F32, kind="ExternalInput").ap()
    cplus1_in = nc.dram_tensor("cplus1", [128, 128], F32, kind="ExternalInput").ap()
    c127rep_in = nc.dram_tensor("c127rep", [128, 512], F32, kind="ExternalInput").ap()
    ident_in = nc.dram_tensor("ident", [128, 128], F32, kind="ExternalInput").ap()
    onehot_in = nc.dram_tensor("onehot", [128, 256], F32, kind="ExternalInput").ap()
    out_t = nc.dram_tensor("out", [T, D], F32, kind="ExternalOutput").ap()

    xn_scr = nc.dram_tensor("xn_scr", [T, D], BF16, kind="Internal").ap()
    q_scr = nc.dram_tensor("q_scr", [D, T], BF16, kind="Internal").ap()
    k_scr = nc.dram_tensor("k_scr", [D, T], BF16, kind="Internal").ap()
    ktd_scr = nc.dram_tensor("ktd_scr", [D, T], BF16, kind="Internal").ap()
    v_scr = nc.dram_tensor("v_scr", [T, D], BF16, kind="Internal").ap()
    sg_scr = nc.dram_tensor("sg_scr", [T, D], BF16, kind="Internal").ap()
    h_scr = nc.dram_tensor("h_scr", [8 * 64 * 128, 256], BF16, kind="Internal").ap()
    h_scr5 = h_scr.rearrange("(tp hb p) t -> tp hb p t", hb=64, p=128)
    attnx_scr = nc.dram_tensor("attnx_scr", [T, D], F32, kind="Internal").ap()
    wgu_my = nc.dram_tensor("wgu_my", [D // NCORE, 2 * HFF], BF16, kind="Internal").ap()
    wp_my = nc.dram_tensor("wp_my", [HFF // NCORE, D], BF16, kind="Internal").ap()
    wgu_hat = nc.dram_tensor("wgu_hat", [D, 2 * HFF], BF16, kind="Internal").ap()
    wp_hat = nc.dram_tensor("wp_hat", [HFF, D], BF16, kind="Internal").ap()
    xnt_scr = nc.dram_tensor("xnt_scr", [128, 32768], BF16, kind="Internal").ap()
    s_end = nc.dram_tensor("s_end", [128, H * HD], BF16, kind="Internal").ap()
    s_gath = nc.dram_tensor("s_gath", [256, H * HD], BF16, kind="Internal").ap()
    scratches = {"xn_scr": (xn_scr, [T, D], BF16), "q_scr": (q_scr, [D, T], BF16),
                 "k_scr": (k_scr, [D, T], BF16), "ktd_scr": (ktd_scr, [D, T], BF16),
                 "v_scr": (v_scr, [T, D], BF16), "sg_scr": (sg_scr, [T, D], BF16),
                 "attnx_scr": (attnx_scr, [T, D], F32),
                 "wgu_hat": (wgu_hat, [D, 2 * HFF], BF16),
                 "wp_hat": (wp_hat, [HFF, D], BF16),
                 "s_gath": (s_gath, [256, H * HD], BF16),
                 "s_end": (s_end, [128, H * HD], BF16),
                 "h_scr": (h_scr, [8 * 64 * 128, 256], BF16)}
    taps = {"onormt": ([128, 32768], BF16), "wA0": ([128, 16384], BF16),
            "xnt2": ([128, 32768], BF16), "gu0": ([128, 2048], F32),
            "p2c1": ([128, 4096], F32), "p2o1": ([128, 4096], F32),
            "ktd15": ([128, 4096], F32)}
    dbg = {}
    for name in dump:
        if name.startswith("_"):
            continue
        if name in taps:
            shp, dt = taps[name]
        else:
            ap0, shp, dt = scratches[name]
        dbg[name] = nc.dram_tensor("dbg_" + name, shp, dt, kind="ExternalOutput").ap()

    A = nc.alloc_sbuf_tensor("A", [128, 32768], BF16)
    B = nc.alloc_sbuf_tensor("B", [128, 32768], BF16)
    PBF = nc.alloc_sbuf_tensor("PBF", [128, 12288], BF16)
    PF = nc.alloc_sbuf_tensor("PF", [128, 4096], F32)
    CA = nc.alloc_sbuf_tensor("CA", [128, 8192], BF16)
    CF = nc.alloc_sbuf_tensor("CF", [128, 1664], F32)
    QB = nc.alloc_sbuf_tensor("QB", [128, 7168], BF16)

    CF_DIFFT, CF_CP1, CF_C127, CF_LDBC, CF_NW16, CF_FLAG = 0, 128, 256, 768, 784, 800
    CF_KD, CF_CD, CF_IDENT, CF_OH, CF_STATS, CF_INVF, CF_SM = 801, 817, 833, 961, 1217, 1233, 1280
    CA_DT, CA_QD, CA_CDI, CA_I128, CA_OH = 0, 2048, 4096, 6144, 6272

    PSW = nc.alloc_psum_tensor("PSW", [128, 2048], F32)
    PS0 = nc.alloc_psum_tensor("PS0", [128, 512], F32)
    PS1 = nc.alloc_psum_tensor("PS1", [128, 512], F32)
    PS2 = nc.alloc_psum_tensor("PS2", [128, 512], F32)
    PSTR = nc.alloc_psum_tensor("PSTR", [128, 1024], BF16)

    class Sem:
        def __init__(self, name):
            self.h = nc.alloc_semaphore(name)
            self.n = 0

        def inc(self, inst, k=1):
            inst.then_inc(self.h, k)
            self.n += k
            return self.n

    scc = Sem("scc")
    spe = Sem("spe")
    sdv = Sem("sdv")
    sac = Sem("sac")

    class DStream:

        def __init__(self, name, k):
            self.sems = [nc.alloc_semaphore(f"{name}{j}") for j in range(k)]
            self.cnt = [0] * k
            self.k = k

        def inc(self, inst, slot=0):
            j = slot % self.k
            inst.then_inc(self.sems[j], 16)
            self.cnt[j] += 16
            return (j, self.cnt[j])

        def tok(self, slot=0):
            j = slot % self.k
            return (j, self.cnt[j])

        def wait(self, eng, token):
            j, v = token
            eng.wait_ge(self.sems[j], v)

        def wait_all(self, eng):
            for j in range(self.k):
                eng.wait_ge(self.sems[j], self.cnt[j])

    def a_(t, off, w):
        return t[:, off:off + w]

    def g3(ap, k):
        return ap.rearrange("p (g k) -> p g k", k=k)

    STREAMS = {}

    with nc.Block() as blk:
        TE, VE, SE, SY, GP = nc.tensor, nc.vector, nc.scalar, nc.sync, nc.gpsimd

        def ST(name, k=1):
            if name not in STREAMS:
                STREAMS[name] = DStream(name, k)
            return STREAMS[name]

        def dS(st, slot, dst, src):
            return st.inc(SY.dma_start(out=dst, in_=src), slot)

        def dTS(st, slot, dst, src):
            return st.inc(SY.dma_start_transpose(dst, src), slot)

        def dA(st, slot, dst, src):
            return st.inc(SE.dma_start(out=dst, in_=src), slot)

        def dG(st, slot, dst, src):
            return st.inc(GP.dma_start(out=dst, in_=src), slot)

        stC = ST("c")
        dS(stC, 0, a_(CF, CF_DIFFT, 128), diffT_in)
        dS(stC, 0, a_(CF, CF_CP1, 128), cplus1_in)
        dS(stC, 0, a_(CF, CF_C127, 512), c127rep_in)
        dS(stC, 0, a_(CF, CF_LDBC, 16), ldbc_in)
        dS(stC, 0, a_(CF, CF_NW16, 16), nw16_in)
        dS(stC, 0, a_(CF, CF_FLAG, 1), flag_in)
        dS(stC, 0, a_(CF, CF_IDENT, 128), ident_in)
        dS(stC, 0, a_(CF, CF_OH, 256), onehot_in)

        stC.wait_all(SE)
        for h in range(H):
            SE.activation(a_(CA, CA_DT + h * 128, 128), a_(CF, CF_DIFFT, 128), AF.Exp,
                          scale=a_(CF, CF_LDBC + h, 1))
            SE.activation(a_(CA, CA_QD + h * 128, 128), a_(CF, CF_CP1, 128), AF.Exp,
                          scale=a_(CF, CF_LDBC + h, 1))
            i = SE.activation(a_(CF, CF_KD + h, 1), a_(CF, CF_C127, 1), AF.Exp,
                              scale=a_(CF, CF_LDBC + h, 1))
        i = SE.activation(a_(CF, CF_CD, 16), a_(CF, CF_LDBC, 16), AF.Exp, scale=float(C))
        consts_act = sac.inc(i)
        stC.wait_all(VE)
        VE.tensor_copy(a_(CA, CA_I128, 128), a_(CF, CF_IDENT, 128))
        i = VE.tensor_copy(a_(CA, CA_OH, 256), a_(CF, CF_OH, 256))
        VE.wait_ge(sdv.h, sdv.inc(i))
        VE.wait_ge(sac.h, consts_act)
        for h in range(H):
            i = VE.tensor_scalar(a_(CA, CA_CDI + h * 128, 128), a_(CA, CA_I128, 128),
                                 a_(CF, CF_CD + h, 1), None, ALU.mult)
        consts_cdi = sdv.inc(i)

        q_tiles = []
        for r in range(32):
            q_tiles.append(("gu", r))
        for r in range(16):
            q_tiles.append(("wp", r))
        if "_noq" in dump:
            q_tiles = []
        qp1 = a_(QB, 2048, 1024); qp2 = a_(QB, 3072, 1024)
        qtq = a_(QB, 4096, 1024); qwh = a_(QB, 5120, 1024)
        redq = a_(CF, CF_SM + 96, 16)
        sbfq = a_(CA, 6528, 16); hsqq = a_(CA, 6560, 16); nhsq = a_(CA, 6592, 16)
        qstate = {"dv": {}, "tr": 0, "ev": 0, "stg": [None, None], "next": 0,
                  "seeded": False}
        stQL = ST("ql", 2)
        stQS = ST("qs", 2)

        def emit_q_tile():
            it = qstate["next"]
            if it >= len(q_tiles):
                return False
            qstate["next"] += 1
            kind, r = q_tiles[it]
            wbf = a_(QB, (it % 2) * 1024, 1024)
            if it >= 2:
                GP.wait_ge(sdv.h, qstate["dv"][it - 2])
            if kind == "gu":
                qld = dG(stQL, it, g3(wbf, 256),
                         wgu_sh_in[r * 512:(r + 1) * 512, :]
                         .rearrange("(a p) d -> p a d", p=128))
            else:
                qld = dG(stQL, it, wbf, wp_sh_in[r * 128:(r + 1) * 128, :])
            stQL.wait(VE, qld)
            i = VE.tensor_reduce(redq, g3(wbf, 64), axis=AX.X, op=ALU.add,
                                 apply_absolute_value=True)
            VE.wait_ge(sdv.h, sdv.inc(i))
            i = VE.tensor_scalar(sbfq, redq, 1.0 / 64.0, 1e-8, ALU.mult, ALU.max)
            VE.wait_ge(sdv.h, sdv.inc(i))
            VE.tensor_scalar(hsqq, sbfq, 0.5, None, ALU.mult)
            i = VE.tensor_scalar(nhsq, sbfq, -0.5, None, ALU.mult)
            VE.wait_ge(sdv.h, sdv.inc(i))
            a0, a1 = bass.broadcast_tensor_aps(g3(wbf, 64), g3(hsqq, 1))
            VE.tensor_tensor(g3(qp1, 64), a0, a1, op=ALU.is_gt)
            b0, b1 = bass.broadcast_tensor_aps(g3(wbf, 64), g3(nhsq, 1))
            i = VE.tensor_tensor(g3(qp2, 64), b0, b1, op=ALU.is_lt)
            qstate["dv"][it] = sdv.inc(i)
            VE.wait_ge(sdv.h, qstate["dv"][it])
            i = VE.tensor_sub(qtq, qp1, qp2)
            VE.wait_ge(sdv.h, sdv.inc(i))
            VE.wait_ge(spe.h, qstate["tr"])
            c0, c1 = bass.broadcast_tensor_aps(g3(qtq, 64), g3(sbfq, 1))
            i = VE.tensor_tensor(g3(qwh, 64), c0, c1, op=ALU.mult)
            wh_dv = sdv.inc(i)
            TE.wait_ge(sdv.h, wh_dv)
            if not qstate["seeded"]:
                qstate["ev"] = xnT_dv
                qstate["seeded"] = True
            if kind == "gu":
                for dhalf in range(2):
                    par = (2 * it + dhalf) % 2
                    stg = a_(QB, 6144 + par * 512, 512)
                    TE.wait_ge(sdv.h, qstate["ev"])
                    for a4 in range(4):
                        i = TE.transpose(PSTR[:, a4 * 128:(a4 + 1) * 128],
                                         qwh[:, a4 * 256 + dhalf * 128:
                                             a4 * 256 + dhalf * 128 + 128],
                                         a_(CA, CA_I128, 128))
                    qstate["tr"] = spe.inc(i)
                    VE.wait_ge(spe.h, qstate["tr"])
                    if qstate["stg"][par]:
                        stQS.wait(VE, qstate["stg"][par])
                    i = VE.tensor_copy(stg, PSTR[:, 0:512])
                    qstate["ev"] = sdv.inc(i)
                    SY.wait_ge(sdv.h, qstate["ev"])
                    qstate["stg"][par] = dS(
                        stQS, par,
                        wgu_my[dhalf * 128:(dhalf + 1) * 128, r * 512:(r + 1) * 512],
                        stg)
            else:
                for half in range(2):
                    par = (2 * it + half) % 2
                    stg = a_(QB, 6144 + par * 512, 512)
                    TE.wait_ge(sdv.h, qstate["ev"])
                    for a4 in range(4):
                        i = TE.transpose(PSTR[:, a4 * 128:(a4 + 1) * 128],
                                         qwh[:, (half * 4 + a4) * 128:
                                             (half * 4 + a4) * 128 + 128],
                                         a_(CA, CA_I128, 128))
                    qstate["tr"] = spe.inc(i)
                    VE.wait_ge(spe.h, qstate["tr"])
                    if qstate["stg"][par]:
                        stQS.wait(VE, qstate["stg"][par])
                    i = VE.tensor_copy(stg, PSTR[:, 0:512])
                    qstate["ev"] = sdv.inc(i)
                    SY.wait_ge(sdv.h, qstate["ev"])
                    dstw = wp_my.rearrange("(a p) d -> p a d", p=128)[
                        :, half * 4:(half + 1) * 4, r * 128:(r + 1) * 128]
                    qstate["stg"][par] = dS(stQS, par, dstw, g3(stg, 128))
            return True

        def emit_q_gathers():
            stQS.wait_all(GP)
            stQL.wait_all(GP)
            if "_nocc" in dump:
                for rr in range(NCORE):
                    i = GP.dma_start(out=wgu_hat[rr * (D // NCORE):(rr + 1) * (D // NCORE), :],
                                     in_=wgu_my)
                    i.then_inc(scc.h, 16); scc.n += 16
                    i = GP.dma_start(out=wp_hat[rr * (HFF // NCORE):(rr + 1) * (HFF // NCORE), :],
                                     in_=wp_my)
                    i.then_inc(scc.h, 16); scc.n += 16
                GP.wait_ge(scc.h, scc.n)
            else:
                GP.collective_compute("AllGather", ALU.bypass,
                                      replica_groups=[list(range(NCORE))],
                                      ins=[wgu_my.opt()], outs=[wgu_hat.opt()]).then_inc(scc.h, 1)
                scc.n += 1
                GP.collective_compute("AllGather", ALU.bypass,
                                      replica_groups=[list(range(NCORE))],
                                      ins=[wp_my.opt()], outs=[wp_hat.opt()]).then_inc(scc.h, 1)
                scc.n += 1


        p0_act = [0] * 16
        p0_dv = [0] * 16
        p0_tr = [0] * 16
        xnT_evac = 0
        for tt in range(16):
            if tt >= 2:
                SY.wait_ge(sac.h, p0_act[tt - 2])
                SY.wait_ge(sdv.h, p0_dv[tt - 2])
            stX = ST("x", 2)
            xl = dS(stX, tt, a_(PF, (tt % 2) * 2048, 2048), x_in[tt * 128:(tt + 1) * 128, :])
            xt = a_(PF, (tt % 2) * 2048, 2048)
            sums = a_(CF, CF_SM, 1); sums2 = a_(CF, CF_SM + 1, 1)
            m_ = a_(CF, CF_SM + 2, 1); var_ = a_(CF, CF_SM + 3, 1)
            sd_ = a_(CF, CF_SM + 4, 1); rstd_ = a_(CF, CF_SM + 5, 1)
            msq_ = a_(CF, CF_SM + 6, 1)
            stX.wait(SE, xl)
            if tt >= 1:
                SE.wait_ge(sdv.h, p0_dv[tt - 1])
            i = SE.activation(PSW[:], xt, AF.Copy, accum_out=sums)
            SE.wait_ge(sac.h, sac.inc(i))
            i = SE.activation(PSW[:], xt, AF.Square, accum_out=sums2)
            p0_act[tt] = sac.inc(i)
            VE.wait_ge(sac.h, p0_act[tt])
            VE.tensor_scalar(m_, sums, 1.0 / D, None, ALU.mult)
            i = VE.tensor_scalar(var_, sums2, 1.0 / D, None, ALU.mult)
            VE.wait_ge(sdv.h, sdv.inc(i))
            i = VE.tensor_tensor(msq_, m_, m_, op=ALU.mult)
            VE.wait_ge(sdv.h, sdv.inc(i))
            i = VE.tensor_tensor(var_, var_, msq_, op=ALU.subtract)
            VE.wait_ge(sdv.h, sdv.inc(i))
            i = VE.tensor_scalar(var_, var_, 1.0, EPS_LN, ALU.mult, ALU.add)
            SE.wait_ge(sdv.h, sdv.inc(i))
            i = SE.activation(sd_, var_, AF.Sqrt)
            VE.wait_ge(sac.h, sac.inc(i))
            i = VE.reciprocal(rstd_, sd_)
            VE.wait_ge(sdv.h, sdv.inc(i))
            if tt >= 2:
                VE.wait_ge(spe.h, p0_tr[tt - 2])
            i = VE.tensor_scalar(a_(PBF, (tt % 2) * 2048, 2048), xt, m_, rstd_,
                                 ALU.subtract, ALU.mult)
            p0_dv[tt] = sdv.inc(i)
            TE.wait_ge(sdv.h, p0_dv[tt])
            for rnd in range(2):
                TE.wait_ge(sdv.h, xnT_evac)
                for a8 in range(8):
                    dt = rnd * 8 + a8
                    i = TE.transpose(PSTR[:, a8 * 128:(a8 + 1) * 128],
                                     a_(PBF, (tt % 2) * 2048 + dt * 128, 128),
                                     a_(CA, CA_I128, 128))
                p0_tr[tt] = spe.inc(i)
                VE.wait_ge(spe.h, p0_tr[tt])
                dstB = B[:].rearrange("p (dt t) -> p dt t", t=2048)[
                    :, rnd * 8:(rnd + 1) * 8, tt * 128:(tt + 1) * 128]
                i = VE.tensor_copy(dstB, g3(PSTR[:], 128))
                xnT_evac = sdv.inc(i)
        xnT_dv = xnT_evac
        stXNT = ST("xnt")
        SY.wait_ge(sdv.h, xnT_dv)
        xnt_st = dS(stXNT, 0, xnt_scr, B[:])

        stWG1 = ST("wg1")
        for dt in range(16):
            wg1_ld = dG(stWG1, 0, a_(PBF, 10240 + dt * 128, 128),
                        wg1_in[dt * 128:(dt + 1) * 128, :])
        stWG1.wait(TE, wg1_ld)
        TE.wait_ge(sdv.h, xnT_dv)
        g1_ev = [0] * 4
        for tc in range(4):
            ps = (PS0, PS1)[tc % 2]
            if tc >= 2:
                TE.wait_ge(sdv.h, g1_ev[tc - 2])
            for dt in range(16):
                i = TE.matmul(ps[:], a_(PBF, 10240 + dt * 128, 128),
                              a_(B, dt * 2048 + tc * 512, 512),
                              start=(dt == 0), stop=(dt == 15))
            spe.inc(i)
            VE.wait_ge(spe.h, spe.n)
            i = VE.tensor_copy(a_(PBF, tc * 512, 512), ps[:])
            g1_ev[tc] = sdv.inc(i)
        g1_done = sdv.n
        g1_pe = spe.n

        stWH = ST("wh2", 2)
        whalf_idx = [0]

        def load_w_half(w_in, half, base, wait_pe):
            GP.wait_ge(spe.h, wait_pe)
            slot = whalf_idx[0]
            whalf_idx[0] += 1
            for dt in range(16):
                r = dG(stWH, slot, a_(A, base + dt * 1024, 1024),
                       w_in[dt * 128:(dt + 1) * 128, half * 1024:(half + 1) * 1024])
            return r

        state = {"qrow_free": None, "krow_free": None, "ktdrow_free": None, "kdrep_rd": 0}
        bank_sil = [0, 0]

        def proj_lhsT(w_in, dst_scr, decay_scr, wait_pe0):
            h0 = load_w_half(w_in, 0, 0, wait_pe0)
            h1 = load_w_half(w_in, 1, 16384, wait_pe0)
            half_ld = (h0, h1)
            for et in range(16):
                emit_q_tile()
                half = et // 8
                hd_ = et % 8
                base = 16384 * half
                stWH.wait(TE, half_ld[half])
                row = a_(PBF, 2048, 2048) if decay_scr is None else a_(PBF, 4096, 2048)
                stR = ST("rq") if decay_scr is None else ST("rk")
                stRT = ST("rkt")
                row_free = state["qrow_free"] if decay_scr is None else state["krow_free"]
                for tc in range(4):
                    ps = (PS0, PS1)[tc % 2]
                    TE.wait_ge(sac.h, bank_sil[tc % 2])
                    for dt in range(16):
                        i = TE.matmul(ps[:], a_(A, base + dt * 1024 + hd_ * 128, 128),
                                      a_(B, dt * 2048 + tc * 512, 512),
                                      start=(dt == 0), stop=(dt == 15))
                    spe.inc(i)
                    SE.wait_ge(spe.h, spe.n)
                    if tc == 0 and row_free:
                        stR.wait(SE, row_free)
                    i = SE.activation(row[:, tc * 512:(tc + 1) * 512], ps[:], AF.Silu)
                    bank_sil[tc % 2] = sac.inc(i)
                act_done = sac.n
                if decay_scr is not None:
                    SE.wait_ge(sdv.h, state["kdrep_rd"])
                    i = SE.activation(a_(PBF, 10240, 512), a_(CF, CF_C127, 512), AF.Exp,
                                      scale=a_(CF, CF_LDBC + et, 1))
                    sac.inc(i)
                    VE.wait_ge(sac.h, sac.n)
                    if state["ktdrow_free"]:
                        stRT.wait(VE, state["ktdrow_free"])
                    r0, r1 = bass.broadcast_tensor_aps(
                        row.rearrange("p (r k) -> p r k", k=512),
                        a_(PBF, 10240, 512).rearrange("p (r k) -> p r k", r=1))
                    i = VE.tensor_tensor(
                        a_(PBF, 6144, 2048).rearrange("p (r k) -> p r k", k=512),
                        r0, r1, op=ALU.mult)
                    state["kdrep_rd"] = sdv.inc(i)
                    if "ktd15" in dump and et == 15:
                        VE.wait_ge(sdv.h, state["kdrep_rd"])
                        VE.tensor_copy(a_(PF, 0, 2048), a_(PBF, 6144, 2048))
                        VE.tensor_copy(a_(PF, 2048, 512), a_(PBF, 10240, 512))
                        i = VE.tensor_copy(a_(PF, 2560, 1024), a_(PBF, 4096, 1024))
                        k15dv = sdv.inc(i)
                        SY.wait_ge(sdv.h, k15dv)
                        dS(ST("dbg2"), 0, dbg["ktd15"], PF[:])
                    SY.wait_ge(sdv.h, state["kdrep_rd"])
                    state["ktdrow_free"] = dS(stRT, 0, decay_scr[et * 128:(et + 1) * 128, :],
                                              a_(PBF, 6144, 2048))
                SY.wait_ge(sac.h, act_done)
                st = dS(stR, 0, dst_scr[et * 128:(et + 1) * 128, :], row)
                if decay_scr is None:
                    state["qrow_free"] = st
                else:
                    state["krow_free"] = st

        TE.wait_ge(sdv.h, g1_done)
        proj_lhsT(wq_in, q_scr, None, 0)
        proj_lhsT(wk_in, k_scr, ktd_scr, spe.n)

        vh0 = load_w_half(wv_in, 0, 0, spe.n)
        vh1 = load_w_half(wv_in, 1, 16384, spe.n)
        vrow_free = {}
        stRV = ST("rv", 2)
        for tt in range(16):
            for half in range(2):
                emit_q_tile()
                emit_q_tile()
                base = 16384 * half
                stWH.wait(TE, (vh0, vh1)[half])
                vr = a_(PBF, 8192 + ((tt * 2 + half) % 2) * 1024, 1024)
                for ec in range(2):
                    ps = (PS0, PS1)[ec]
                    TE.wait_ge(sac.h, bank_sil[ec])
                    for dt in range(16):
                        i = TE.matmul(ps[:], a_(B, dt * 2048 + tt * 128, 128),
                                      a_(A, base + dt * 1024 + ec * 512, 512),
                                      start=(dt == 0), stop=(dt == 15))
                    spe.inc(i)
                    SE.wait_ge(spe.h, spe.n)
                    if ec == 0 and vrow_free.get((tt * 2 + half) % 2):
                        stRV.wait(SE, vrow_free[(tt * 2 + half) % 2])
                    i = SE.activation(vr[:, ec * 512:(ec + 1) * 512], ps[:], AF.Silu)
                    bank_sil[ec] = sac.inc(i)
                SY.wait_ge(sac.h, sac.n)
                vrow_free[(tt * 2 + half) % 2] = dS(
                    stRV, (tt * 2 + half),
                    v_scr[tt * 128:(tt + 1) * 128, half * 1024:(half + 1) * 1024], vr)
        p1_pe = spe.n

        while emit_q_tile():
            pass
        emit_q_gathers()

        VE.wait_ge(sdv.h, consts_cdi)
        for s_ in ("rq", "rk", "rkt", "rv"):
            ST(s_).wait_all(VE)
        i = VE.memset(a_(QB, 0, 2048), 0.0)
        a_sprev = sdv.inc(i)
        stACH = ST("ach", 2)
        for s_ in ("rq", "rk", "rkt", "rv"):
            ST(s_).wait_all(SY)

        def a_slots(ch):
            return (a_(PBF, 2048 + (ch % 2) * 4096, 2048),
                    a_(PBF, 4096 + (ch % 2) * 4096, 2048))

        def a_loads(ch):
            kt_, v_ = a_slots(ch)
            dTS(stACH, ch, kt_, ktd_scr[:, ch * 128:(ch + 1) * 128])
            return dS(stACH, ch, v_, v_scr[ch * 128:(ch + 1) * 128, :])

        a_ld = {0: a_loads(0), 1: a_loads(1)}
        a_mm = {}
        for ch in range(NCH):
            scur = a_(QB, (ch % 2) * 2048, 2048)
            snxt = a_(QB, ((ch + 1) % 2) * 2048, 2048)
            ktb, vb = a_slots(ch)
            stACH.wait(TE, a_ld[ch])
            TE.wait_ge(sdv.h, a_sprev)
            if ch == 0:
                TE.wait_ge(sac.h, sac.n)
            for q in range(4):
                ps = (PS0, PS1, PS2, PSW[:, 0:512])[q]
                for hh in range(4 * q, 4 * q + 4):
                    TE.matmul(ps[:, (hh % 4) * 128:(hh % 4 + 1) * 128],
                              ktb[:, hh * 128:(hh + 1) * 128],
                              vb[:, hh * 128:(hh + 1) * 128],
                              start=(hh % 4 == 0), stop=False, skip_group_check=True)
                for hh in range(4 * q, 4 * q + 4):
                    i = TE.matmul(ps[:, (hh % 4) * 128:(hh % 4 + 1) * 128],
                                  a_(CA, CA_CDI + hh * 128, 128),
                                  scur[:, hh * 128:(hh + 1) * 128],
                                  start=False, stop=(hh % 4 == 3), skip_group_check=True)
                spe.inc(i)
                VE.wait_ge(spe.h, spe.n)
                i = VE.tensor_copy(snxt[:, q * 512:(q + 1) * 512], ps[:])
                sdv.inc(i)
            a_sprev = sdv.n
            a_mm[ch] = spe.n
            if ch + 2 < NCH:
                SY.wait_ge(spe.h, a_mm[ch])
                a_ld[ch + 2] = a_loads(ch + 2)
        s_final = a_(QB, (NCH % 2) * 2048, 2048)
        stSE = ST("se")
        SY.wait_ge(sdv.h, a_sprev)
        se_st = dS(stSE, 0, s_end, s_final)
        stSE.wait(GP, se_st)
        if "_nocc" in dump:
            i = GP.dma_start(out=s_gath[0:128, :], in_=s_end)
            i.then_inc(scc.h, 16); scc.n += 16
        else:
            GP.collective_compute("AllGather", ALU.bypass,
                                  replica_groups=[[0, 1], [2, 3], [4, 5], [6, 7]],
                                  ins=[s_end.opt()], outs=[s_gath.opt()]).then_inc(scc.h, 1)
            scc.n += 1
        cc_s = scc.n

        stWG2 = ST("wg2")
        wg2_ld = dG(stWG2, 0, a_(PBF, 2048, 2048), wg2_in)
        stSG = ST("sgst", 2)
        sg_st = [None, None]
        g_idx = 0
        for tt in range(16):
            for ec in range(4):
                ps = (PS0, PS1)[g_idx % 2]
                if g_idx == 0:
                    stWG2.wait(TE, wg2_ld)
                    TE.wait_ge(sdv.h, sdv.n)
                    g_sig = [sac.n, sac.n]
                TE.wait_ge(sac.h, g_sig[g_idx % 2])
                i = TE.matmul(ps[:], a_(PBF, tt * 128, 128),
                              a_(PBF, 2048 + ec * 512, 512), start=True, stop=True)
                spe.inc(i)
                SE.wait_ge(spe.h, spe.n)
                if sg_st[g_idx % 2]:
                    stSG.wait(SE, sg_st[g_idx % 2])
                i = SE.activation(a_(PBF, 4096 + (g_idx % 2) * 512, 512), ps[:], AF.Sigmoid)
                g_sig[g_idx % 2] = sac.inc(i)
                SY.wait_ge(sac.h, sac.n)
                sg_st[g_idx % 2] = dS(
                    stSG, g_idx,
                    sg_scr[tt * 128:(tt + 1) * 128, ec * 512:(ec + 1) * 512],
                    a_(PBF, 4096 + (g_idx % 2) * 512, 512))
                g_idx += 1
        g_pe = spe.n

        def bslot(stream, ch):
            offs = {"q": 0, "k": 4096, "kt": 8192, "v": 12288, "sg": 16384}
            return a_(B, offs[stream] + (ch % 2) * 2048, 2048)

        stSI = ST("si")
        SY.wait_ge(scc.h, scc.n)
        stXNT.wait(SY, xnt_st)
        si_ld = dS(stSI, 0, a_(B, 20480, 2048), s_gath[0:128, :])
        stSI.wait(VE, si_ld)
        i = VE.tensor_scalar(a_(QB, 0, 2048), a_(B, 20480, 2048),
                             a_(CF, CF_FLAG, 1), None, ALU.mult)
        s_ready = sdv.inc(i)

        stC2S = ST("c2s", 2)
        stC2A = ST("c2a", 2)

        def p2_loads(ch):
            dS(stC2S, ch, g3(bslot("q", ch), 128),
               q_scr[:, ch * 128:(ch + 1) * 128].rearrange("(a p) t -> p a t", p=128))
            dS(stC2S, ch, g3(bslot("k", ch), 128),
               k_scr[:, ch * 128:(ch + 1) * 128].rearrange("(a p) t -> p a t", p=128))
            l1 = dTS(stC2S, ch, bslot("kt", ch), ktd_scr[:, ch * 128:(ch + 1) * 128])
            dA(stC2A, ch, bslot("v", ch), v_scr[ch * 128:(ch + 1) * 128, :])
            l2 = dA(stC2A, ch, bslot("sg", ch), sg_scr[ch * 128:(ch + 1) * 128, :])
            return (l1, l2)

        ST("rq").wait_all(SY)
        ST("rk").wait_all(SY)
        ST("sgst", 2).wait_all(SY)
        ST("rq").wait_all(SE)
        ST("rv", 2).wait_all(SE)
        ST("sgst", 2).wait_all(SE)
        stXNT.wait(SE, xnt_st)
        p2_ld = {0: p2_loads(0), 1: p2_loads(1)}
        sct_ev = {}
        o_pe_prev = g_pe
        su_pe_prev = g_pe
        odot_prev = 0
        tr_pe_prev = 0
        p2_sdone = s_ready
        for ch in range(NCH):
            scur = a_(QB, (ch % 2) * 2048, 2048)
            snxt = a_(QB, ((ch + 1) % 2) * 2048, 2048)
            stC2S.wait(TE, p2_ld[ch][0])
            stC2A.wait(TE, p2_ld[ch][1])
            if isinstance(p2_sdone, tuple):
                TE.wait_ge(sac.h, p2_sdone[1])
            else:
                TE.wait_ge(sdv.h, p2_sdone)
            if ch == 0:
                TE.wait_ge(sac.h, sac.n)
            VE.wait_ge(spe.h, o_pe_prev)
            VE.wait_ge(sac.h, sac.n)
            for hq in range(4):
                ps = (PS0, PS1)[hq % 2]
                TE.wait_ge(sdv.h, sct_ev.get((ch, hq - 2), sct_ev.get((ch - 1, hq + 2), 0)))
                for hh in range(4 * hq, 4 * hq + 4):
                    i = TE.matmul(ps[:, (hh % 4) * 128:(hh % 4 + 1) * 128],
                                  bslot("k", ch)[:, hh * 128:(hh + 1) * 128],
                                  bslot("q", ch)[:, hh * 128:(hh + 1) * 128],
                                  start=(hh % 4 == 0), stop=(hh % 4 == 3),
                                  skip_group_check=True)
                spe.inc(i)
                VE.wait_ge(spe.h, spe.n)
                i = VE.tensor_tensor(a_(PBF, hq * 512, 512), ps[:],
                                     a_(CA, CA_DT + hq * 512, 512), op=ALU.mult)
                sct_ev[(ch, hq)] = sdv.inc(i)
            i = VE.tensor_mul(a_(PBF, 2048, 2048), bslot("q", ch), a_(CA, CA_QD, 2048))
            qtl_dv = sdv.inc(i)
            TE.wait_ge(sdv.h, qtl_dv)
            TE.wait_ge(sdv.h, odot_prev)
            for hh in range(H):
                TE.wait_ge(sdv.h, sct_ev[(ch, hh // 4)]) if hh % 4 == 0 else None
                TE.matmul(PSW[:, hh * 128:(hh + 1) * 128],
                          a_(PBF, hh * 128, 128),
                          bslot("v", ch)[:, hh * 128:(hh + 1) * 128],
                          start=(hh % 4 == 0), stop=False, skip_group_check=True)
                i = TE.matmul(PSW[:, hh * 128:(hh + 1) * 128],
                              a_(PBF, 2048 + hh * 128, 128),
                              scur[:, hh * 128:(hh + 1) * 128],
                              start=False, stop=(hh % 4 == 3), skip_group_check=True)
            o_pe = spe.inc(i)
            su_ev = 0
            for q in range(4):
                if q > 0:
                    TE.wait_ge(sac.h, su_ev)
                for hh in range(4 * q, 4 * q + 4):
                    TE.matmul(PS2[:, (hh % 4) * 128:(hh % 4 + 1) * 128],
                              bslot("kt", ch)[:, hh * 128:(hh + 1) * 128],
                              bslot("v", ch)[:, hh * 128:(hh + 1) * 128],
                              start=(hh % 4 == 0), stop=False, skip_group_check=True)
                for hh in range(4 * q, 4 * q + 4):
                    i = TE.matmul(PS2[:, (hh % 4) * 128:(hh % 4 + 1) * 128],
                                  a_(CA, CA_CDI + hh * 128, 128),
                                  scur[:, hh * 128:(hh + 1) * 128],
                                  start=False, stop=(hh % 4 == 3), skip_group_check=True)
                spe.inc(i)
                SE.wait_ge(spe.h, spe.n)
                i = SE.activation(snxt[:, q * 512:(q + 1) * 512], PS2[:], AF.Copy)
                su_ev = sac.inc(i)
            p2_sdone = ("sac", su_ev)
            su_pe = spe.n
            VE.wait_ge(spe.h, o_pe)
            if "p2o1" in dump and ch == 1:
                VE.tensor_copy(a_(PF, 0, 2048), PSW[:])
                VE.tensor_copy(a_(PF, 2048, 512), scur[:, 1536:2048])
                VE.tensor_copy(a_(PF, 2560, 128), a_(PBF, 2048 + 15 * 128, 128))
                VE.tensor_copy(a_(PF, 2688, 128), a_(PBF, 15 * 128, 128))
                i = VE.tensor_copy(a_(PF, 2816, 128), bslot("v", ch)[:, 15 * 128:16 * 128])
                o1dv = sdv.inc(i)
                SY.wait_ge(sdv.h, o1dv)
                dS(ST("dbg2"), 0, dbg["p2o1"], PF[:])
            i = VE.tensor_mul(a_(PBF, 2048, 2048), PSW[:], bslot("sg", ch))
            odot_dv = sdv.inc(i)
            odot_prev = odot_dv
            SE.wait_ge(sdv.h, odot_dv)
            i = SE.activation(a_(PBF, 4096, 2048), a_(PBF, 2048, 2048), AF.Square)
            osq_ac = sac.inc(i)
            VE.wait_ge(sac.h, osq_ac)
            i = VE.tensor_reduce(a_(CF, CF_SM + 16, 16), g3(a_(PBF, 4096, 2048), 128),
                                 axis=AX.X, op=ALU.add)
            VE.wait_ge(sdv.h, sdv.inc(i))
            i = VE.tensor_scalar(a_(CF, CF_SM + 32, 16), a_(CF, CF_SM + 16, 16),
                                 1.0 / HD, EPS_ATTN, ALU.mult, ALU.add)
            SE.wait_ge(sdv.h, sdv.inc(i))
            i = SE.activation(a_(CF, CF_SM + 48, 16), a_(CF, CF_SM + 32, 16), AF.Sqrt)
            VE.wait_ge(sac.h, sac.inc(i))
            i = VE.reciprocal(a_(CF, CF_SM + 64, 16), a_(CF, CF_SM + 48, 16))
            VE.wait_ge(sdv.h, sdv.inc(i))
            VE.wait_ge(spe.h, tr_pe_prev)
            in0, in1 = bass.broadcast_tensor_aps(g3(a_(PBF, 2048, 2048), 128),
                                                 g3(a_(CF, CF_SM + 64, 16), 1))
            i = VE.tensor_tensor(g3(a_(PBF, 6144, 2048), 128), in0, in1, op=ALU.mult)
            onrm_dv = sdv.inc(i)
            if "p2c1" in dump and ch == 1:
                VE.tensor_copy(a_(PF, 0, 2048), a_(PBF, 2048, 2048))
                VE.tensor_copy(a_(PF, 2048, 16), a_(CF, CF_SM + 16, 16))
                VE.tensor_copy(a_(PF, 2064, 16), a_(CF, CF_SM + 64, 16))
                i = VE.tensor_copy(a_(PF, 2080, 2016), a_(PBF, 6144, 2016))
                c1dv = sdv.inc(i)
                SY.wait_ge(sdv.h, c1dv)
                dS(ST("dbg2"), 0, dbg["p2c1"], PF[:])
            tr_ev = {}
            for grp in range(2):
                TE.wait_ge(sdv.h, onrm_dv)
                if ch == 0 and grp == 0:
                    p2_trev = {}
                if grp == 1:
                    TE.wait_ge(sac.h, p2_trev[(ch, 0)])
                elif ch > 0:
                    TE.wait_ge(sac.h, p2_trev[(ch - 1, 1)])
                for s8 in range(8):
                    hh = grp * 8 + s8
                    i = TE.transpose(PSTR[:, s8 * 128:(s8 + 1) * 128],
                                     a_(PBF, 6144 + hh * 128, 128), a_(CA, CA_I128, 128))
                spe.inc(i)
                SE.wait_ge(spe.h, spe.n)
                dst = A[:].rearrange("p (dt t) -> p dt t", t=2048)[
                    :, grp * 8:(grp + 1) * 8, ch * 128:(ch + 1) * 128]
                i = SE.activation(dst, g3(PSTR[:], 128), AF.Copy)
                p2_trev[(ch, grp)] = sac.inc(i)
            tr_pe_prev = spe.n
            o_pe_prev = su_pe
            if ch + 2 < NCH:
                SY.wait_ge(spe.h, su_pe)
                SY.wait_ge(sdv.h, odot_dv)
                SE.wait_ge(sdv.h, odot_dv)
                p2_ld[ch + 2] = p2_loads(ch + 2)
        p2_done = sdv.n
        p2_done_ac = sac.n

        p3_ev = {}
        p3_st = {}
        p3_ev_list = [0]
        p3_st_list = [None]
        wo_scaled = 0
        stWO = ST("wo", 2)
        stP3X = ST("p3x", 2)
        stP3T = ST("p3t", 2)
        for eh in range(2):
            GP.wait_ge(sdv.h, p2_done if eh == 0 else wo_scaled)
            for dt in range(16):
                wo_ld = dG(stWO, eh, a_(B, 16384 + dt * 1024, 1024),
                           wo_in[dt * 128:(dt + 1) * 128, eh * 1024:(eh + 1) * 1024])
            stWO.wait(VE, wo_ld)
            VE.wait_ge(spe.h, spe.n)
            for dt in range(16):
                i = VE.tensor_scalar(a_(B, dt * 1024, 1024), a_(B, 16384 + dt * 1024, 1024),
                                     a_(CF, CF_NW16 + dt, 1), None, ALU.mult)
            wo_scaled = sdv.inc(i)
            TE.wait_ge(sdv.h, wo_scaled)
            if eh == 0:
                TE.wait_ge(sac.h, p2_done_ac)
            for tt in range(16):
                idx = eh * 16 + tt
                pso = (tt % 2) * 1024
                if idx >= 2:
                    TE.wait_ge(sdv.h, p3_ev_list[idx - 2 + 1])
                for ec in range(2):
                    for dt in range(16):
                        i = TE.matmul(PSW[:, pso + ec * 512:pso + (ec + 1) * 512],
                                      a_(A, dt * 2048 + tt * 128, 128),
                                      a_(B, dt * 1024 + ec * 512, 512),
                                      start=(dt == 0), stop=(dt == 15))
                spe.inc(i)
                if idx >= 2:
                    SE.wait_ge(sdv.h, p3_ev_list[idx - 2 + 1])
                xld = dA(stP3X, idx, a_(PF, (idx % 2) * 1024, 1024),
                         x_in[tt * 128:(tt + 1) * 128, eh * 1024:(eh + 1) * 1024])
                stP3X.wait(VE, xld)
                VE.wait_ge(spe.h, spe.n)
                if idx >= 2:
                    stP3T.wait(VE, p3_st_list[idx - 2 + 1])
                i = VE.tensor_add(a_(PF, 2048 + (idx % 2) * 1024, 1024),
                                  PSW[:, pso:pso + 1024], a_(PF, (idx % 2) * 1024, 1024))
                p3_ev_list.append(sdv.inc(i))
                SY.wait_ge(sdv.h, sdv.n)
                p3_st_list.append(dS(
                    stP3T, idx,
                    attnx_scr[tt * 128:(tt + 1) * 128, eh * 1024:(eh + 1) * 1024],
                    a_(PF, 2048 + (idx % 2) * 1024, 1024)))
        p3_pe = spe.n
        onormt_ld = None
        if "onormt" in dump:
            GP.wait_ge(spe.h, p3_pe)
            onormt_ld = dG(ST("dbg3"), 0, dbg["onormt"], A[:])

        stP3T.wait_all(SY)
        stXNT.wait_all(SY)
        xnT_ld2 = dS(stXNT, 0, B[:], xnt_scr)
        NSUP = 16
        SUPH = 512

        stSUPS = ST("sups", 2)
        stSUPA = ST("supa", 2)

        def p4_load_super(s, par, wait_pe):
            base = (par % 2) * 16384
            SY.wait_ge(spe.h, wait_pe)
            SE.wait_ge(spe.h, wait_pe)
            if onormt_ld is not None:
                ST("dbg3").wait(SY, onormt_ld)
                ST("dbg3").wait(SE, onormt_ld)
            for gup in range(2):
                coff = gup * HFF + s * SUPH
                boff = base + gup * 8192
                for dt in range(16):
                    dst = a_(A, boff + dt * 512, 512)
                    srcw = wgu_hat[dt * 128:(dt + 1) * 128, coff:coff + SUPH]
                    if dt % 2 == 0:
                        ls = dS(stSUPS, s, dst, srcw)
                    else:
                        l2 = dA(stSUPA, s, dst, srcw)
            return (ls, l2)

        SY.wait_ge(scc.h, scc.n)
        SE.wait_ge(scc.h, scc.n)
        sup_ld = {0: p4_load_super(0, 0, p3_pe), 1: p4_load_super(1, 1, p3_pe)}
        wa0_ld = None
        if "wA0" in dump:
            stSUPS.wait(GP, sup_ld[0][0])
            stSUPA.wait(GP, sup_ld[0][1])
            wa0_ld = dG(ST("dbg3"), 0, dbg["wA0"], a_(A, 0, 16384))
        if "xnt2" in dump:
            stXNT.wait(GP, xnT_ld2)
            dG(ST("dbg3"), 0, dbg["xnt2"], B[:])
        first_stats = True
        PP = [(PS0, PS1), (PSW[:, 0:512], PSW[:, 512:1024]),
              (PSW[:, 1024:1536], PSW[:, 1536:2048])]
        sil_done = [0] * 3
        h_done = [0] * 3
        hst_done = [None] * 3
        stH = ST("hst", 3)
        stats_q = []
        stats_pe = 0
        pidx = 0
        for s in range(NSUP):
            base = (s % 2) * 16384
            stSUPS.wait(TE, sup_ld[s][0])
            stSUPA.wait(TE, sup_ld[s][1])
            if s == 0:
                stXNT.wait(TE, xnT_ld2)
                TE.wait_ge(sdv.h, p3_ev_list[-1])
            for pb in range(4):
                for tc in range(4):
                    par = pidx % 3
                    pidx += 1
                    psg, psu = PP[par]
                    TE.wait_ge(sac.h, sil_done[par])
                    TE.wait_ge(sdv.h, h_done[par])
                    for dt in range(16):
                        TE.matmul(psg[:], a_(A, base + dt * 512 + pb * 128, 128),
                                  a_(B, dt * 2048 + tc * 512, 512),
                                  start=(dt == 0), stop=(dt == 15))
                    for dt in range(16):
                        i = TE.matmul(psu[:], a_(A, base + 8192 + dt * 512 + pb * 128, 128),
                                      a_(B, dt * 2048 + tc * 512, 512),
                                      start=(dt == 0), stop=(dt == 15))
                    gu_pe = spe.inc(i)
                    if "gu0" in dump and s == 0 and pb == 0 and tc == 0:
                        VE.wait_ge(spe.h, gu_pe)
                        VE.tensor_copy(a_(PF, 0, 512), psg[:])
                        i = VE.tensor_copy(a_(PF, 512, 512), psu[:])
                        gu0dv = sdv.inc(i)
                    SE.wait_ge(spe.h, gu_pe)
                    SE.wait_ge(sdv.h, h_done[par])
                    i = SE.activation(a_(PBF, par * 1536, 512), psg[:], AF.Silu)
                    sil_done[par] = sac.inc(i)
                    VE.wait_ge(sac.h, sil_done[par])
                    if hst_done[par]:
                        stH.wait(VE, hst_done[par])
                    VE.wait_ge(spe.h, stats_pe)
                    i = VE.tensor_mul(a_(PBF, 512 + par * 1536, 512),
                                      a_(PBF, par * 1536, 512), psu[:])
                    h_done[par] = sdv.inc(i)
                    SE.wait_ge(sdv.h, h_done[par])
                    SE.wait_ge(spe.h, stats_pe)
                    i = SE.activation(a_(PBF, 1024 + par * 1536, 512),
                                      a_(PBF, 512 + par * 1536, 512), AF.Square)
                    hsq_ac = sac.inc(i)
                    stats_q.append((par, tc, hsq_ac))
                    if len(stats_q) >= 3:
                        spar, stc, sac_v = stats_q.pop(0)
                        TE.wait_ge(sac.h, sac_v)
                        for ts in range(4):
                            gsub = stc * 4 + ts
                            i = TE.matmul(PS2[:, 0:16],
                                          a_(PBF, 1024 + spar * 1536 + ts * 128, 128),
                                          a_(CA, CA_OH + gsub * 16, 16),
                                          start=first_stats, stop=False,
                                          skip_group_check=True)
                            first_stats = False
                        stats_pe = spe.inc(i)
                    if "gu0" in dump and s == 0 and pb == 0 and tc == 0:
                        VE.wait_ge(sac.h, sil_done[par])
                        VE.tensor_copy(a_(PF, 1024, 512), a_(PBF, par * 1536, 512))
                        VE.wait_ge(sdv.h, h_done[par])
                        i = VE.tensor_copy(a_(PF, 1536, 512), a_(PBF, 512 + par * 1536, 512))
                        gu0dv = sdv.inc(i)
                        SY.wait_ge(sdv.h, gu0dv)
                        dS(ST("dbg2"), 0, dbg["gu0"], PF[:, 0:2048])
                    SY.wait_ge(sdv.h, h_done[par])
                    hb = s * 4 + pb
                    hst_done[par] = dS(
                        stH, par,
                        h_scr5[tc * 2:(tc + 1) * 2, hb, :, :].rearrange("a p t -> p a t"),
                        g3(a_(PBF, 512 + par * 1536, 512), 256))
            if s + 2 < NSUP:
                if wa0_ld is not None and s == 0:
                    ST("dbg3").wait(SY, wa0_ld)
                    ST("dbg3").wait(SE, wa0_ld)
                sup_ld[s + 2] = p4_load_super(s + 2, s, spe.n)
        while stats_q:
            spar, stc, sac_v = stats_q.pop(0)
            TE.wait_ge(sac.h, sac_v)
            for ts in range(4):
                gsub = stc * 4 + ts
                i = TE.matmul(PS2[:, 0:16],
                              a_(PBF, 1024 + spar * 1536 + ts * 128, 128),
                              a_(CA, CA_OH + gsub * 16, 16),
                              start=first_stats, stop=False, skip_group_check=True)
                first_stats = False
            stats_pe = spe.inc(i)
        p4_pe = spe.n

        VE.wait_ge(spe.h, p4_pe)
        i = VE.tensor_scalar(a_(CF, CF_STATS, 16), PS2[:, 0:16], 1.0 / HFF, EPS_FFN,
                             ALU.mult, ALU.add)
        SE.wait_ge(sdv.h, sdv.inc(i))
        i = SE.activation(a_(CF, CF_STATS, 16), a_(CF, CF_STATS, 16), AF.Sqrt)
        VE.wait_ge(sac.h, sac.inc(i))
        i = VE.reciprocal(a_(CF, CF_INVF, 16), a_(CF, CF_STATS, 16))
        invf_dv = sdv.inc(i)

        p5_ev_list = [0, 0, 0, 0]
        p5_st_list = [None] * 4
        p5_pe_pair = [0, 0]
        stWP5 = ST("wp5", 2)
        stWP5A = ST("wp5a", 2)
        stHT = ST("ht", 2)
        stHT2 = ST("ht2", 2)
        stHT3 = ST("ht3", 2)
        stAX = ST("ax", 4)
        stOUT = ST("out", 4)
        idx = 0
        last_q_pe = p4_pe
        for dq in range(4):
            SY.wait_ge(spe.h, last_q_pe)
            SE.wait_ge(spe.h, last_q_pe)
            GP.wait_ge(spe.h, last_q_pe)
            if dq == 0:
                stP3T.wait_all(SE)
                TE.wait_ge(sac.h, sac.n)
                TE.wait_ge(sdv.h, invf_dv)
            for hb in range(64):
                dst = a_(A, hb * 512, 512)
                srcw = wp_hat[hb * 128:(hb + 1) * 128, dq * 512:(dq + 1) * 512]
                if hb % 2 == 0:
                    wp_s = dS(stWP5, dq, dst, srcw)
                else:
                    wp_2 = dA(stWP5A, dq, dst, srcw)
            stWP5.wait(TE, wp_s)
            stWP5A.wait(TE, wp_2)
            for ttp in range(8):
                bs = a_(B, (ttp % 2) * 16384, 16384)
                SY.wait_ge(spe.h, p5_pe_pair[ttp % 2])
                if dq == 0 and ttp < 2:
                    stH.wait_all(SY)
                SE.wait_ge(spe.h, p5_pe_pair[ttp % 2])
                GP.wait_ge(spe.h, p5_pe_pair[ttp % 2])
                for hb8 in range(8):
                    dstp = g3(bs, 256)[:, hb8 * 8:(hb8 + 1) * 8, :]
                    srcp = h_scr5[ttp][hb8 * 8:(hb8 + 1) * 8].rearrange("a p t -> p a t")
                    if hb8 % 3 == 0:
                        ld = dS(stHT, ttp, dstp, srcp)
                    elif hb8 % 3 == 1:
                        ld2 = dA(stHT2, ttp, dstp, srcp)
                    else:
                        ld3 = dG(stHT3, ttp, dstp, srcp)
                stHT.wait(TE, ld)
                stHT2.wait(TE, ld2)
                stHT3.wait(TE, ld3)
                for sub in range(2):
                    tt = ttp * 2 + sub
                    q4 = idx % 4
                    psum = PSW[:, q4 * 512:(q4 + 1) * 512]
                    if idx >= 4:
                        TE.wait_ge(sdv.h, p5_ev_list[idx - 4 + 4])
                    for hb in range(64):
                        i = TE.matmul(psum, bs[:, hb * 256 + sub * 128:
                                               hb * 256 + sub * 128 + 128],
                                      a_(A, hb * 512, 512),
                                      start=(hb == 0), stop=(hb == 63))
                    mm_pe = spe.inc(i)
                    p5_pe_pair[ttp % 2] = mm_pe
                    last_q_pe = mm_pe
                    if idx >= 4:
                        SE.wait_ge(sdv.h, p5_ev_list[idx - 4 + 4])
                    axld = dA(stAX, idx, a_(PF, q4 * 512, 512),
                              attnx_scr[tt * 128:(tt + 1) * 128,
                                        dq * 512:(dq + 1) * 512])
                    stAX.wait(VE, axld)
                    VE.wait_ge(spe.h, mm_pe)
                    if idx == 0:
                        VE.wait_ge(sdv.h, invf_dv)
                    if idx >= 4 and p5_st_list[idx - 4 + 4]:
                        stOUT.wait(VE, p5_st_list[idx - 4 + 4])
                    i = VE.tensor_scalar(a_(PF, 2048 + q4 * 512, 512), psum,
                                         a_(CF, CF_INVF + tt, 1), None, ALU.mult)
                    VE.wait_ge(sdv.h, sdv.inc(i))
                    i = VE.tensor_add(a_(PF, 2048 + q4 * 512, 512),
                                      a_(PF, 2048 + q4 * 512, 512),
                                      a_(PF, q4 * 512, 512))
                    p5_ev_list.append(sdv.inc(i))
                    SY.wait_ge(sdv.h, sdv.n)
                    i = SY.dma_start(out=out_t[tt * 128:(tt + 1) * 128,
                                               dq * 512:(dq + 1) * 512],
                                     in_=a_(PF, 2048 + q4 * 512, 512))
                    p5_st_list.append(stOUT.inc(i, idx))
                    idx += 1
        stOUT.wait_all(SY)

        stDBG = ST("dbg")
        for name in dump:
            if name.startswith("_") or name in taps:
                continue
            ap0, shp, dt = scratches[name]
            stOUT.wait_all(GP)
            d_ = dG(stDBG, 0, dbg[name], ap0)
        if dump:
            stDBG.wait_all(GP)
            ST("dbg2").wait_all(GP)

    return nc


def _host_consts():
    idx = np.arange(128)
    diff = idx[None, :] - idx[:, None]
    diffT = np.where(diff >= 0, diff, 1e5).astype(np.float32)
    cplus1 = np.broadcast_to((idx + 1)[None, :], (128, 128)).astype(np.float32).copy()
    c127 = 127 - (np.arange(512) % 128)
    c127rep = np.broadcast_to(c127[None, :], (128, 512)).astype(np.float32).copy()
    ident = np.eye(128, dtype=np.float32)
    onehot = np.zeros((128, 256), dtype=np.float32)
    for j in range(16):
        onehot[:, j * 16 + j] = 1.0
    return diffT, cplus1, c127rep, ident, onehot


def kernel(L0_attn_x, L0_attn_log_decay, L0_attn_Wq, L0_attn_Wk, L0_attn_Wv,
           L0_attn_Wo, L0_attn_Wg1, L0_attn_Wg2, L0_attn_norm_w,
           L0_ffn_w_gate_up, L0_ffn_w_proj, _dump=(), _nc=None):
    f32 = np.float32
    x = np.ascontiguousarray(np.asarray(L0_attn_x, dtype=f32))
    ld = np.asarray(L0_attn_log_decay, dtype=f32)
    Wq = np.ascontiguousarray(np.asarray(L0_attn_Wq, dtype=f32))
    Wk = np.ascontiguousarray(np.asarray(L0_attn_Wk, dtype=f32))
    Wv = np.ascontiguousarray(np.asarray(L0_attn_Wv, dtype=f32))
    Wo = np.ascontiguousarray(np.asarray(L0_attn_Wo, dtype=f32))
    Wg1 = np.ascontiguousarray(np.asarray(L0_attn_Wg1, dtype=f32))
    Wg2 = np.ascontiguousarray(np.asarray(L0_attn_Wg2, dtype=f32))
    nw = np.asarray(L0_attn_norm_w, dtype=f32)
    Wgu = np.ascontiguousarray(np.asarray(L0_ffn_w_gate_up, dtype=f32))
    Wp = np.ascontiguousarray(np.asarray(L0_ffn_w_proj, dtype=f32))

    diffT, cplus1, c127rep, ident, onehot = _host_consts()
    nw16 = np.ascontiguousarray(nw.reshape(16, 128).T)
    ldbc = np.ascontiguousarray(np.broadcast_to(ld[None, :], (128, 16)))

    nc = _nc if _nc is not None else build_program(dump=_dump)
    in_maps = []
    for c in range(NCORE):
        b, hf = c // 2, c % 2
        m = {
            "x": np.ascontiguousarray(x[b, hf * T:(hf + 1) * T, :]),
            "Wq": Wq, "Wk": Wk, "Wv": Wv, "Wo": Wo, "Wg1": Wg1, "Wg2": Wg2,
            "nw16": nw16, "ldbc": ldbc,
            "wgu_sh": np.ascontiguousarray(Wgu[:, c * 256:(c + 1) * 256]),
            "wp_sh": np.ascontiguousarray(Wp[:, c * 1024:(c + 1) * 1024]),
            "flag": np.full((128, 1), float(hf), dtype=f32),
            "diffT": diffT, "cplus1": cplus1, "c127rep": c127rep,
            "ident": ident, "onehot": onehot,
        }
        in_maps.append(m)
    res = run_bass_kernel_spmd(nc, in_maps, core_ids=list(range(NCORE)))
    out = np.empty((4, 4096, 2048), dtype=f32)
    for c in range(NCORE):
        b, hf = c // 2, c % 2
        out[b, hf * T:(hf + 1) * T, :] = res.results[c]["out"]
    if _dump:
        return out, res
    return out
